# version 02
# baseline (speedup 1.0000x reference)
import numpy as np
import ml_dtypes
from contextlib import ExitStack

import concourse.bass as bass
import concourse.mybir as mybir
from concourse.bass_utils import run_bass_kernel_spmd

F32 = mybir.dt.float32
BF16 = mybir.dt.bfloat16
AF = mybir.ActivationFunctionType
ALU = mybir.AluOpType

D = 1024
S = 4096
TOK = 2048
H, HD = 16, 64
HDIM = 4096
DI = 2048
DSL = 256
NST = 16
T16 = 16384
NCORE = 8
CTILE = 132 * D

STAGE = 4

RG = [[0, 1, 2, 3, 4, 5, 6, 7]]


class Rec:
    def __init__(self):
        self.ops = {e: [] for e in "savtg"}

    def s(self, fn): self.ops["s"].append(fn)
    def a(self, fn): self.ops["a"].append(fn)
    def v(self, fn): self.ops["v"].append(fn)
    def t(self, fn): self.ops["t"].append(fn)
    def g(self, fn): self.ops["g"].append(fn)


def build_nc():
    nc = bass.Bass("TRN2", target_bir_lowering=False)

    def inp(name, shape, dt=F32):
        return nc.dram_tensor(name, shape, dt, kind="ExternalInput").ap()

    xT_full = inp("xT_full", [D, S])
    x_ownT = inp("x_ownT", [D, TOK])
    circ2 = {k: inp(f"c{k}2", [CTILE]) for k in "qkvo"}
    bias_in = {k: inp(f"w{k}b_c", [128, 8]) for k in "qkvo"}
    wvb_row_in = inp("wvb_row", [1, D])
    w1T_in = inp("w1T", [D, HDIM])
    w2T_in = inp("w2T", [HDIM, D])
    w1N_in = inp("w1N", [HDIM, D])
    w2N_in = inp("w2N", [D, HDIM])
    b1_c = inp("b1_c", [128, HDIM // 128])
    b2_c = inp("b2_c", [128, D // 128])
    w_inT_x = inp("w_inT_x", [D, DSL])
    w_inT_z = inp("w_inT_z", [D, DSL])
    conv_w_c = inp("conv_w_c", [128, 2, 4])
    conv_b_c = inp("conv_b_c", [128, 2])
    wxT_sl = inp("wxT_sl", [DSL, 33])
    dtw_c = inp("dtw_c", [128, 2])
    dtb_c = inp("dtb_c", [128, 2])
    alog_c = inp("alog_c", [128, 2, NST])
    dpar_c = inp("dpar_c", [128, 2])
    w_outT_in = inp("w_outT", [DI, D])

    outT = nc.dram_tensor("outT", [D, TOK], F32, kind="ExternalOutput").ap()
    DBG = True
    if DBG:
        dbg_ln1 = nc.dram_tensor("dbg_ln1", [D, S], BF16, kind="ExternalOutput").ap()
        dbg_om = nc.dram_tensor("dbg_om", [D, TOK], BF16, kind="ExternalOutput").ap()
        dbg_kT = nc.dram_tensor("dbg_kT", [128, S], BF16, kind="ExternalOutput").ap()
        dbg_qT = nc.dram_tensor("dbg_qT", [128, TOK], BF16, kind="ExternalOutput").ap()
        dbg_vtm = nc.dram_tensor("dbg_vtm", [128, 32 * 130], BF16, kind="ExternalOutput").ap()
        dbg_neg = nc.dram_tensor("dbg_neg", [128, 2], F32, kind="ExternalOutput").ap()
        dbg_sc = nc.dram_tensor("dbg_sc", [2, 2 * NST], F32, kind="ExternalOutput").ap()
        dbg_m = nc.dram_tensor("dbg_m", [128, 2], BF16, kind="ExternalOutput").ap()
        dbg_al = nc.dram_tensor("dbg_al", [128, 48], F32, kind="ExternalOutput").ap()
        dbg_ln2 = nc.dram_tensor("dbg_ln2", [128, 8, 512], BF16, kind="ExternalOutput").ap()
        dbg_h = nc.dram_tensor("dbg_h", [128, 4, 512], BF16, kind="ExternalOutput").ap()
        dbg_xb = nc.dram_tensor("dbg_xb", [128, 8, 512], F32, kind="ExternalOutput").ap()
        dbg_u = nc.dram_tensor("dbg_u", [DSL, 2048], F32, kind="ExternalOutput").ap()
        dbg_z = nc.dram_tensor("dbg_z", [DSL, 2048], F32, kind="ExternalOutput").ap()
        dbg_bcd = nc.dram_tensor("dbg_bcd", [33, 2048], F32, kind="ExternalOutput").ap()
        dbg_g = nc.dram_tensor("dbg_g", [DI, 2048], BF16, kind="ExternalOutput").ap()
        dbg_l3 = nc.dram_tensor("dbg_l3", [D, 2048], BF16, kind="ExternalOutput").ap()
        dbg_gr = nc.dram_tensor("dbg_gr", [128, 16, 512], BF16, kind="ExternalOutput").ap()
        dbg_wo = nc.dram_tensor("dbg_wo", [128, 16, 1024], BF16, kind="ExternalOutput").ap()
        dbg_wo2 = nc.dram_tensor("dbg_wo2", [128, 16, 1024], BF16, kind="ExternalOutput").ap()
        dbg_sq = nc.dram_tensor("dbg_sq", [128, 4096], BF16, kind="ExternalOutput").ap()

    def scratch(name, shape, dt=F32):
        return nc.dram_tensor(name, shape, dt, kind="Internal").ap()

    x1_d = scratch("x1_d", [D, TOK])
    x2_d = scratch("x2_d", [D, TOK])
    x3_d = scratch("x3_d", [D, TOK])
    s1T_d = scratch("s1T_d", [D, HDIM], BF16)
    s2T_d = scratch("s2T_d", [HDIM, D], BF16)
    ln3_loc = scratch("ln3_loc", [D, TOK], BF16)
    ln3_all = scratch("ln3_all", [NCORE * D, TOK], BF16)
    bcd_part = scratch("bcd_part", [33, T16])
    bcd_all = scratch("bcd_all", [33, T16])
    u_d = scratch("u_d", [DSL, T16])
    zs_d = scratch("zs_d", [DSL, T16])
    gated_loc = scratch("gated_loc", [DSL, T16], BF16)
    gated_all = scratch("gated_all", [DI, T16], BF16)

    es = ExitStack()
    with es:
        sbt = lambda name, shape, dt: es.enter_context(nc.sbuf_tensor(name, shape, dt))

        bias_q = sbt("bias_q", [128, 8], F32)
        bias_k = sbt("bias_k", [128, 8], F32)
        bias_v = sbt("bias_v", [128, 8], F32)
        bias_o = sbt("bias_o", [128, 8], F32)
        vbrow = sbt("vbrow", [1, D], BF16)
        ones_col = sbt("ones_col", [128, 1], BF16)
        ones_row = sbt("ones_row", [1, 128], BF16)
        ones_rowf = sbt("ones_rowf", [1, 128], F32)
        onesq = sbt("onesq", [128, 128], F32)
        maskAB = sbt("maskAB", [128, 2], BF16)
        iota_col = sbt("iota_col", [128, 1], mybir.dt.int32)
        negMrow = sbt("negMrow", [1, 2], F32)
        negMb = sbt("negMb", [128, 2], F32)
        nsc = sbt("nsc", [1, 8], F32)
        carry = sbt("carry", [128, 2, NST], F32)
        A_sb = sbt("A_sb", [128, 2, NST], F32)
        alpha1 = sbt("alpha1", [128, HDIM // 128], F32)
        alpha2 = sbt("alpha2", [128, D // 128], F32)
        b1cols = sbt("b1cols", [128, HDIM // 128], F32)
        b2cols = sbt("b2cols", [128, D // 128], F32)
        conv_sb = sbt("conv_sb", [128, 2, 4], F32)
        convb_sb = sbt("convb_sb", [128, 2], F32)
        dtw_sb = sbt("dtw_sb", [128, 2], F32)
        dtb_sb = sbt("dtb_sb", [128, 2], F32)
        dpar_sb = sbt("dpar_sb", [128, 2], F32)
        wioT = sbt("wioT", [128, 8, 512], BF16)
        wxTb = sbt("wxTb", [128, 2, 33], BF16)

        arA = sbt("arA", [128, 16384], F32)
        arB = sbt("arB", [128, 16384], F32)
        arC = sbt("arC", [128, 8192], F32)
        arD = sbt("arD", [128, 8192], F32)

        ps = [es.enter_context(nc.psum_tensor(f"ps{i}", [128, 1024], F32))
              for i in range(4)]

        sems = {}
        for name in ("sd", "sg", "ssc", "pe", "va", "vd", "vg", "vc", "so"):
            sems[name] = es.enter_context(nc.semaphore(name))
        C = {k: 0 for k in sems}

        R = Rec()

        def W(eng, name, val=None):
            v = C[name] if val is None else val
            sem = sems[name]
            getattr(R, eng)(lambda h: h.wait_ge(sem, v))

        def INC(name, n=1):
            C[name] += n
            return C[name]

        arA_b = arA.ap().bitcast(BF16)
        arB_b = arB.ap().bitcast(BF16)
        arC_b = arC.ap().bitcast(BF16)
        arD_b = arD.ap().bitcast(BF16)
        arD_f = arD.ap()

        ln1 = arA_b.rearrange("p (j t) -> p j t", j=8)
        CT = {k: arB_b[:, 8192 * i:8192 * (i + 1)].rearrange("p (j i) -> p j i", j=8)
              for i, k in enumerate("qkvo")}
        om = arC_b.rearrange("p (j t) -> p j t", j=8)
        stage = arC.ap().rearrange("p (r j t) -> p r j t", r=2, j=8)

        ln_xb = arD_b[:, 0:4096].rearrange("p (j t) -> p j t", j=8)
        ln_x2 = arD_b[:, 4096:8192].rearrange("p (j t) -> p j t", j=8)
        lns_ln = arD_f[:, 4096:6144].rearrange("p (k t) -> p k t", k=4)
        tmp_ln = arD_f[:, 6144:6656]

        kTv = arD_b[:, 0:4096]
        vtm = arD_b[:, 4096:8256].rearrange("p (tt c) -> p tt c", c=130)
        qTv = arD_b[:, 8256:10304]
        pT = arD_b[:, 10304:14400].rearrange("p (r t) -> p r t", r=4)
        rb64 = arD_f[:, 5152:6176]
        lns_at = arD_f[:, 6176:7200].rearrange("p (k t) -> p k t", k=2)
        lns_at2 = arD_f[:, 5152:6176].rearrange("p (k t) -> p k t", k=2)
        x0c = arD_f[:, 0:512]
        x1c = arD_f[:, 512:1024]
        tmpo = arD_f[:, 1024:1536]

        def p0():
            def _s(h):
                for t, src in ((bias_q, bias_in["q"]), (bias_k, bias_in["k"]),
                               (bias_v, bias_in["v"]), (bias_o, bias_in["o"]),
                               (arD_f[0:1, 0:1024], wvb_row_in), (b1cols, b1_c), (b2cols, b2_c),
                               (conv_sb, conv_w_c), (convb_sb, conv_b_c),
                               (dtw_sb, dtw_c), (dtb_sb, dtb_c), (dpar_sb, dpar_c),
                               (A_sb, alog_c)):
                    h.dma_start(out=t[:], in_=src).then_inc(sems["sd"], 16)
            R.s(_s)
            INC("sd", 16 * 13)

            R.g(lambda h: h.iota(iota_col[:], pattern=[[0, 1]], base=0,
                                 channel_multiplier=1).then_inc(sems["vg"], 1))
            INC("vg")
            W("v", "sd")
            W("v", "vg")

            def _v(h):
                h.memset(ones_col[:], 1.0)
                h.memset(ones_row[:], 1.0)
                h.memset(ones_rowf[:], 1.0)
                h.memset(onesq[:], 1.0)
                h.memset(nsc[:], 0.0)
                h.tensor_single_scalar(out=maskAB[:, 0:1], in_=iota_col[:], scalar=64, op=ALU.is_lt)
                h.tensor_single_scalar(out=maskAB[:, 1:2], in_=iota_col[:], scalar=64, op=ALU.is_ge)
                h.tensor_copy(out=vbrow[:], in_=arD_f[0:1, 0:1024]).then_inc(sems["vd"], 1)
            R.v(_v)
            INC("vd")
            W("a", "sd")
            R.a(lambda h: h.activation(out=A_sb[:], in_=A_sb.ap() if False else A_sb[:],
                                       func=AF.Exp).then_inc(sems["va"], 1))
            INC("va")
            W("v", "va")
            R.v(lambda h: h.tensor_single_scalar(out=A_sb[:], in_=A_sb[:],
                                                 scalar=-1.0, op=ALU.mult)
                .then_inc(sems["vd"], 1))
            INC("vd")

        def ln_rec(src_dram, dst_store, ntok, keep_store=None):
            nchunks = ntok // 512
            srcR = src_dram.rearrange("(j p) t -> p j t", p=128)
            for c in range(nchunks):
                st = stage[:, c % 2]
                W("s", "vd")
                R.s(lambda h, st=st, c=c: h.dma_start(
                    out=st[:], in_=srcR[:, :, 512 * c:512 * (c + 1)])
                    .then_inc(sems["sd"], 16))
                INC("sd", 16)
                W("a", "sd")
                W("a", "pe", C["pe"])

                def _a(h, st=st):
                    for jc in range(8):
                        h.activation(out=ln_xb[:, jc, :], in_=st[:, jc, :], func=AF.Identity)
                    for jc in range(8):
                        r = h.activation(out=ln_x2[:, jc, :], in_=st[:, jc, :], func=AF.Square)
                    r.then_inc(sems["va"], 1)
                R.a(_a)
                INC("va")
                W("t", "va")
                W("t", "vd", C["vd"])

                def _t(h):
                    for jc in range(8):
                        h.matmul(ps[3][0:1, 0:512], ones_col[:], ln_xb[:, jc, :],
                                 start=(jc == 0), stop=(jc == 7))
                    for jc in range(8):
                        r = h.matmul(ps[3][32:33, 512:1024], ones_col[:], ln_x2[:, jc, :],
                                     start=(jc == 0), stop=(jc == 7))
                    r.then_inc(sems["pe"], 1)
                R.t(_t)
                INC("pe")
                W("v", "pe")

                def _v1(h):
                    h.tensor_single_scalar(out=lns_ln[0:1, 0, :], in_=ps[3][0:1, 0:512],
                                           scalar=1.0 / D, op=ALU.mult)
                    h.tensor_single_scalar(out=lns_ln[0:1, 1, :], in_=ps[3][32:33, 512:1024],
                                           scalar=1.0 / D, op=ALU.mult)
                    h.tensor_mul(out=lns_ln[0:1, 2, :], in0=lns_ln[0:1, 0, :],
                                 in1=lns_ln[0:1, 0, :])
                    h.tensor_sub(out=lns_ln[0:1, 3, :], in0=lns_ln[0:1, 1, :],
                                 in1=lns_ln[0:1, 2, :])
                    r = h.tensor_single_scalar(out=lns_ln[0:1, 3, :], in_=lns_ln[0:1, 3, :],
                                               scalar=1e-5, op=ALU.add)
                    r.then_inc(sems["vd"], 1)
                R.v(_v1)
                INC("vd")
                W("a", "vd")
                R.a(lambda h: h.activation(out=lns_ln[0:1, 3, :], in_=lns_ln[0:1, 3, :],
                                           func=AF.Sqrt).then_inc(sems["va"], 1))
                INC("va")
                W("v", "va")
                R.v(lambda h: h.reciprocal(out=lns_ln[0:1, 1, :], in_=lns_ln[0:1, 3, :])
                    .then_inc(sems["vd"], 1))
                INC("vd")
                W("t", "vd")

                def _t2(h):
                    h.matmul(ps[2][:, 0:512], ones_rowf[:], lns_ln[0:1, 0, :],
                             start=True, stop=True)
                    h.matmul(ps[2][:, 512:1024], ones_rowf[:], lns_ln[0:1, 1, :],
                             start=True, stop=True).then_inc(sems["pe"], 1)
                R.t(_t2)
                INC("pe")
                W("v", "pe")

                def _v2(h, st=st, c=c):
                    for jc in range(8):
                        h.tensor_sub(out=tmp_ln[:], in0=st[:, jc, :], in1=ps[2][:, 0:512])
                        r = h.tensor_mul(out=dst_store(jc, c), in0=tmp_ln[:],
                                         in1=ps[2][:, 512:1024])
                        if keep_store is not None:
                            r = h.tensor_copy(out=keep_store(jc, c), in_=st[:, jc, :])
                    r.then_inc(sems["vd"], 1)
                R.v(_v2)
                INC("vd")

        def p1():
            ln_rec(xT_full, lambda jc, c: ln1[:, jc, 512 * c:512 * (c + 1)], S)

        def p2():
            for k in "qkvo":
                for jc in range(8):
                    st2 = stage[:, jc % 2, 0:2]
                    W("s", "va")
                    W("s", "vd")
                    off = 2 * D - 128 * jc
                    R.s(lambda h, off=off, k=k, st2=st2: h.dma_start(
                        out=st2[:],
                        in_=bass.AP(circ2[k].tensor, off, [[D - 1, 128], [1, D]]))
                        .then_inc(sems["sd"], 16))
                    INC("sd", 16)
                    W("a", "sd")
                    R.a(lambda h, k=k, jc=jc, st2=st2: h.activation(
                        out=CT[k][:, jc, :], in_=st2.rearrange("p a t -> p (a t)"),
                        func=AF.Identity).then_inc(sems["va"], 1))
                    INC("va")

        def p3():
            dP = 16384
            qT_h = qTv.tensor
            QOFF0 = qTv.offset

            prj = [ps[2][:, 0:512], ps[2][:, 512:1024],
                   ps[3][:, 0:512], ps[3][:, 512:1024]]
            prj_free = [C["va"]] * 4

            W("v", "vd", C["vd"])
            R.v(lambda h: h.memset(
                bass.AP(vtm.tensor, vtm.offset + 64, [[dP, 128], [130, 32], [65, 2]]),
                1.0).then_inc(sems["vd"], 1))
            INC("vd")
            vtm_ones_vd = C["vd"]

            for hp in range(8):
                for f in range(8):
                    slot = f % 4
                    W("t", "va", prj_free[slot])

                    def _t(h, hp=hp, f=f, slot=slot):
                        for jc in range(8):
                            r = h.matmul(prj[slot],
                                         CT["k"][:, jc, 128 * hp:128 * (hp + 1)],
                                         ln1[:, jc, 512 * f:512 * (f + 1)],
                                         start=(jc == 0), stop=(jc == 7))
                        r.then_inc(sems["pe"], 1)
                    R.t(_t)
                    INC("pe")
                    W("a", "pe")
                    R.a(lambda h, f=f, slot=slot, hp=hp: h.activation(
                        out=kTv[:, 512 * f:512 * (f + 1)], in_=prj[slot],
                        func=AF.Identity, bias=bias_k[:, hp:hp + 1])
                        .then_inc(sems["va"], 1))
                    INC("va")
                    prj_free[slot] = C["va"]

                for tt in range(32):
                    slot = tt % 4
                    W("t", "va", prj_free[slot])

                    def _t(h, hp=hp, tt=tt, slot=slot):
                        for jc in range(8):
                            h.matmul(prj[slot][:, 0:128],
                                     ln1[:, jc, 128 * tt:128 * (tt + 1)],
                                     CT["v"][:, jc, 128 * hp:128 * (hp + 1)],
                                     start=(jc == 0), stop=False)
                        r = h.matmul(prj[slot][:, 0:128], ones_row[:],
                                     vbrow[0:1, 128 * hp:128 * (hp + 1)],
                                     start=False, stop=True)
                        r.then_inc(sems["pe"], 1)
                    R.t(_t)
                    INC("pe")
                    W("a", "pe")

                    def _a(h, tt=tt, slot=slot):
                        h.activation(out=vtm[:, tt, 0:64], in_=prj[slot][:, 0:64],
                                     func=AF.Identity)
                        r = h.activation(out=vtm[:, tt, 65:129], in_=prj[slot][:, 64:128],
                                         func=AF.Identity)
                        r.then_inc(sems["va"], 1)
                    R.a(_a)
                    INC("va")
                    prj_free[slot] = C["va"]

                for f in range(4):
                    slot = f % 4
                    W("t", "va", prj_free[slot])

                    def _t(h, hp=hp, f=f, slot=slot):
                        for jc in range(8):
                            r = h.matmul(prj[slot],
                                         CT["q"][:, jc, 128 * hp:128 * (hp + 1)],
                                         ln1[:, jc, 512 * f:512 * (f + 1)],
                                         start=(jc == 0), stop=(jc == 7))
                        r.then_inc(sems["pe"], 1)
                    R.t(_t)
                    INC("pe")
                    W("a", "pe")
                    R.a(lambda h, f=f, slot=slot, hp=hp: h.activation(
                        out=qTv[:, 512 * f:512 * (f + 1)], in_=prj[slot],
                        func=AF.Identity, bias=bias_q[:, hp:hp + 1])
                        .then_inc(sems["va"], 1))
                    INC("va")
                    prj_free[slot] = C["va"]

                sqscr = pT.rearrange("p r t -> p (r t)")
                qacc = carry[0:2, 0, 0:1]
                kacc = carry[0:2, 0, 1:2]
                ntmp = carry[0:2, 0, 2:3]
                negM2 = carry[0:2, 0, 3:4]
                qmaxs = carry[0:2, 1, 0:4]
                kmaxs = carry[0:2, 1, 4:12]
                for (srcv, nch, maxs, acc) in ((qTv, 4, qmaxs, qacc),
                                               (kTv, 8, kmaxs, kacc)):
                    W("a", "vd", C["vd"])
                    R.a(lambda h, srcv=srcv, nch=nch: h.activation(
                        out=sqscr[:, 0:512 * nch], in_=srcv, func=AF.Square)
                        .then_inc(sems["va"], 1))
                    INC("va")
                    W("t", "va")
                    for ch in range(nch):
                        pst = ps[0] if ch % 2 == 0 else ps[1]
                        W("t", "vd")
                        R.t(lambda h, ch=ch, pst=pst: h.matmul(
                            pst[0:2, 0:512], maskAB[:],
                            sqscr[:, 512 * ch:512 * (ch + 1)],
                            start=True, stop=True).then_inc(sems["pe"], 1))
                        INC("pe")
                        W("v", "pe")
                        R.v(lambda h, ch=ch, pst=pst, maxs=maxs: h.reduce_max(
                            out=maxs[:, ch:ch + 1], in_=pst[0:2, 0:512],
                            axis=mybir.AxisListType.X).then_inc(sems["vd"], 1))
                        INC("vd")
                    W("v", "vd")
                    R.v(lambda h, maxs=maxs, acc=acc, nch=nch: h.reduce_max(
                        out=acc, in_=maxs[:, 0:nch],
                        axis=mybir.AxisListType.X).then_inc(sems["vd"], 1))
                    INC("vd")
                W("v", "vd")
                R.v(lambda h: h.tensor_mul(out=ntmp, in0=qacc, in1=kacc)
                    .then_inc(sems["vd"], 1))
                INC("vd")
                W("a", "vd")
                R.a(lambda h: h.activation(out=ntmp, in_=ntmp, func=AF.Sqrt)
                    .then_inc(sems["va"], 1))
                INC("va")
                W("v", "va")
                R.v(lambda h: h.tensor_single_scalar(
                    out=negM2, in_=ntmp, scalar=-0.125 * 1.02, op=ALU.mult)
                    .then_inc(sems["vd"], 1))
                INC("vd")
                W("s", "vd")
                R.s(lambda h: h.dma_start(out=negMrow[0:1, 0:2], in_=negM2)
                    .then_inc(sems["sd"], 16))
                INC("sd", 16)
                W("t", "sd")
                R.t(lambda h: h.matmul(ps[0][:, 0:2], ones_rowf[:], negMrow[0:1, 0:2],
                                       start=True, stop=True).then_inc(sems["pe"], 1))
                INC("pe")
                W("a", "pe")
                R.a(lambda h: h.activation(out=negMb[:], in_=ps[0][:, 0:2],
                                           func=AF.Identity).then_inc(sems["va"], 1))
                INC("va")

                if DBG and hp == 7:
                    W("g", "va", C["va"])
                    W("g", "vd", C["vd"])
                    def _gd(h):
                        h.dma_start(out=dbg_kT, in_=kTv).then_inc(sems["so"], 16)
                        h.dma_start(out=dbg_qT, in_=qTv).then_inc(sems["so"], 16)
                        h.dma_start(out=dbg_vtm, in_=vtm.rearrange("p a b -> p (a b)")).then_inc(sems["so"], 16)
                        h.dma_start(out=dbg_neg, in_=negMb[:]).then_inc(sems["so"], 16)
                        h.dma_start(out=dbg_sc, in_=carry[0:2, :, :].rearrange("p a b -> p (a b)")).then_inc(sems["so"], 16)
                        h.dma_start(out=dbg_m, in_=maskAB[:]).then_inc(sems["so"], 16)
                        h.dma_start(out=dbg_sq, in_=pT.rearrange("p r t -> p (r t)")).then_inc(sems["so"], 16)
                    R.g(_gd)
                    INC("so", 112)
                    W("t", "so", C["so"])
                W("t", "vd", vtm_ones_vd)
                for qc in range(2):
                    for kc in range(32):
                        W("t", "va")

                        def _t(h, qc=qc, kc=kc):
                            for fh in range(2):
                                for head in (0, 1):
                                    pst = ps[0] if head == 0 else ps[1]
                                    rhs = bass.AP(
                                        qT_h,
                                        64 * head * dP + QOFF0 + 1024 * qc + 512 * fh,
                                        [[dP, 64], [1, 512]])
                                    r = h.matmul(
                                        pst[:, 512 * fh:512 * (fh + 1)],
                                        kTv[64 * head:64 * (head + 1),
                                            128 * kc:128 * (kc + 1)],
                                        rhs, start=True, stop=True,
                                        tile_position=(64 * head, 0))
                            r.then_inc(sems["pe"], 1)
                        R.t(_t)
                        INC("pe")
                        W("a", "pe")

                        def _a(h, kc=kc):
                            h.activation(out=pT[:, 2 * (kc % 2) + 0, :], in_=ps[0][:],
                                         func=AF.Exp, scale=0.125, bias=negMb[:, 0:1])
                            r = h.activation(out=pT[:, 2 * (kc % 2) + 1, :], in_=ps[1][:],
                                             func=AF.Exp, scale=0.125, bias=negMb[:, 1:2])
                            r.then_inc(sems["va"], 1)
                        R.a(_a)
                        INC("va")
                        W("t", "va")

                        def _t2(h, kc=kc):
                            for head in (0, 1):
                                pst = ps[2] if head == 0 else ps[3]
                                for fh in range(2):
                                    r = h.matmul(
                                        pst[0:65, 512 * fh:512 * (fh + 1)],
                                        vtm[:, kc, 65 * head:65 * head + 65],
                                        pT[:, 2 * (kc % 2) + head,
                                           512 * fh:512 * (fh + 1)],
                                        start=(kc == 0), stop=(kc == 31))
                            r.then_inc(sems["pe"], 1)
                        R.t(_t2)
                        INC("pe")

                    W("v", "pe")

                    def _v(h):
                        h.reciprocal(out=lns_at[0:1, 0, :], in_=ps[2][64:65, 0:512])
                        h.reciprocal(out=lns_at[0:1, 1, :], in_=ps[2][64:65, 512:1024])
                        h.reciprocal(out=lns_at2[0:1, 0, :], in_=ps[3][64:65, 0:512])
                        r = h.reciprocal(out=lns_at2[0:1, 1, :], in_=ps[3][64:65, 512:1024])
                        r.then_inc(sems["vd"], 1)
                    R.v(_v)
                    INC("vd")
                    W("t", "vd")

                    def _t3(h):
                        h.matmul(ps[0][0:64, 0:512], ones_rowf[0:1, 0:64],
                                 lns_at[0:1, 0, :], start=True, stop=True)
                        h.matmul(ps[0][0:64, 512:1024], ones_rowf[0:1, 0:64],
                                 lns_at[0:1, 1, :], start=True, stop=True)
                        h.matmul(ps[1][0:64, 0:512], ones_rowf[0:1, 0:64],
                                 lns_at2[0:1, 0, :], start=True, stop=True)
                        r = h.matmul(ps[1][0:64, 512:1024], ones_rowf[0:1, 0:64],
                                     lns_at2[0:1, 1, :], start=True, stop=True)
                        r.then_inc(sems["pe"], 1)
                    R.t(_t3)
                    INC("pe")
                    W("a", "pe")

                    def _a2(h):
                        h.activation(out=rb64[0:64, :], in_=ps[0][0:64, :], func=AF.Identity)
                        r = h.activation(out=rb64[64:128, :], in_=ps[1][0:64, :],
                                         func=AF.Identity)
                        r.then_inc(sems["va"], 1)
                    R.a(_a2)
                    INC("va")
                    W("v", "va")

                    def _v2(h, hp=hp, qc=qc):
                        h.tensor_mul(out=om[0:64, hp, 1024 * qc:1024 * (qc + 1)],
                                     in0=ps[2][0:64, :], in1=rb64[0:64, :])
                        r = h.tensor_mul(out=om[64:128, hp, 1024 * qc:1024 * (qc + 1)],
                                         in0=ps[3][0:64, :], in1=rb64[64:128, :])
                        r.then_inc(sems["vd"], 1)
                    R.v(_v2)
                    INC("vd")
                    W("t", "vd")

            if DBG:
                W("g", "vd", C["vd"])
                def _gd2(h):
                    h.dma_start(out=dbg_om.rearrange("(j p) t -> p j t", p=128),
                                in_=om[:]).then_inc(sems["so"], 16)
                R.g(_gd2)
                INC("so", 16)
            W("t", "vd")
            x_ownR = x_ownT.rearrange("(j p) t -> p j t", p=128)
            x1R = x1_d.rearrange("(j p) t -> p j t", p=128)
            outR = outT.rearrange("(j p) t -> p j t", p=128)
            for mt in range(8):
                for f in range(4):
                    pst = ps[0] if (mt * 4 + f) % 2 == 0 else ps[1]
                    W("s", "vd", C["vd"])
                    R.s(lambda h, mt=mt, f=f: h.dma_start(
                        out=x0c[:], in_=x_ownR[:, mt, 512 * f:512 * (f + 1)])
                        .then_inc(sems["sd"], 16))
                    INC("sd", 16)
                    W("t", "va", C["va"])

                    def _t(h, mt=mt, f=f, pst=pst):
                        for jc in range(8):
                            r = h.matmul(pst[:, 0:512],
                                         CT["o"][:, jc, 128 * mt:128 * (mt + 1)],
                                         om[:, jc, 512 * f:512 * (f + 1)],
                                         start=(jc == 0), stop=(jc == 7))
                        r.then_inc(sems["pe"], 1)
                    R.t(_t)
                    INC("pe")
                    W("a", "pe")
                    W("a", "vd", C["vd"])
                    R.a(lambda h, mt=mt, pst=pst: h.activation(
                        out=tmpo[:], in_=pst[:, 0:512], func=AF.Identity,
                        bias=bias_o[:, mt:mt + 1]).then_inc(sems["va"], 1))
                    INC("va")
                    W("v", "va")
                    W("v", "sd")
                    R.v(lambda h: h.tensor_add(out=x1c[:], in0=x0c[:], in1=tmpo[:])
                        .then_inc(sems["vd"], 1))
                    INC("vd")
                    W("g", "vd")

                    def _g(h, mt=mt, f=f):
                        r = h.dma_start(out=x1R[:, mt, 512 * f:512 * (f + 1)], in_=x1c[:])
                        r.then_inc(sems["sg"], 16)
                        if STAGE == 1:
                            h.dma_start(out=outR[:, mt, 512 * f:512 * (f + 1)],
                                        in_=x1c[:]).then_inc(sems["so"], 16)
                    R.g(_g)
                    INC("sg", 16)
                    if STAGE == 1:
                        INC("so", 16)
                    W("v", "sg", C["sg"])

        s1T = arA_b.rearrange("p (j t) -> p j t", j=8)
        s2T = arB_b.rearrange("p (j t) -> p j t", j=32)
        w1R = w1T_in.rearrange("(j p) t -> p j t", p=128)
        w2R = w2T_in.rearrange("(j p) t -> p j t", p=128)
        arC_f = arC.ap()

        def p4():
            for jc in range(8):
                stg = arC_f[:, 4096 * (jc % 2):4096 * (jc % 2) + 4096]
                W("s", "va", C["va"])
                R.s(lambda h, jc=jc, stg=stg: h.dma_start(
                    out=stg, in_=w1R[:, jc, :]).then_inc(sems["sd"], 16))
                INC("sd", 16)
                W("a", "sd")
                R.a(lambda h, jc=jc, stg=stg: h.activation(
                    out=s1T[:, jc, :], in_=stg, func=AF.Sign).then_inc(sems["va"], 1))
                INC("va")
            for hc in range(32):
                stg = arC_f[:, 4096 * (hc % 2):4096 * (hc % 2) + 1024]
                W("s", "va", C["va"])
                R.s(lambda h, hc=hc, stg=stg: h.dma_start(
                    out=stg, in_=w2R[:, hc, :]).then_inc(sems["sd"], 16))
                INC("sd", 16)
                W("a", "sd")
                R.a(lambda h, hc=hc, stg=stg: h.activation(
                    out=s2T[:, hc, :], in_=stg, func=AF.Sign).then_inc(sems["va"], 1))
                INC("va")
            w1NR = w1N_in.rearrange("(m p) t -> p m t", p=128)
            w2NR = w2N_in.rearrange("(m p) t -> p m t", p=128)
            for (wR, ntile, alout, dd) in ((w1NR, 32, alpha1, D), (w2NR, 8, alpha2, HDIM)):
                for Mt in range(ntile):
                    stg = arC_f[:, dd * (Mt % 2):dd * (Mt % 2) + dd]
                    W("s", "vd", C["vd"])
                    W("s", "va", C["va"])
                    R.s(lambda h, wR=wR, Mt=Mt, stg=stg: h.dma_start(
                        out=stg, in_=wR[:, Mt, :]).then_inc(sems["sd"], 16))
                    INC("sd", 16)
                    W("a", "sd")
                    W("a", "vd", C["vd"])
                    absdst = arD_f[:, 0:dd]
                    R.a(lambda h, stg=stg, absdst=absdst, dd=dd: h.activation(
                        out=absdst, in_=stg, func=AF.Abs, scale=1.0 / dd)
                        .then_inc(sems["va"], 1))
                    INC("va")
                    W("v", "va")

                    def _v(h, absdst=absdst, alout=alout, Mt=Mt):
                        r = h.tensor_reduce(out=alout[:, Mt:Mt + 1], in_=absdst,
                                            axis=mybir.AxisListType.X, op=ALU.add)
                        r.then_inc(sems["vd"], 1)
                    R.v(_v)
                    INC("vd")
            W("g", "vd", C["vd"])
            R.g(lambda h: h.dma_start(out=dbg_sq[:, 0:2048], in_=arD_f[:, 0:1024].bitcast(BF16))
                .then_inc(sems["so"], 16))
            INC("so", 16)
            W("v", "so", C["so"])
            W("g", "vd", C["vd"])
            R.g(lambda h: h.dma_start(out=dbg_al[:, 0:32], in_=alpha1[:])
                .then_inc(sems["so"], 16))
            INC("so", 16)
            W("v", "so", C["so"])
            W("g", "va", C["va"])
            s1R = s1T_d.rearrange("(j p) t -> p j t", p=128)
            s2R = s2T_d.rearrange("(j p) t -> p j t", p=128)

            def _g(h):
                h.dma_start(out=s1R[:, :, :], in_=s1T[:]).then_inc(sems["sg"], 16)
                h.dma_start(out=s2R[:, :, :], in_=s2T[:]).then_inc(sems["sg"], 16)
            R.g(_g)
            INC("sg", 32)

        xbuf = arD_f[:, 0:4096].rearrange("p (j t) -> p j t", j=8)
        lnsF = arD_f[:, 4096:5120].rearrange("p (k t) -> p k t", k=2)
        tmpo2 = arD_f[:, 5120:5632]
        ln2c = arD_b[:, 11264:15360].rearrange("p (j t) -> p j t", j=8)
        castb = arD_b[:, 15360:15872]
        sqb = arD_b[:, 15872:16384]

        def ffn_chunks(xsrc_d, xdst_d, also_out):
            xsR = xsrc_d.rearrange("(j p) t -> p j t", p=128)
            xdR = xdst_d.rearrange("(j p) t -> p j t", p=128)
            outR2 = outT.rearrange("(j p) t -> p j t", p=128)
            W("s", "sg", C["sg"])
            for c in range(4):
                W("s", "vd", C["vd"])
                W("s", "sg", C["sg"])
                R.s(lambda h, c=c: h.dma_start(
                    out=xbuf[:], in_=xsR[:, :, 512 * c:512 * (c + 1)])
                    .then_inc(sems["sd"], 16))
                INC("sd", 16)
                W("a", "sd")
                for jc in range(8):
                    if jc > 0:
                        W("a", "pe", C["pe"])

                    def _a(h, jc=jc):
                        h.activation(out=castb, in_=xbuf[:, jc, :], func=AF.Identity)
                        r = h.activation(out=sqb, in_=xbuf[:, jc, :], func=AF.Square)
                        r.then_inc(sems["va"], 1)
                    R.a(_a)
                    INC("va")
                    W("t", "va")

                    def _t(h, jc=jc):
                        h.matmul(ps[3][0:1, 0:512], ones_col[:], castb,
                                 start=(jc == 0), stop=(jc == 7))
                        r = h.matmul(ps[3][0:1, 512:1024], ones_col[:], sqb,
                                     start=(jc == 0), stop=(jc == 7))
                        r.then_inc(sems["pe"], 1)
                    R.t(_t)
                    INC("pe")
                W("v", "pe")

                def _v1(h):
                    h.tensor_single_scalar(out=lnsF[0:1, 0, :], in_=ps[3][0:1, 0:512],
                                           scalar=1.0 / D, op=ALU.mult)
                    h.tensor_single_scalar(out=tmpo2[0:1, :], in_=ps[3][0:1, 512:1024],
                                           scalar=1.0 / D, op=ALU.mult)
                    h.tensor_mul(out=castb[0:1, :], in0=lnsF[0:1, 0, :],
                                 in1=lnsF[0:1, 0, :])
                    h.tensor_sub(out=tmpo2[0:1, :], in0=tmpo2[0:1, :],
                                 in1=castb[0:1, :])
                    r = h.tensor_single_scalar(out=tmpo2[0:1, :],
                                               in_=tmpo2[0:1, :],
                                               scalar=1e-5, op=ALU.add)
                    r.then_inc(sems["vd"], 1)
                R.v(_v1)
                INC("vd")
                W("a", "vd")
                R.a(lambda h: h.activation(out=tmpo2[0:1, :],
                                           in_=tmpo2[0:1, :],
                                           func=AF.Sqrt).then_inc(sems["va"], 1))
                INC("va")
                W("v", "va")
                R.v(lambda h: h.reciprocal(out=lnsF[0:1, 1, :], in_=tmpo2[0:1, :])
                    .then_inc(sems["vd"], 1))
                INC("vd")
                W("t", "vd")

                def _t2(h):
                    h.matmul(ps[2][:, 0:512], ones_rowf[:], lnsF[0:1, 0, :],
                             start=True, stop=True)
                    r = h.matmul(ps[2][:, 512:1024], ones_rowf[:], lnsF[0:1, 1, :],
                                 start=True, stop=True)
                    r.then_inc(sems["pe"], 1)
                R.t(_t2)
                INC("pe")
                W("v", "pe")

                def _v2(h):
                    for jc in range(8):
                        h.tensor_sub(out=castb, in0=xbuf[:, jc, :], in1=ps[2][:, 0:512])
                        r = h.tensor_mul(out=ln2c[:, jc, :], in0=castb,
                                         in1=ps[2][:, 512:1024])
                    r.then_inc(sems["vd"], 1)
                R.v(_v2)
                INC("vd")
                if also_out and c == 0:
                    W("g", "vd", C["vd"])
                    def _gdf(h):
                        h.dma_start(out=dbg_ln2, in_=ln2c[:]).then_inc(sems["so"], 16)
                        h.dma_start(out=dbg_xb, in_=xbuf[:]).then_inc(sems["so"], 16)
                        al = dbg_al
                        h.dma_start(out=al[:, 32:40], in_=alpha2[:]).then_inc(sems["so"], 16)
                        h.dma_start(out=al[:, 40:44], in_=b2cols[:, 0:4]).then_inc(sems["so"], 16)
                        h.dma_start(out=al[:, 44:48], in_=bias_q[:, 0:4]).then_inc(sems["so"], 16)
                    R.g(_gdf)
                    INC("so", 80)
                    W("v", "so", C["so"])
                h_bf = arC_b.rearrange("p (m t) -> p m t", m=32)
                W("t", "vd")
                for Mt in range(32):
                    pst = ps[0] if Mt % 2 == 0 else ps[1]
                    W("t", "va", C["va"])

                    def _t3(h, Mt=Mt, pst=pst):
                        for jc in range(8):
                            r = h.matmul(pst[:, 0:512], s1T[:, jc, 128 * Mt:128 * (Mt + 1)],
                                         ln2c[:, jc, :], start=(jc == 0), stop=(jc == 7))
                        r.then_inc(sems["pe"], 1)
                    R.t(_t3)
                    INC("pe")
                    W("a", "pe")
                    R.a(lambda h, Mt=Mt, pst=pst: h.activation(
                        out=h_bf[:, Mt, :], in_=pst[:, 0:512], func=AF.Gelu,
                        scale=alpha1[:, Mt:Mt + 1], bias=b1cols[:, Mt:Mt + 1])
                        .then_inc(sems["va"], 1))
                    INC("va")
                if also_out and c == 0:
                    W("g", "va", C["va"])
                    R.g(lambda h: h.dma_start(out=dbg_h, in_=h_bf[:, 0:4, :])
                        .then_inc(sems["so"], 16))
                    INC("so", 16)
                    W("t", "so", C["so"])
                for Mt in range(8):
                    pst = ps[0] if Mt % 2 == 0 else ps[1]
                    W("t", "va", C["va"])

                    def _t4(h, Mt=Mt, pst=pst):
                        for hc in range(32):
                            r = h.matmul(pst[:, 0:512], s2T[:, hc, 128 * Mt:128 * (Mt + 1)],
                                         h_bf[:, hc, :], start=(hc == 0), stop=(hc == 31))
                        r.then_inc(sems["pe"], 1)
                    R.t(_t4)
                    INC("pe")
                    W("a", "pe")
                    W("a", "vd", C["vd"])
                    R.a(lambda h, Mt=Mt, pst=pst: h.activation(
                        out=tmpo2[:], in_=pst[:, 0:512], func=AF.Identity,
                        scale=alpha2[:, Mt:Mt + 1], bias=b2cols[:, Mt:Mt + 1])
                        .then_inc(sems["va"], 1))
                    INC("va")
                    W("v", "va")
                    R.v(lambda h, Mt=Mt: h.tensor_add(
                        out=xbuf[:, Mt, :], in0=xbuf[:, Mt, :], in1=tmpo2[:])
                        .then_inc(sems["vd"], 1))
                    INC("vd")
                    W("g", "vd")

                    def _g(h, Mt=Mt, c=c):
                        r = h.dma_start(out=xdR[:, Mt, 512 * c:512 * (c + 1)],
                                        in_=xbuf[:, Mt, :])
                        r.then_inc(sems["sg"], 16)
                        if also_out:
                            h.dma_start(out=outR2[:, Mt, 512 * c:512 * (c + 1)],
                                        in_=xbuf[:, Mt, :]).then_inc(sems["so"], 16)
                    R.g(_g)
                    INC("sg", 16)
                    if also_out:
                        INC("so", 16)
                    W("v", "sg", C["sg"])

        def p6():
            xsR = x2_d.rearrange("(j p) t -> p j t", p=128)
            l3R = ln3_loc.rearrange("(j p) t -> p j t", p=128)
            W("s", "sg", C["sg"])
            for c in range(4):
                W("s", "vd", C["vd"])
                W("s", "sg", C["sg"])
                R.s(lambda h, c=c: h.dma_start(
                    out=xbuf[:], in_=xsR[:, :, 512 * c:512 * (c + 1)])
                    .then_inc(sems["sd"], 16))
                INC("sd", 16)
                W("a", "sd")
                for jc in range(8):
                    if jc > 0:
                        W("a", "pe", C["pe"])

                    def _a(h, jc=jc):
                        h.activation(out=castb, in_=xbuf[:, jc, :], func=AF.Identity)
                        r = h.activation(out=sqb, in_=xbuf[:, jc, :], func=AF.Square)
                        r.then_inc(sems["va"], 1)
                    R.a(_a)
                    INC("va")
                    W("t", "va")

                    def _t(h, jc=jc):
                        h.matmul(ps[3][0:1, 0:512], ones_col[:], castb,
                                 start=(jc == 0), stop=(jc == 7))
                        r = h.matmul(ps[3][0:1, 512:1024], ones_col[:], sqb,
                                     start=(jc == 0), stop=(jc == 7))
                        r.then_inc(sems["pe"], 1)
                    R.t(_t)
                    INC("pe")
                W("v", "pe")

                def _v1(h):
                    h.tensor_single_scalar(out=lnsF[0:1, 0, :], in_=ps[3][0:1, 0:512],
                                           scalar=1.0 / D, op=ALU.mult)
                    h.tensor_single_scalar(out=tmpo2[0:1, :], in_=ps[3][0:1, 512:1024],
                                           scalar=1.0 / D, op=ALU.mult)
                    h.tensor_mul(out=castb[0:1, :], in0=lnsF[0:1, 0, :],
                                 in1=lnsF[0:1, 0, :])
                    h.tensor_sub(out=tmpo2[0:1, :], in0=tmpo2[0:1, :], in1=castb[0:1, :])
                    r = h.tensor_single_scalar(out=tmpo2[0:1, :], in_=tmpo2[0:1, :],
                                               scalar=1e-5, op=ALU.add)
                    r.then_inc(sems["vd"], 1)
                R.v(_v1)
                INC("vd")
                W("a", "vd")
                R.a(lambda h: h.activation(out=tmpo2[0:1, :], in_=tmpo2[0:1, :],
                                           func=AF.Sqrt).then_inc(sems["va"], 1))
                INC("va")
                W("v", "va")
                R.v(lambda h: h.reciprocal(out=lnsF[0:1, 1, :], in_=tmpo2[0:1, :])
                    .then_inc(sems["vd"], 1))
                INC("vd")
                W("t", "vd")

                def _t2(h):
                    h.matmul(ps[2][:, 0:512], ones_rowf[:], lnsF[0:1, 0, :],
                             start=True, stop=True)
                    r = h.matmul(ps[2][:, 512:1024], ones_rowf[:], lnsF[0:1, 1, :],
                                 start=True, stop=True)
                    r.then_inc(sems["pe"], 1)
                R.t(_t2)
                INC("pe")
                W("v", "pe")

                def _v2(h):
                    for jc in range(8):
                        h.tensor_sub(out=castb, in0=xbuf[:, jc, :], in1=ps[2][:, 0:512])
                        r = h.tensor_mul(out=ln2c[:, jc, :], in0=castb,
                                         in1=ps[2][:, 512:1024])
                    r.then_inc(sems["vd"], 1)
                R.v(_v2)
                INC("vd")
                W("g", "vd")
                R.g(lambda h, c=c: h.dma_start(
                    out=l3R[:, :, 512 * c:512 * (c + 1)], in_=ln2c[:])
                    .then_inc(sems["sg"], 16))
                INC("sg", 16)
                W("v", "sg", C["sg"])
            W("g", "sg", C["sg"])
            R.g(lambda h: h.collective_compute(
                "AllGather", ALU.bypass, replica_groups=RG,
                ins=[ln3_loc], outs=[ln3_all]).then_inc(sems["vc"], 1))
            INC("vc")

        def p7():
            for (wsrc, off) in ((w_inT_x, 0), (w_inT_z, 256)):
                wr = wsrc.rearrange("(j p) e -> p j e", p=128)
                W("s", "va", C["va"])
                R.s(lambda h, wr=wr: h.dma_start(
                    out=arC_f[:, 0:2048].rearrange("p (j e) -> p j e", j=8),
                    in_=wr).then_inc(sems["sd"], 16))
                INC("sd", 16)
                W("a", "sd")

                def _a(h, off=off):
                    src2 = arC_f[:, 0:2048].rearrange("p (j e) -> p j e", j=8)
                    for jc in range(8):
                        r = h.activation(out=wioT[:, jc, off:off + 256],
                                         in_=src2[:, jc, :], func=AF.Identity)
                    r.then_inc(sems["va"], 1)
                R.a(_a)
                INC("va")
            wxr = wxT_sl.rearrange("(k p) n -> p k n", p=128)
            W("s", "va", C["va"])
            R.s(lambda h: h.dma_start(
                out=arC_f[:, 0:66].rearrange("p (k n) -> p k n", k=2), in_=wxr)
                .then_inc(sems["sd"], 16))
            INC("sd", 16)
            W("a", "sd")

            def _awx(h):
                src2 = arC_f[:, 0:66].rearrange("p (k n) -> p k n", k=2)
                h.activation(out=wxTb[:, 0, :], in_=src2[:, 0, :], func=AF.Identity)
                r = h.activation(out=wxTb[:, 1, :], in_=src2[:, 1, :], func=AF.Identity)
                r.then_inc(sems["va"], 1)
            R.a(_awx)
            INC("va")

            lnall = ln3_all.rearrange("(bk j p) t -> bk p j t", p=128, j=8)
            uR = u_d.rearrange("(k p) t -> p k t", p=128)
            zR = zs_d.rearrange("(k p) t -> p k t", p=128)
            bpR = bcd_part
            rhsT = arA_b[:, 0:16384].rearrange("p (j t) -> p j t", j=8)
            ublk = arC_f[:, 0:4102].rearrange("p (k t) -> p k t", k=2)
            zblk = arB.ap()[:, 0:4096].rearrange("p (k t) -> p k t", k=2)
            uS = arA.ap()[:, 8192:12288].rearrange("p (k t) -> p k t", k=2)
            halo = arA.ap()[:, 12288:12294].rearrange("p (k t) -> p k t", k=2)
            xpev = arA.ap()[:, 12300:14348][0:33, :]
            W("v", "vc", C["vc"])
            for tcb in range(8):
                seq_start = (tcb % 2 == 0)
                W("s", "vc", C["vc"])
                W("s", "va", C["va"])
                W("s", "pe", C["pe"])
                R.s(lambda h, tcb=tcb: h.dma_start(
                    out=rhsT[:], in_=lnall[tcb]).then_inc(sems["sd"], 16))
                INC("sd", 16)
                for th in range(2):
                    W("t", "sd")
                    W("t", "va", C["va"])

                    def _t(h, th=th):
                        for jc in range(8):
                            for Mt in range(4):
                                for tc in range(2):
                                    r = h.matmul(
                                        ps[Mt][:, 512 * tc:512 * (tc + 1)],
                                        wioT[:, jc, 128 * Mt:128 * (Mt + 1)],
                                        rhsT[:, jc, 1024 * th + 512 * tc:
                                             1024 * th + 512 * tc + 512],
                                        start=(jc == 0), stop=(jc == 7))
                        r.then_inc(sems["pe"], 1)
                    R.t(_t)
                    INC("pe")
                    W("a", "pe")

                    def _a(h, th=th):
                        for Mt in range(2):
                            h.activation(out=ublk[:, Mt, 3 + 1024 * th:3 + 1024 * th + 1024],
                                         in_=ps[Mt][:, 0:1024], func=AF.Identity)
                        for Mt in range(2):
                            r = h.activation(out=zblk[:, Mt, 1024 * th:1024 * th + 1024],
                                             in_=ps[Mt + 2][:, 0:1024], func=AF.Silu)
                        r.then_inc(sems["va"], 1)
                    R.a(_a)
                    INC("va")
                W("v", "va")
                if seq_start:
                    R.v(lambda h: h.memset(ublk[:, :, 0:3], 0.0).then_inc(sems["vd"], 1))
                else:
                    R.v(lambda h: h.tensor_copy(out=ublk[:, :, 0:3], in_=halo[:])
                        .then_inc(sems["vd"], 1))
                INC("vd")
                W("v", "vd")

                def _vc(h):
                    for k in range(2):
                        h.tensor_scalar(out=uS[:, k, :], in0=ublk[:, k, 3:2051],
                                        scalar1=conv_sb[:, k, 3:4],
                                        scalar2=convb_sb[:, k:k + 1],
                                        op0=ALU.mult, op1=ALU.add)
                        for sh in (1, 2, 3):
                            r = h.scalar_tensor_tensor(
                                out=uS[:, k, :], in0=ublk[:, k, 3 - sh:2051 - sh],
                                scalar=conv_sb[:, k, 3 - sh:4 - sh],
                                in1=uS[:, k, :], op0=ALU.mult, op1=ALU.add)
                    r = h.tensor_copy(out=halo[:], in_=ublk[:, :, 2048:2051])
                    r.then_inc(sems["vd"], 1)
                R.v(_vc)
                INC("vd")
                W("a", "vd")
                R.a(lambda h: h.activation(out=uS[:, :, :].rearrange("p a b -> p (a b)"),
                                           in_=uS[:, :, :].rearrange("p a b -> p (a b)"),
                                           func=AF.Silu).then_inc(sems["va"], 1))
                INC("va")
                ubf = arA_b[:, 28672:32768].rearrange("p (k t) -> p k t", k=2)
                W("v", "va")
                R.v(lambda h: h.tensor_copy(out=ubf[:, :, :], in_=uS[:, :, :])
                    .then_inc(sems["vd"], 1))
                INC("vd")
                for tc in range(4):
                    W("t", "vd", C["vd"])
                    W("t", "va", C["va"])

                    def _tx(h, tc=tc):
                        for k in range(2):
                            r = h.matmul(ps[0][0:33, 512 * (tc % 2):512 * (tc % 2) + 512],
                                         wxTb[:, k, :],
                                         ubf[:, k, 512 * tc:512 * (tc + 1)],
                                         start=(k == 0), stop=(k == 1))
                        r.then_inc(sems["pe"], 1)
                    R.t(_tx)
                    INC("pe")
                    W("a", "pe")
                    R.a(lambda h, tc=tc: h.activation(
                        out=xpev[:, 512 * tc:512 * (tc + 1)],
                        in_=ps[0][0:33, 512 * (tc % 2):512 * (tc % 2) + 512],
                        func=AF.Identity).then_inc(sems["va"], 1))
                    INC("va")
                W("g", "va", C["va"])
                W("g", "vd", C["vd"])

                def _g(h, tcb=tcb):
                    h.dma_start(out=uR[:, :, 2048 * tcb:2048 * (tcb + 1)],
                                in_=uS[:, :, :]).then_inc(sems["sg"], 16)
                    h.dma_start(out=zR[:, :, 2048 * tcb:2048 * (tcb + 1)],
                                in_=zblk[:, :, :]).then_inc(sems["sg"], 16)
                    h.dma_start(out=bpR[:, 2048 * tcb:2048 * (tcb + 1)],
                                in_=xpev[:]).then_inc(sems["sg"], 16)
                R.g(_g)
                INC("sg", 48)
                W("a", "sg", C["sg"])
                W("v", "sg", C["sg"])
            W("g", "sg", C["sg"])
            R.g(lambda h: h.collective_compute(
                "AllReduce", ALU.add, replica_groups=RG,
                ins=[bcd_part], outs=[bcd_all]).then_inc(sems["vc"], 1))
            INC("vc")

        def p8():
            gR = gated_loc.rearrange("(k p) t -> p k t", p=128)
            uR = u_d.rearrange("(k p) t -> p k t", p=128)
            zR = zs_d.rearrange("(k p) t -> p k t", p=128)
            bcdc = arB.ap()[0:33, 0:2048]
            slotA = arB.ap()[0:1, 14336:16384]
            slotB = arB.ap()[32:33, 14336:16384]
            slotC = arB.ap()[64:65, 14336:16384]
            slotD = arB.ap()[64:65, 0:2048]
            scr1 = arB.ap()[:, 2048:6144].rearrange("p (k t) -> p k t", k=2)
            scr2 = arB.ap()[:, 6144:10240].rearrange("p (k t) -> p k t", k=2)
            tmpy = arB.ap()[:, 10240:14336].rearrange("p (k t) -> p k t", k=2)
            dt_v = arD_f[:, 0:4096].rearrange("p (k t) -> p k t", k=2)
            dtu_v = arD_f[:, 4096:8192].rearrange("p (k t) -> p k t", k=2)
            dA_v = arC_f[:, 0:4096].rearrange("p (k t) -> p k t", k=2)
            hbuf = arC_f[:, 4096:8192].rearrange("p (k t) -> p k t", k=2)
            uc_v = arA.ap()[:, 0:4096].rearrange("p (k t) -> p k t", k=2)
            zc_v = arA.ap()[:, 4096:8192].rearrange("p (k t) -> p k t", k=2)
            yacc = arA.ap()[:, 8192:12288].rearrange("p (k t) -> p k t", k=2)
            gbf = arA_b[:, 24576:28672].rearrange("p (k t) -> p k t", k=2)

            for sc in range(8):
                seq_start = (sc % 2 == 0)
                W("s", "vc", C["vc"])
                W("s", "vd", C["vd"])
                W("s", "va", C["va"])

                def _s(h, sc=sc):
                    h.dma_start(out=bcdc, in_=bcd_all[:, 2048 * sc:2048 * (sc + 1)])                         .then_inc(sems["sd"], 16)
                    h.dma_start(out=uc_v[:, :, :], in_=uR[:, :, 2048 * sc:2048 * (sc + 1)])                         .then_inc(sems["sd"], 16)
                    h.dma_start(out=zc_v[:, :, :], in_=zR[:, :, 2048 * sc:2048 * (sc + 1)])                         .then_inc(sems["sd"], 16)
                R.s(_s)
                INC("sd", 48)
                W("s", "sd", C["sd"])
                R.s(lambda h: h.dma_start(out=slotA, in_=bcdc[32:33, :])
                    .then_inc(sems["sd"], 16))
                INC("sd", 16)
                W("t", "sd")
                W("t", "va", C["va"])

                def _tb(h):
                    for tc in range(4):
                        pst = ps[2] if tc < 2 else ps[3]
                        r = h.matmul(pst[:, 512 * (tc % 2):512 * (tc % 2) + 512],
                                     ones_rowf[:], slotA[0:1, 512 * tc:512 * (tc + 1)],
                                     start=True, stop=True)
                    r.then_inc(sems["pe"], 1)
                R.t(_tb)
                INC("pe")
                W("a", "pe")

                def _asp(h):
                    for k in range(2):
                        for half in range(2):
                            pst = ps[2] if half == 0 else ps[3]
                            h.activation(out=scr1[:, k, 1024 * half:1024 * (half + 1)],
                                         in_=pst[:, 0:1024], func=AF.Identity,
                                         scale=dtw_sb[:, k:k + 1], bias=dtb_sb[:, k:k + 1])
                    h.activation(out=scr2.rearrange("p a b -> p (a b)"),
                                 in_=scr1.rearrange("p a b -> p (a b)"), func=AF.Abs)
                    h.activation(out=scr2.rearrange("p a b -> p (a b)"),
                                 in_=scr2.rearrange("p a b -> p (a b)"), func=AF.Exp,
                                 scale=-1.0)
                    r = h.activation(out=scr2.rearrange("p a b -> p (a b)"),
                                     in_=scr2.rearrange("p a b -> p (a b)"), func=AF.Ln,
                                     bias=1.0)
                    r.then_inc(sems["va"], 1)
                R.a(_asp)
                INC("va")
                W("v", "va")

                def _vsp(h):
                    h.scalar_tensor_tensor(out=dt_v.rearrange("p a b -> p (a b)"),
                                           in0=scr1.rearrange("p a b -> p (a b)"),
                                           scalar=0.0, in1=scr2.rearrange("p a b -> p (a b)"),
                                           op0=ALU.max, op1=ALU.add)
                    r = h.tensor_mul(out=dtu_v.rearrange("p a b -> p (a b)"),
                                     in0=dt_v.rearrange("p a b -> p (a b)"),
                                     in1=uc_v.rearrange("p a b -> p (a b)"))
                    r.then_inc(sems["vd"], 1)
                R.v(_vsp)
                INC("vd")
                for n in range(NST):
                    sB = slotB if n % 2 == 0 else slotA
                    sC = slotC if n % 2 == 0 else slotD
                    obB = onesq[32:33, :] if n % 2 == 0 else ones_rowf[:]
                    obC = onesq[64:65, :]
                    W("s", "pe", C["pe"])

                    def _sn(h, n=n, sB=sB, sC=sC):
                        h.dma_start(out=sB, in_=bcdc[n:n + 1, :]).then_inc(sems["sd"], 16)
                        h.dma_start(out=sC, in_=bcdc[NST + n:NST + n + 1, :]).then_inc(sems["sd"], 16)
                    R.s(_sn)
                    INC("sd", 32)
                    W("t", "vd", C["vd"])
                    W("t", "va", C["va"])
                    W("t", "sd", C["sd"])

                    def _tn(h, n=n, sB=sB, sC=sC, obB=obB, obC=obC):
                        for tc in range(4):
                            pst = ps[0] if tc < 2 else ps[1]
                            h.matmul(pst[:, 512 * (tc % 2):512 * (tc % 2) + 512],
                                     obB, sB[0:1, 512 * tc:512 * (tc + 1)],
                                     start=True, stop=True)
                        for tc in range(4):
                            pst = ps[2] if tc < 2 else ps[3]
                            r = h.matmul(pst[:, 512 * (tc % 2):512 * (tc % 2) + 512],
                                         obC, sC[0:1, 512 * tc:512 * (tc + 1)],
                                         start=True, stop=True)
                        r.then_inc(sems["pe"], 1)
                    R.t(_tn)
                    INC("pe")
                    W("a", "pe", C["pe"])
                    W("a", "vd", C["vd"])

                    def _an(h, n=n):
                        for k in range(2):
                            r = h.activation(out=dA_v[:, k, :], in_=dt_v[:, k, :],
                                             func=AF.Exp, scale=A_sb[:, k, n:n + 1])
                        r.then_inc(sems["va"], 1)
                    R.a(_an)
                    INC("va")
                    W("v", "va")
                    W("v", "pe", C["pe"])

                    def _vn(h, n=n, sc=sc, seq_start=seq_start):
                        for k in range(2):
                            for half in range(2):
                                pst = ps[0] if half == 0 else ps[1]
                                h.tensor_mul(out=hbuf[:, k, 1024 * half:1024 * (half + 1)],
                                             in0=dtu_v[:, k, 1024 * half:1024 * (half + 1)],
                                             in1=pst[:, 0:1024])
                        for k in range(2):
                            ini = 0.0 if seq_start else carry[:, k, n:n + 1]
                            h.tensor_tensor_scan(out=hbuf[:, k, :], data0=dA_v[:, k, :],
                                                 data1=hbuf[:, k, :], initial=ini,
                                                 op0=ALU.mult, op1=ALU.add)
                        h.tensor_copy(out=carry[:, 0, n:n + 1], in_=hbuf[:, 0, 2047:2048])
                        h.tensor_copy(out=carry[:, 1, n:n + 1], in_=hbuf[:, 1, 2047:2048])
                        for k in range(2):
                            for half in range(2):
                                pst = ps[2] if half == 0 else ps[3]
                                h.tensor_mul(out=tmpy[:, k, 1024 * half:1024 * (half + 1)],
                                             in0=hbuf[:, k, 1024 * half:1024 * (half + 1)],
                                             in1=pst[:, 0:1024])
                        if n == 0:
                            r = h.tensor_copy(out=yacc.rearrange("p a b -> p (a b)"),
                                              in_=tmpy.rearrange("p a b -> p (a b)"))
                        else:
                            r = h.tensor_add(out=yacc.rearrange("p a b -> p (a b)"),
                                             in0=yacc.rearrange("p a b -> p (a b)"),
                                             in1=tmpy.rearrange("p a b -> p (a b)"))
                        r.then_inc(sems["vd"], 1)
                    R.v(_vn)
                    INC("vd")
                W("v", "vd")

                def _vg(h):
                    for k in range(2):
                        h.scalar_tensor_tensor(out=yacc[:, k, :], in0=uc_v[:, k, :],
                                               scalar=dpar_sb[:, k:k + 1],
                                               in1=yacc[:, k, :],
                                               op0=ALU.mult, op1=ALU.add)
                    r = h.tensor_mul(out=gbf.rearrange("p a b -> p (a b)"),
                                     in0=yacc.rearrange("p a b -> p (a b)"),
                                     in1=zc_v.rearrange("p a b -> p (a b)"))
                    r.then_inc(sems["vd"], 1)
                R.v(_vg)
                INC("vd")
                W("g", "vd")
                R.g(lambda h, sc=sc: h.dma_start(
                    out=gR[:, :, 2048 * sc:2048 * (sc + 1)], in_=gbf[:, :, :])
                    .then_inc(sems["sg"], 16))
                INC("sg", 16)
                W("v", "sg", C["sg"])
            W("g", "sg", C["sg"])
            R.g(lambda h: h.collective_compute(
                "AllGather", ALU.bypass, replica_groups=RG,
                ins=[gated_loc], outs=[gated_all]).then_inc(sems["vc"], 1))
            INC("vc")

        def p9():
            woR = w_outT_in.rearrange("(k p) dd -> p k dd", p=128)
            woT = arC_b[:, 0:16384].rearrange("p (k dd) -> p k dd", k=16)
            for half in range(2):
                stg9 = arD_f[:, 0:8192] if half == 0 else arB.ap()[:, 0:8192]
                W("s", "va", C["va"])
                W("s", "vd", C["vd"])
                R.s(lambda h, half=half, stg9=stg9: h.dma_start(
                    out=stg9.rearrange("p (k dd) -> p k dd", k=8),
                    in_=woR[:, 8 * half:8 * (half + 1), :]).then_inc(sems["sd"], 16))
                INC("sd", 16)
                W("a", "sd")
                W("a", "vd", C["vd"])
                W("a", "sg", C["sg"])

                def _a(h, half=half, stg9=stg9):
                    src2 = stg9.rearrange("p (k dd) -> p k dd", k=8)
                    for k in range(8):
                        r = h.activation(out=woT[:, 8 * half + k, :], in_=src2[:, k, :],
                                         func=AF.Identity)
                    r.then_inc(sems["va"], 1)
                R.a(_a)
                INC("va")
            if DBG:
                W("g", "va", C["va"])
                R.g(lambda h: h.dma_start(out=dbg_wo2, in_=woT[:, :, :])
                    .then_inc(sems["so"], 16))
                INC("so", 16)
            roff = nc.gpsimd.alloc_register("roff")
            ga_h = gated_all.tensor
            x2R = x2_d.rearrange("(j p) t -> p j t", p=128)
            x3R = x3_d.rearrange("(j p) t -> p j t", p=128)
            outR3 = outT.rearrange("(j p) t -> p j t", p=128)
            grhs = arD_b[:, 0:8192].rearrange("p (k t) -> p k t", k=16)
            x2c = arD_f[:, 4096:4608]
            tmpr = arD_f[:, 4608:5120]
            W("g", "vc", C["vc"])
            for f in range(4):
                W("g", "pe", C["pe"])
                W("g", "vd", C["vd"])
                W("g", "va", C["va"])

                def _s(h, f=f):
                    pid = h.partition_id()
                    h.reg_alu(roff, pid, TOK, ALU.mult)
                    h.reg_alu(roff, roff, 512 * f, ALU.add)
                    h.dma_start(out=grhs[:, :, :],
                                in_=bass.AP(ga_h, roff,
                                            [[T16, 128], [128 * T16, 16], [1, 512]]))                         .then_inc(sems["sg"], 16)
                R.g(_s)
                INC("sg", 16)
                if DBG and f == 0:
                    W("g", "sg", C["sg"])
                    W("g", "va", C["va"])
                    def _gdw(h):
                        h.dma_start(out=dbg_gr, in_=grhs[:, :, :]).then_inc(sems["so"], 16)
                        h.dma_start(out=dbg_wo, in_=woT[:, :, :]).then_inc(sems["so"], 16)
                    R.g(_gdw)
                    INC("so", 32)
                    W("t", "so", C["so"])
                for Mt in range(8):
                    pst = ps[0] if Mt % 2 == 0 else ps[1]
                    W("t", "sg", C["sg"])
                    W("t", "va", C["va"])

                    def _t(h, Mt=Mt, pst=pst):
                        for ec in range(16):
                            r = h.matmul(pst[:, 0:512], woT[:, ec, 128 * Mt:128 * (Mt + 1)],
                                         grhs[:, ec, :], start=(ec == 0), stop=(ec == 15))
                        r.then_inc(sems["pe"], 1)
                    R.t(_t)
                    INC("pe")
                    W("s", "vd", C["vd"])
                    W("s", "sg", C["sg"])
                    R.s(lambda h, Mt=Mt, f=f: h.dma_start(
                        out=x2c, in_=x2R[:, Mt, 512 * f:512 * (f + 1)])
                        .then_inc(sems["sd"], 16))
                    INC("sd", 16)
                    W("a", "pe")
                    W("a", "vd", C["vd"])
                    R.a(lambda h, pst=pst: h.activation(
                        out=tmpr, in_=pst[:, 0:512], func=AF.Identity)
                        .then_inc(sems["va"], 1))
                    INC("va")
                    W("v", "va")
                    W("v", "sd")
                    R.v(lambda h: h.tensor_add(out=x2c, in0=x2c, in1=tmpr)
                        .then_inc(sems["vd"], 1))
                    INC("vd")
                    W("g", "vd")

                    def _g(h, Mt=Mt, f=f):
                        r = h.dma_start(out=x3R[:, Mt, 512 * f:512 * (f + 1)], in_=x2c)
                        r.then_inc(sems["sg"], 16)
                        if STAGE == 3:
                            h.dma_start(out=outR3[:, Mt, 512 * f:512 * (f + 1)],
                                        in_=x2c).then_inc(sems["so"], 16)
                    R.g(_g)
                    INC("sg", 16)
                    if STAGE == 3:
                        INC("so", 16)
                    W("v", "sg", C["sg"])

        def p10():
            s1R = s1T_d.rearrange("(j p) t -> p j t", p=128)
            s2R = s2T_d.rearrange("(j p) t -> p j t", p=128)
            W("s", "va", C["va"])
            W("s", "vd", C["vd"])
            W("s", "pe", C["pe"])
            W("s", "sg", C["sg"])

            def _s(h):
                h.dma_start(out=s1T[:], in_=s1R).then_inc(sems["sd"], 16)
                h.dma_start(out=s2T[:], in_=s2R).then_inc(sems["sd"], 16)
            R.s(_s)
            INC("sd", 32)
            W("t", "sd")
            ffn_chunks(x3_d, x1_d, STAGE == 4)

        p0()
        p1()
        if DBG:
            W("g", "vd", C["vd"])
            def _gdbg(h):
                h.dma_start(out=dbg_ln1.rearrange("(j p) t -> p j t", p=128),
                            in_=ln1[:]).then_inc(sems["so"], 16)
            R.g(_gdbg)
            INC("so", 16)
        p2()
        p3()
        if STAGE >= 2:
            p4()
            ffn_chunks(x1_d, x2_d, STAGE == 2)
        if STAGE >= 3:
            p6()
            p7()
            p8()
            p9()
        if STAGE >= 4:
            p10()

        if DBG and STAGE >= 3:
            W("g", "sg", C["sg"])
            W("g", "vc", C["vc"])

            def _gm(h):
                h.dma_start(out=dbg_u, in_=u_d[:, 0:2048]).then_inc(sems["so"], 16)
                h.dma_start(out=dbg_z, in_=zs_d[:, 0:2048]).then_inc(sems["so"], 16)
                h.dma_start(out=dbg_bcd, in_=bcd_all[:, 0:2048]).then_inc(sems["so"], 16)
                h.dma_start(out=dbg_g, in_=gated_all[:, 0:2048]).then_inc(sems["so"], 16)
                h.dma_start(out=dbg_l3, in_=ln3_all[0:D, 0:2048]).then_inc(sems["so"], 16)
            R.g(_gm)
            INC("so", 80)
        W("g", "sg", C["sg"])
        if C["so"]:
            W("g", "so", C["so"])
        R.g(lambda h: h.nop())

        with nc.Block() as blk:
            @blk.sync
            def _(h):
                for fn in R.ops["s"]:
                    fn(h)

            @blk.scalar
            def _(h):
                for fn in R.ops["a"]:
                    fn(h)

            @blk.vector
            def _(h):
                for fn in R.ops["v"]:
                    fn(h)

            @blk.tensor
            def _(h):
                for fn in R.ops["t"]:
                    fn(h)

            @blk.gpsimd
            def _(h):
                for fn in R.ops["g"]:
                    fn(h)

    return nc


_NC_CACHE = None


def _get_nc():
    global _NC_CACHE
    if _NC_CACHE is None:
        _NC_CACHE = build_nc()
    return _NC_CACHE


def _prep_inmaps(inputs):
    f32 = lambda a: np.ascontiguousarray(np.asarray(a, dtype=np.float32))
    x = f32(inputs["L0_attn_a_x"])
    circ = {k: np.tile(f32(inputs[f"L0_attn_a_w{k}_c"]), CTILE // D) for k in "qkvo"}
    bias_cols = {k: np.ascontiguousarray(
        f32(inputs[f"L0_attn_a_w{k}_b"]).reshape(8, 128).T) for k in "qkvo"}
    wvb_row = f32(inputs["L0_attn_a_wv_b"]).reshape(1, D)
    w1T = np.ascontiguousarray(f32(inputs["L0_ffn_w1"]).T)
    w2T = np.ascontiguousarray(f32(inputs["L0_ffn_w2"]).T)
    b1c = np.ascontiguousarray(f32(inputs["L0_ffn_b1"]).reshape(HDIM // 128, 128).T)
    b2c = np.ascontiguousarray(f32(inputs["L0_ffn_b2"]).reshape(D // 128, 128).T)
    inw = f32(inputs["L0_attn_b_in_proj_w"])
    convw = f32(inputs["L0_attn_b_conv_w"])
    convb = f32(inputs["L0_attn_b_conv_b"])
    xpw = f32(inputs["L0_attn_b_x_proj_w"])
    dtw = f32(inputs["L0_attn_b_dt_proj_w"])[:, 0]
    dtb = f32(inputs["L0_attn_b_dt_proj_b"])
    alog = f32(inputs["L0_attn_b_A_log"])
    dpar = f32(inputs["L0_attn_b_D_param"])
    w_outT = np.ascontiguousarray(f32(inputs["L0_attn_b_out_proj_w"]).T)

    def cols2(v256):
        return np.ascontiguousarray(v256.reshape(2, 128).T)

    in_maps = []
    for c in range(NCORE):
        b, hh = c // 2, c % 2
        sl = slice(DSL * c, DSL * (c + 1))
        m = dict(
            xT_full=np.ascontiguousarray(np.concatenate(
                [x[b, TOK * hh:TOK * (hh + 1), :],
                 x[b, TOK * (1 - hh):TOK * (2 - hh), :]], axis=0).T),
            x_ownT=np.ascontiguousarray(x[b, TOK * hh:TOK * (hh + 1), :].T),
            w1T=w1T, w2T=w2T, b1_c=b1c, b2_c=b2c,
            w1N=f32(inputs["L0_ffn_w1"]), w2N=f32(inputs["L0_ffn_w2"]),
            wvb_row=wvb_row,
            w_inT_x=np.ascontiguousarray(inw[sl, :].T),
            w_inT_z=np.ascontiguousarray(inw[DI + DSL * c:DI + DSL * (c + 1), :].T),
            conv_w_c=np.ascontiguousarray(
                convw[sl].reshape(2, 128, 4).transpose(1, 0, 2)),
            conv_b_c=cols2(convb[sl]),
            wxT_sl=np.ascontiguousarray(xpw[:, sl].T),
            dtw_c=cols2(dtw[sl]),
            dtb_c=cols2(dtb[sl]),
            alog_c=np.ascontiguousarray(
                alog[sl].reshape(2, 128, NST).transpose(1, 0, 2)),
            dpar_c=cols2(dpar[sl]),
            w_outT=w_outT,
        )
        for k in "qkvo":
            m[f"c{k}2"] = circ[k]
            m[f"w{k}b_c"] = bias_cols[k]
        in_maps.append(m)
    return in_maps


def kernel(**inputs):
    nc = _get_nc()
    in_maps = _prep_inmaps(inputs)
    res = run_bass_kernel_spmd(nc, in_maps, core_ids=list(range(NCORE)))
    out = np.empty((4, S, D), dtype=np.float32)
    for c in range(NCORE):
        b, hh = c // 2, c % 2
        out[b, TOK * hh:TOK * (hh + 1), :] = res.results[c]["outT"].T
    return out


# version 05
# speedup vs baseline: 1.0035x; 1.0035x over previous
import numpy as np
import ml_dtypes
from contextlib import ExitStack

import concourse.bass as bass
import concourse.mybir as mybir
from concourse.bass_utils import run_bass_kernel_spmd

F32 = mybir.dt.float32
BF16 = mybir.dt.bfloat16
AF = mybir.ActivationFunctionType
ALU = mybir.AluOpType

D = 1024
S = 4096
TOK = 2048
H, HD = 16, 64
HDIM = 4096
DI = 2048
DSL = 256
NST = 16
T16 = 16384
NCORE = 8
CTILE = 132 * D

STAGE = 4

RG = [[0, 1, 2, 3, 4, 5, 6, 7]]


class Rec:
    def __init__(self):
        self.ops = {e: [] for e in "savtg"}

    def s(self, fn): self.ops["s"].append(fn)
    def a(self, fn): self.ops["a"].append(fn)
    def v(self, fn): self.ops["v"].append(fn)
    def t(self, fn): self.ops["t"].append(fn)
    def g(self, fn): self.ops["g"].append(fn)


def build_nc():
    nc = bass.Bass("TRN2", target_bir_lowering=False)

    def inp(name, shape, dt=F32):
        return nc.dram_tensor(name, shape, dt, kind="ExternalInput").ap()

    xT_full = inp("xT_full", [D, S])
    x_ownT = inp("x_ownT", [D, TOK])
    circ2 = {k: inp(f"c{k}2", [CTILE]) for k in "qkvo"}
    bias_in = {k: inp(f"w{k}b_c", [128, 8]) for k in "qkvo"}
    wvb_row_in = inp("wvb_row", [1, D])
    w1T_in = inp("w1T", [D, HDIM])
    w2T_in = inp("w2T", [HDIM, D])
    w1N_in = inp("w1N", [HDIM, D])
    w2N_in = inp("w2N", [D, HDIM])
    b1_c = inp("b1_c", [128, HDIM // 128])
    b2_c = inp("b2_c", [128, D // 128])
    w_inT_x = inp("w_inT_x", [D, DSL])
    w_inT_z = inp("w_inT_z", [D, DSL])
    conv_w_c = inp("conv_w_c", [128, 2, 4])
    conv_b_c = inp("conv_b_c", [128, 2])
    wxT_sl = inp("wxT_sl", [DSL, 33])
    dtw_c = inp("dtw_c", [128, 2])
    dtb_c = inp("dtb_c", [128, 2])
    alog_c = inp("alog_c", [128, 2, NST])
    dpar_c = inp("dpar_c", [128, 2])
    w_outT_in = inp("w_outT", [DI, D])

    outT = nc.dram_tensor("outT", [D, TOK], F32, kind="ExternalOutput").ap()
    DBG = False
    if DBG:
        dbg_ln1 = nc.dram_tensor("dbg_ln1", [D, S], BF16, kind="ExternalOutput").ap()
        dbg_om = nc.dram_tensor("dbg_om", [D, TOK], BF16, kind="ExternalOutput").ap()
        dbg_kT = nc.dram_tensor("dbg_kT", [128, S], BF16, kind="ExternalOutput").ap()
        dbg_qT = nc.dram_tensor("dbg_qT", [128, TOK], BF16, kind="ExternalOutput").ap()
        dbg_vtm = nc.dram_tensor("dbg_vtm", [128, 32 * 130], BF16, kind="ExternalOutput").ap()
        dbg_neg = nc.dram_tensor("dbg_neg", [128, 2], F32, kind="ExternalOutput").ap()
        dbg_sc = nc.dram_tensor("dbg_sc", [2, 2 * NST], F32, kind="ExternalOutput").ap()
        dbg_m = nc.dram_tensor("dbg_m", [128, 2], BF16, kind="ExternalOutput").ap()
        dbg_al = nc.dram_tensor("dbg_al", [128, 48], F32, kind="ExternalOutput").ap()
        dbg_ln2 = nc.dram_tensor("dbg_ln2", [128, 8, 512], BF16, kind="ExternalOutput").ap()
        dbg_h = nc.dram_tensor("dbg_h", [128, 4, 512], BF16, kind="ExternalOutput").ap()
        dbg_xb = nc.dram_tensor("dbg_xb", [128, 8, 512], F32, kind="ExternalOutput").ap()
        dbg_u = nc.dram_tensor("dbg_u", [DSL, 2048], F32, kind="ExternalOutput").ap()
        dbg_z = nc.dram_tensor("dbg_z", [DSL, 2048], F32, kind="ExternalOutput").ap()
        dbg_bcd = nc.dram_tensor("dbg_bcd", [33, 2048], F32, kind="ExternalOutput").ap()
        dbg_g = nc.dram_tensor("dbg_g", [DI, 2048], BF16, kind="ExternalOutput").ap()
        dbg_l3 = nc.dram_tensor("dbg_l3", [D, 2048], BF16, kind="ExternalOutput").ap()
        dbg_gr = nc.dram_tensor("dbg_gr", [128, 16, 512], BF16, kind="ExternalOutput").ap()
        dbg_wo = nc.dram_tensor("dbg_wo", [128, 16, 1024], BF16, kind="ExternalOutput").ap()
        dbg_wo2 = nc.dram_tensor("dbg_wo2", [128, 16, 1024], BF16, kind="ExternalOutput").ap()
        dbg_sq = nc.dram_tensor("dbg_sq", [128, 4096], BF16, kind="ExternalOutput").ap()

    def scratch(name, shape, dt=F32):
        return nc.dram_tensor(name, shape, dt, kind="Internal").ap()

    x1_d = scratch("x1_d", [D, TOK])
    x2_d = scratch("x2_d", [D, TOK])
    x3_d = scratch("x3_d", [D, TOK])
    s1T_d = scratch("s1T_d", [D, HDIM], BF16)
    s2T_d = scratch("s2T_d", [HDIM, D], BF16)
    ln3_loc = scratch("ln3_loc", [D, TOK], BF16)
    ln3_all = scratch("ln3_all", [NCORE * D, TOK], BF16)
    bcd_part = scratch("bcd_part", [33, T16])
    bcd_all = scratch("bcd_all", [33, T16])
    u_d = scratch("u_d", [DSL, T16])
    zs_d = scratch("zs_d", [DSL, T16])
    gated_loc = scratch("gated_loc", [DSL, T16], BF16)
    gated_all = scratch("gated_all", [DI, T16], BF16)

    es = ExitStack()
    with es:
        sbt = lambda name, shape, dt: es.enter_context(nc.sbuf_tensor(name, shape, dt))

        bias_q = sbt("bias_q", [128, 8], F32)
        bias_k = sbt("bias_k", [128, 8], F32)
        bias_v = sbt("bias_v", [128, 8], F32)
        bias_o = sbt("bias_o", [128, 8], F32)
        vbrow = sbt("vbrow", [1, D], BF16)
        ones_col = sbt("ones_col", [128, 1], BF16)
        ones_row = sbt("ones_row", [1, 128], BF16)
        ones_rowf = sbt("ones_rowf", [1, 128], F32)
        onesq = sbt("onesq", [128, 128], F32)
        maskAB = sbt("maskAB", [128, 2], BF16)
        iota_col = sbt("iota_col", [128, 1], mybir.dt.int32)
        negMrow = sbt("negMrow", [1, 2], F32)
        negMb = sbt("negMb", [128, 2], F32)
        nsc = sbt("nsc", [1, 8], F32)
        carry = sbt("carry", [128, 2, NST], F32)
        A_sb = sbt("A_sb", [128, 2, NST], F32)
        alpha1 = sbt("alpha1", [128, HDIM // 128], F32)
        alpha2 = sbt("alpha2", [128, D // 128], F32)
        b1cols = sbt("b1cols", [128, HDIM // 128], F32)
        b2cols = sbt("b2cols", [128, D // 128], F32)
        conv_sb = sbt("conv_sb", [128, 2, 4], F32)
        convb_sb = sbt("convb_sb", [128, 2], F32)
        dtw_sb = sbt("dtw_sb", [128, 2], F32)
        dtb_sb = sbt("dtb_sb", [128, 2], F32)
        dpar_sb = sbt("dpar_sb", [128, 2], F32)
        wioT = sbt("wioT", [128, 8, 512], BF16)
        wxTb = sbt("wxTb", [128, 2, 33], BF16)

        arA = sbt("arA", [128, 16384], F32)
        arB = sbt("arB", [128, 16384], F32)
        arC = sbt("arC", [128, 8192], F32)
        arD = sbt("arD", [128, 8192], F32)

        ps = [es.enter_context(nc.psum_tensor(f"ps{i}", [128, 1024], F32))
              for i in range(4)]

        sems = {}
        for name in ("sd", "sg", "ssc", "pe", "va", "vd", "vg", "vc", "so"):
            sems[name] = es.enter_context(nc.semaphore(name))
        C = {k: 0 for k in sems}

        R = Rec()

        def W(eng, name, val=None):
            v = C[name] if val is None else val
            sem = sems[name]
            getattr(R, eng)(lambda h: h.wait_ge(sem, v))

        def INC(name, n=1):
            C[name] += n
            return C[name]

        arA_b = arA.ap().bitcast(BF16)
        arB_b = arB.ap().bitcast(BF16)
        arC_b = arC.ap().bitcast(BF16)
        arD_b = arD.ap().bitcast(BF16)
        arD_f = arD.ap()

        ln1 = arA_b.rearrange("p (j t) -> p j t", j=8)
        CT = {k: arB_b[:, 8192 * i:8192 * (i + 1)].rearrange("p (j i) -> p j i", j=8)
              for i, k in enumerate("qkvo")}
        om = arC_b.rearrange("p (j t) -> p j t", j=8)
        stage = arC.ap().rearrange("p (r j t) -> p r j t", r=2, j=8)

        ln_xb = arD_b[:, 0:4096].rearrange("p (j t) -> p j t", j=8)
        ln_x2 = arD_b[:, 4096:8192].rearrange("p (j t) -> p j t", j=8)
        lns_ln = arD_f[:, 4096:6144].rearrange("p (k t) -> p k t", k=4)
        tmp_ln = arD_f[:, 6144:6656]

        kTv = arD_b[:, 0:4096]
        vtm = arD_b[:, 4096:8256].rearrange("p (tt c) -> p tt c", c=130)
        qTv = arD_b[:, 8256:10304]
        pT = arD_b[:, 10304:14400].rearrange("p (r t) -> p r t", r=4)
        rb64 = arD_f[:, 5152:6176]
        lns_at = arD_f[:, 6176:7200].rearrange("p (k t) -> p k t", k=2)
        lns_at2 = arD_f[:, 5152:6176].rearrange("p (k t) -> p k t", k=2)
        x0c = arD_f[:, 0:512]
        x1c = arD_f[:, 512:1024]
        tmpo = arD_f[:, 1024:1536]

        def p0():
            def _s(h):
                for t, src in ((bias_q, bias_in["q"]), (bias_k, bias_in["k"]),
                               (bias_v, bias_in["v"]), (bias_o, bias_in["o"]),
                               (arD_f[0:1, 0:1024], wvb_row_in), (b1cols, b1_c), (b2cols, b2_c),
                               (conv_sb, conv_w_c), (convb_sb, conv_b_c),
                               (dtw_sb, dtw_c), (dtb_sb, dtb_c), (dpar_sb, dpar_c),
                               (A_sb, alog_c)):
                    h.dma_start(out=t[:], in_=src).then_inc(sems["sd"], 16)
            R.s(_s)
            INC("sd", 16 * 13)

            R.g(lambda h: h.iota(iota_col[:], pattern=[[0, 1]], base=0,
                                 channel_multiplier=1).then_inc(sems["vg"], 1))
            INC("vg")
            W("v", "sd")
            W("v", "vg")

            def _v(h):
                h.memset(ones_col[:], 1.0)
                h.memset(ones_row[:], 1.0)
                h.memset(ones_rowf[:], 1.0)
                h.memset(onesq[:], 1.0)
                h.memset(nsc[:], 0.0)
                h.tensor_single_scalar(out=maskAB[:, 0:1], in_=iota_col[:], scalar=64, op=ALU.is_lt)
                h.tensor_single_scalar(out=maskAB[:, 1:2], in_=iota_col[:], scalar=64, op=ALU.is_ge)
                h.tensor_copy(out=vbrow[:], in_=arD_f[0:1, 0:1024]).then_inc(sems["vd"], 1)
            R.v(_v)
            INC("vd")
            W("a", "sd")
            R.a(lambda h: h.activation(out=A_sb[:], in_=A_sb.ap() if False else A_sb[:],
                                       func=AF.Exp).then_inc(sems["va"], 1))
            INC("va")
            W("v", "va")
            R.v(lambda h: h.tensor_single_scalar(out=A_sb[:], in_=A_sb[:],
                                                 scalar=-1.0, op=ALU.mult)
                .then_inc(sems["vd"], 1))
            INC("vd")

        def ln_rec(src_dram, dst_store, ntok, keep_store=None):
            nchunks = ntok // 512
            srcR = src_dram.rearrange("(j p) t -> p j t", p=128)
            for c in range(nchunks):
                st = stage[:, c % 2]
                W("s", "vd")
                R.s(lambda h, st=st, c=c: h.dma_start(
                    out=st[:], in_=srcR[:, :, 512 * c:512 * (c + 1)])
                    .then_inc(sems["sd"], 16))
                INC("sd", 16)
                W("a", "sd")
                W("a", "pe", C["pe"])

                def _a(h, st=st):
                    for jc in range(8):
                        h.activation(out=ln_xb[:, jc, :], in_=st[:, jc, :], func=AF.Identity)
                    for jc in range(8):
                        r = h.activation(out=ln_x2[:, jc, :], in_=st[:, jc, :], func=AF.Square)
                    r.then_inc(sems["va"], 1)
                R.a(_a)
                INC("va")
                W("t", "va")
                W("t", "vd", C["vd"])

                def _t(h):
                    for jc in range(8):
                        h.matmul(ps[3][0:1, 0:512], ones_col[:], ln_xb[:, jc, :],
                                 start=(jc == 0), stop=(jc == 7))
                    for jc in range(8):
                        r = h.matmul(ps[3][32:33, 512:1024], ones_col[:], ln_x2[:, jc, :],
                                     start=(jc == 0), stop=(jc == 7))
                    r.then_inc(sems["pe"], 1)
                R.t(_t)
                INC("pe")
                W("v", "pe")

                def _v1(h):
                    h.tensor_single_scalar(out=lns_ln[0:1, 0, :], in_=ps[3][0:1, 0:512],
                                           scalar=1.0 / D, op=ALU.mult)
                    h.tensor_single_scalar(out=lns_ln[0:1, 1, :], in_=ps[3][32:33, 512:1024],
                                           scalar=1.0 / D, op=ALU.mult)
                    h.tensor_mul(out=lns_ln[0:1, 2, :], in0=lns_ln[0:1, 0, :],
                                 in1=lns_ln[0:1, 0, :])
                    h.tensor_sub(out=lns_ln[0:1, 3, :], in0=lns_ln[0:1, 1, :],
                                 in1=lns_ln[0:1, 2, :])
                    r = h.tensor_single_scalar(out=lns_ln[0:1, 3, :], in_=lns_ln[0:1, 3, :],
                                               scalar=1e-5, op=ALU.add)
                    r.then_inc(sems["vd"], 1)
                R.v(_v1)
                INC("vd")
                W("a", "vd")
                R.a(lambda h: h.activation(out=lns_ln[0:1, 3, :], in_=lns_ln[0:1, 3, :],
                                           func=AF.Sqrt).then_inc(sems["va"], 1))
                INC("va")
                W("v", "va")
                R.v(lambda h: h.reciprocal(out=lns_ln[0:1, 1, :], in_=lns_ln[0:1, 3, :])
                    .then_inc(sems["vd"], 1))
                INC("vd")
                W("t", "vd")

                def _t2(h):
                    h.matmul(ps[2][:, 0:512], ones_rowf[:], lns_ln[0:1, 0, :],
                             start=True, stop=True)
                    h.matmul(ps[2][:, 512:1024], ones_rowf[:], lns_ln[0:1, 1, :],
                             start=True, stop=True).then_inc(sems["pe"], 1)
                R.t(_t2)
                INC("pe")
                W("v", "pe")

                def _v2(h, st=st, c=c):
                    for jc in range(8):
                        h.tensor_sub(out=tmp_ln[:], in0=st[:, jc, :], in1=ps[2][:, 0:512])
                        r = h.tensor_mul(out=dst_store(jc, c), in0=tmp_ln[:],
                                         in1=ps[2][:, 512:1024])
                        if keep_store is not None:
                            r = h.tensor_copy(out=keep_store(jc, c), in_=st[:, jc, :])
                    r.then_inc(sems["vd"], 1)
                R.v(_v2)
                INC("vd")

        def p1():
            ln_rec(xT_full, lambda jc, c: ln1[:, jc, 512 * c:512 * (c + 1)], S)

        def p2():
            for k in "qkvo":
                for jc in range(8):
                    st2 = stage[:, jc % 2, 0:2]
                    W("s", "va")
                    W("s", "vd")
                    off = 2 * D - 128 * jc
                    R.s(lambda h, off=off, k=k, st2=st2: h.dma_start(
                        out=st2[:],
                        in_=bass.AP(circ2[k].tensor, off, [[D - 1, 128], [1, D]]))
                        .then_inc(sems["sd"], 16))
                    INC("sd", 16)
                    W("a", "sd")
                    R.a(lambda h, k=k, jc=jc, st2=st2: h.activation(
                        out=CT[k][:, jc, :], in_=st2.rearrange("p a t -> p (a t)"),
                        func=AF.Identity).then_inc(sems["va"], 1))
                    INC("va")

        def p3():
            dP = 16384
            qT_h = qTv.tensor
            QOFF0 = qTv.offset

            prj = [ps[2][:, 0:512], ps[2][:, 512:1024],
                   ps[3][:, 0:512], ps[3][:, 512:1024]]
            prj_free = [C["va"]] * 4

            W("v", "vd", C["vd"])
            R.v(lambda h: h.memset(
                bass.AP(vtm.tensor, vtm.offset + 64, [[dP, 128], [130, 32], [65, 2]]),
                1.0).then_inc(sems["vd"], 1))
            INC("vd")
            vtm_ones_vd = C["vd"]

            for hp in range(8):
                for f in range(8):
                    slot = f % 4
                    W("t", "va", prj_free[slot])

                    def _t(h, hp=hp, f=f, slot=slot):
                        for jc in range(8):
                            r = h.matmul(prj[slot],
                                         CT["k"][:, jc, 128 * hp:128 * (hp + 1)],
                                         ln1[:, jc, 512 * f:512 * (f + 1)],
                                         start=(jc == 0), stop=(jc == 7))
                        r.then_inc(sems["pe"], 1)
                    R.t(_t)
                    INC("pe")
                    W("a", "pe")
                    R.a(lambda h, f=f, slot=slot, hp=hp: h.activation(
                        out=kTv[:, 512 * f:512 * (f + 1)], in_=prj[slot],
                        func=AF.Identity, bias=bias_k[:, hp:hp + 1])
                        .then_inc(sems["va"], 1))
                    INC("va")
                    prj_free[slot] = C["va"]

                for tt in range(32):
                    slot = tt % 4
                    W("t", "va", prj_free[slot])

                    def _t(h, hp=hp, tt=tt, slot=slot):
                        for jc in range(8):
                            h.matmul(prj[slot][:, 0:128],
                                     ln1[:, jc, 128 * tt:128 * (tt + 1)],
                                     CT["v"][:, jc, 128 * hp:128 * (hp + 1)],
                                     start=(jc == 0), stop=False)
                        r = h.matmul(prj[slot][:, 0:128], ones_row[:],
                                     vbrow[0:1, 128 * hp:128 * (hp + 1)],
                                     start=False, stop=True)
                        r.then_inc(sems["pe"], 1)
                    R.t(_t)
                    INC("pe")
                    W("a", "pe")

                    def _a(h, tt=tt, slot=slot):
                        h.activation(out=vtm[:, tt, 0:64], in_=prj[slot][:, 0:64],
                                     func=AF.Identity)
                        r = h.activation(out=vtm[:, tt, 65:129], in_=prj[slot][:, 64:128],
                                         func=AF.Identity)
                        r.then_inc(sems["va"], 1)
                    R.a(_a)
                    INC("va")
                    prj_free[slot] = C["va"]

                for f in range(4):
                    slot = f % 4
                    W("t", "va", prj_free[slot])

                    def _t(h, hp=hp, f=f, slot=slot):
                        for jc in range(8):
                            r = h.matmul(prj[slot],
                                         CT["q"][:, jc, 128 * hp:128 * (hp + 1)],
                                         ln1[:, jc, 512 * f:512 * (f + 1)],
                                         start=(jc == 0), stop=(jc == 7))
                        r.then_inc(sems["pe"], 1)
                    R.t(_t)
                    INC("pe")
                    W("a", "pe")
                    R.a(lambda h, f=f, slot=slot, hp=hp: h.activation(
                        out=qTv[:, 512 * f:512 * (f + 1)], in_=prj[slot],
                        func=AF.Identity, bias=bias_q[:, hp:hp + 1])
                        .then_inc(sems["va"], 1))
                    INC("va")
                    prj_free[slot] = C["va"]

                sqscr = pT.rearrange("p r t -> p (r t)")
                qacc = carry[0:2, 0, 0:1]
                kacc = carry[0:2, 0, 1:2]
                ntmp = carry[0:2, 0, 2:3]
                negM2 = carry[0:2, 0, 3:4]
                qmaxs = carry[0:2, 1, 0:4]
                kmaxs = carry[0:2, 1, 4:12]
                for (srcv, nch, maxs, acc) in ((qTv, 4, qmaxs, qacc),
                                               (kTv, 8, kmaxs, kacc)):
                    W("a", "vd", C["vd"])
                    R.a(lambda h, srcv=srcv, nch=nch: h.activation(
                        out=sqscr[:, 0:512 * nch], in_=srcv, func=AF.Square)
                        .then_inc(sems["va"], 1))
                    INC("va")
                    W("t", "va")
                    for ch in range(nch):
                        pst = ps[0] if ch % 2 == 0 else ps[1]
                        W("t", "vd")
                        R.t(lambda h, ch=ch, pst=pst: h.matmul(
                            pst[0:2, 0:512], maskAB[:],
                            sqscr[:, 512 * ch:512 * (ch + 1)],
                            start=True, stop=True).then_inc(sems["pe"], 1))
                        INC("pe")
                        W("v", "pe")
                        R.v(lambda h, ch=ch, pst=pst, maxs=maxs: h.reduce_max(
                            out=maxs[:, ch:ch + 1], in_=pst[0:2, 0:512],
                            axis=mybir.AxisListType.X).then_inc(sems["vd"], 1))
                        INC("vd")
                    W("v", "vd")
                    R.v(lambda h, maxs=maxs, acc=acc, nch=nch: h.reduce_max(
                        out=acc, in_=maxs[:, 0:nch],
                        axis=mybir.AxisListType.X).then_inc(sems["vd"], 1))
                    INC("vd")
                W("v", "vd")
                R.v(lambda h: h.tensor_mul(out=ntmp, in0=qacc, in1=kacc)
                    .then_inc(sems["vd"], 1))
                INC("vd")
                W("a", "vd")
                R.a(lambda h: h.activation(out=ntmp, in_=ntmp, func=AF.Sqrt)
                    .then_inc(sems["va"], 1))
                INC("va")
                W("v", "va")
                R.v(lambda h: h.tensor_single_scalar(
                    out=negM2, in_=ntmp, scalar=-0.125 * 1.02, op=ALU.mult)
                    .then_inc(sems["vd"], 1))
                INC("vd")
                W("s", "vd")
                R.s(lambda h: h.dma_start(out=negMrow[0:1, 0:2], in_=negM2)
                    .then_inc(sems["sd"], 16))
                INC("sd", 16)
                W("t", "sd")
                R.t(lambda h: h.matmul(ps[0][:, 0:2], ones_rowf[:], negMrow[0:1, 0:2],
                                       start=True, stop=True).then_inc(sems["pe"], 1))
                INC("pe")
                W("a", "pe")
                R.a(lambda h: h.activation(out=negMb[:], in_=ps[0][:, 0:2],
                                           func=AF.Identity).then_inc(sems["va"], 1))
                INC("va")

                if DBG and hp == 7:
                    W("g", "va", C["va"])
                    W("g", "vd", C["vd"])
                    def _gd(h):
                        h.dma_start(out=dbg_kT, in_=kTv).then_inc(sems["so"], 16)
                        h.dma_start(out=dbg_qT, in_=qTv).then_inc(sems["so"], 16)
                        h.dma_start(out=dbg_vtm, in_=vtm.rearrange("p a b -> p (a b)")).then_inc(sems["so"], 16)
                        h.dma_start(out=dbg_neg, in_=negMb[:]).then_inc(sems["so"], 16)
                        h.dma_start(out=dbg_sc, in_=carry[0:2, :, :].rearrange("p a b -> p (a b)")).then_inc(sems["so"], 16)
                        h.dma_start(out=dbg_m, in_=maskAB[:]).then_inc(sems["so"], 16)
                        h.dma_start(out=dbg_sq, in_=pT.rearrange("p r t -> p (r t)")).then_inc(sems["so"], 16)
                    R.g(_gd)
                    INC("so", 112)
                    W("t", "so", C["so"])
                W("t", "vd", vtm_ones_vd)
                for qc in range(2):
                    for kc in range(32):
                        W("t", "va")

                        def _t(h, qc=qc, kc=kc):
                            for fh in range(2):
                                for head in (0, 1):
                                    pst = ps[0] if head == 0 else ps[1]
                                    rhs = bass.AP(
                                        qT_h,
                                        64 * head * dP + QOFF0 + 1024 * qc + 512 * fh,
                                        [[dP, 64], [1, 512]])
                                    r = h.matmul(
                                        pst[:, 512 * fh:512 * (fh + 1)],
                                        kTv[64 * head:64 * (head + 1),
                                            128 * kc:128 * (kc + 1)],
                                        rhs, start=True, stop=True,
                                        tile_position=(64 * head, 0))
                            r.then_inc(sems["pe"], 1)
                        R.t(_t)
                        INC("pe")
                        W("a", "pe")

                        def _a(h, kc=kc):
                            h.activation(out=pT[:, 2 * (kc % 2) + 0, :], in_=ps[0][:],
                                         func=AF.Exp, scale=0.125, bias=negMb[:, 0:1])
                            r = h.activation(out=pT[:, 2 * (kc % 2) + 1, :], in_=ps[1][:],
                                             func=AF.Exp, scale=0.125, bias=negMb[:, 1:2])
                            r.then_inc(sems["va"], 1)
                        R.a(_a)
                        INC("va")
                        W("t", "va")

                        def _t2(h, kc=kc):
                            for head in (0, 1):
                                pst = ps[2] if head == 0 else ps[3]
                                for fh in range(2):
                                    r = h.matmul(
                                        pst[0:65, 512 * fh:512 * (fh + 1)],
                                        vtm[:, kc, 65 * head:65 * head + 65],
                                        pT[:, 2 * (kc % 2) + head,
                                           512 * fh:512 * (fh + 1)],
                                        start=(kc == 0), stop=(kc == 31))
                            r.then_inc(sems["pe"], 1)
                        R.t(_t2)
                        INC("pe")

                    W("v", "pe")

                    def _v(h):
                        h.reciprocal(out=lns_at[0:1, 0, :], in_=ps[2][64:65, 0:512])
                        h.reciprocal(out=lns_at[0:1, 1, :], in_=ps[2][64:65, 512:1024])
                        h.reciprocal(out=lns_at2[0:1, 0, :], in_=ps[3][64:65, 0:512])
                        r = h.reciprocal(out=lns_at2[0:1, 1, :], in_=ps[3][64:65, 512:1024])
                        r.then_inc(sems["vd"], 1)
                    R.v(_v)
                    INC("vd")
                    W("t", "vd")

                    def _t3(h):
                        h.matmul(ps[0][0:64, 0:512], ones_rowf[0:1, 0:64],
                                 lns_at[0:1, 0, :], start=True, stop=True)
                        h.matmul(ps[0][0:64, 512:1024], ones_rowf[0:1, 0:64],
                                 lns_at[0:1, 1, :], start=True, stop=True)
                        h.matmul(ps[1][0:64, 0:512], ones_rowf[0:1, 0:64],
                                 lns_at2[0:1, 0, :], start=True, stop=True)
                        r = h.matmul(ps[1][0:64, 512:1024], ones_rowf[0:1, 0:64],
                                     lns_at2[0:1, 1, :], start=True, stop=True)
                        r.then_inc(sems["pe"], 1)
                    R.t(_t3)
                    INC("pe")
                    W("a", "pe")

                    def _a2(h):
                        h.activation(out=rb64[0:64, :], in_=ps[0][0:64, :], func=AF.Identity)
                        r = h.activation(out=rb64[64:128, :], in_=ps[1][0:64, :],
                                         func=AF.Identity)
                        r.then_inc(sems["va"], 1)
                    R.a(_a2)
                    INC("va")
                    W("v", "va")

                    def _v2(h, hp=hp, qc=qc):
                        h.tensor_mul(out=om[0:64, hp, 1024 * qc:1024 * (qc + 1)],
                                     in0=ps[2][0:64, :], in1=rb64[0:64, :])
                        r = h.tensor_mul(out=om[64:128, hp, 1024 * qc:1024 * (qc + 1)],
                                         in0=ps[3][0:64, :], in1=rb64[64:128, :])
                        r.then_inc(sems["vd"], 1)
                    R.v(_v2)
                    INC("vd")
                    W("t", "vd")

            if DBG:
                W("g", "vd", C["vd"])
                def _gd2(h):
                    h.dma_start(out=dbg_om.rearrange("(j p) t -> p j t", p=128),
                                in_=om[:]).then_inc(sems["so"], 16)
                R.g(_gd2)
                INC("so", 16)
            W("t", "vd")
            x_ownR = x_ownT.rearrange("(j p) t -> p j t", p=128)
            x1R = x1_d.rearrange("(j p) t -> p j t", p=128)
            outR = outT.rearrange("(j p) t -> p j t", p=128)
            for mt in range(8):
                for f in range(4):
                    pst = ps[0] if (mt * 4 + f) % 2 == 0 else ps[1]
                    W("s", "vd", C["vd"])
                    R.s(lambda h, mt=mt, f=f: h.dma_start(
                        out=x0c[:], in_=x_ownR[:, mt, 512 * f:512 * (f + 1)])
                        .then_inc(sems["sd"], 16))
                    INC("sd", 16)
                    W("t", "va", C["va"])

                    def _t(h, mt=mt, f=f, pst=pst):
                        for jc in range(8):
                            r = h.matmul(pst[:, 0:512],
                                         CT["o"][:, jc, 128 * mt:128 * (mt + 1)],
                                         om[:, jc, 512 * f:512 * (f + 1)],
                                         start=(jc == 0), stop=(jc == 7))
                        r.then_inc(sems["pe"], 1)
                    R.t(_t)
                    INC("pe")
                    W("a", "pe")
                    W("a", "vd", C["vd"])
                    R.a(lambda h, mt=mt, pst=pst: h.activation(
                        out=tmpo[:], in_=pst[:, 0:512], func=AF.Identity,
                        bias=bias_o[:, mt:mt + 1]).then_inc(sems["va"], 1))
                    INC("va")
                    W("v", "va")
                    W("v", "sd")
                    R.v(lambda h: h.tensor_add(out=x1c[:], in0=x0c[:], in1=tmpo[:])
                        .then_inc(sems["vd"], 1))
                    INC("vd")
                    W("g", "vd")

                    def _g(h, mt=mt, f=f):
                        r = h.dma_start(out=x1R[:, mt, 512 * f:512 * (f + 1)], in_=x1c[:])
                        r.then_inc(sems["sg"], 16)
                        if STAGE == 1:
                            h.dma_start(out=outR[:, mt, 512 * f:512 * (f + 1)],
                                        in_=x1c[:]).then_inc(sems["so"], 16)
                    R.g(_g)
                    INC("sg", 16)
                    if STAGE == 1:
                        INC("so", 16)
                    W("v", "sg", C["sg"])

        s1T = arA_b.rearrange("p (j t) -> p j t", j=8)
        s2T = arB_b.rearrange("p (j t) -> p j t", j=32)
        w1R = w1T_in.rearrange("(j p) t -> p j t", p=128)
        w2R = w2T_in.rearrange("(j p) t -> p j t", p=128)
        arC_f = arC.ap()

        def p4():
            for jc in range(8):
                stg = arC_f[:, 4096 * (jc % 2):4096 * (jc % 2) + 4096]
                W("s", "va", C["va"])
                R.s(lambda h, jc=jc, stg=stg: h.dma_start(
                    out=stg, in_=w1R[:, jc, :]).then_inc(sems["sd"], 16))
                INC("sd", 16)
                W("a", "sd")
                R.a(lambda h, jc=jc, stg=stg: h.activation(
                    out=s1T[:, jc, :], in_=stg, func=AF.Sign).then_inc(sems["va"], 1))
                INC("va")
            for hc in range(32):
                stg = arC_f[:, 4096 * (hc % 2):4096 * (hc % 2) + 1024]
                W("s", "va", C["va"])
                R.s(lambda h, hc=hc, stg=stg: h.dma_start(
                    out=stg, in_=w2R[:, hc, :]).then_inc(sems["sd"], 16))
                INC("sd", 16)
                W("a", "sd")
                R.a(lambda h, hc=hc, stg=stg: h.activation(
                    out=s2T[:, hc, :], in_=stg, func=AF.Sign).then_inc(sems["va"], 1))
                INC("va")
            w1NR = w1N_in.rearrange("(m p) t -> p m t", p=128)
            w2NR = w2N_in.rearrange("(m p) t -> p m t", p=128)
            for (wR, ntile, alout, dd) in ((w1NR, 32, alpha1, D), (w2NR, 8, alpha2, HDIM)):
                for Mt in range(ntile):
                    stg = arC_f[:, dd * (Mt % 2):dd * (Mt % 2) + dd]
                    W("s", "vd", C["vd"])
                    W("s", "va", C["va"])
                    R.s(lambda h, wR=wR, Mt=Mt, stg=stg: h.dma_start(
                        out=stg, in_=wR[:, Mt, :]).then_inc(sems["sd"], 16))
                    INC("sd", 16)
                    W("a", "sd")
                    W("a", "vd", C["vd"])
                    absdst = arD_f[:, 0:dd]
                    R.a(lambda h, stg=stg, absdst=absdst, dd=dd: h.activation(
                        out=absdst, in_=stg, func=AF.Abs, scale=1.0 / dd)
                        .then_inc(sems["va"], 1))
                    INC("va")
                    W("v", "va")

                    def _v(h, absdst=absdst, alout=alout, Mt=Mt):
                        r = h.tensor_reduce(out=alout[:, Mt:Mt + 1], in_=absdst,
                                            axis=mybir.AxisListType.X, op=ALU.add)
                        r.then_inc(sems["vd"], 1)
                    R.v(_v)
                    INC("vd")
            if DBG:
                W("g", "vd", C["vd"])
                R.g(lambda h: h.dma_start(out=dbg_al[:, 0:32], in_=alpha1[:])
                    .then_inc(sems["so"], 16))
                INC("so", 16)
                W("v", "so", C["so"])
            W("g", "va", C["va"])
            s1R = s1T_d.rearrange("(j p) t -> p j t", p=128)
            s2R = s2T_d.rearrange("(j p) t -> p j t", p=128)

            def _g(h):
                h.dma_start(out=s1R[:, :, :], in_=s1T[:]).then_inc(sems["sg"], 16)
                h.dma_start(out=s2R[:, :, :], in_=s2T[:]).then_inc(sems["sg"], 16)
            R.g(_g)
            INC("sg", 32)

        xbuf = arD_f[:, 0:4096].rearrange("p (j t) -> p j t", j=8)
        lnsF = arD_f[:, 4096:5120].rearrange("p (k t) -> p k t", k=2)
        tmpo2 = arD_f[:, 5120:5632]
        ln2c = arD_b[:, 11264:15360].rearrange("p (j t) -> p j t", j=8)
        castb = arD_b[:, 15360:15872]
        sqb = arD_b[:, 15872:16384]

        def ffn_chunks(xsrc_d, xdst_d, also_out):
            xsR = xsrc_d.rearrange("(j p) t -> p j t", p=128)
            xdR = xdst_d.rearrange("(j p) t -> p j t", p=128)
            outR2 = outT.rearrange("(j p) t -> p j t", p=128)
            W("s", "sg", C["sg"])
            for c in range(4):
                W("s", "vd", C["vd"])
                W("s", "sg", C["sg"])
                R.s(lambda h, c=c: h.dma_start(
                    out=xbuf[:], in_=xsR[:, :, 512 * c:512 * (c + 1)])
                    .then_inc(sems["sd"], 16))
                INC("sd", 16)
                W("a", "sd")
                for jc in range(8):
                    if jc > 0:
                        W("a", "pe", C["pe"])

                    def _a(h, jc=jc):
                        h.activation(out=castb, in_=xbuf[:, jc, :], func=AF.Identity)
                        r = h.activation(out=sqb, in_=xbuf[:, jc, :], func=AF.Square)
                        r.then_inc(sems["va"], 1)
                    R.a(_a)
                    INC("va")
                    W("t", "va")

                    def _t(h, jc=jc):
                        h.matmul(ps[3][0:1, 0:512], ones_col[:], castb,
                                 start=(jc == 0), stop=(jc == 7))
                        r = h.matmul(ps[3][0:1, 512:1024], ones_col[:], sqb,
                                     start=(jc == 0), stop=(jc == 7))
                        r.then_inc(sems["pe"], 1)
                    R.t(_t)
                    INC("pe")
                W("v", "pe")

                def _v1(h):
                    h.tensor_single_scalar(out=lnsF[0:1, 0, :], in_=ps[3][0:1, 0:512],
                                           scalar=1.0 / D, op=ALU.mult)
                    h.tensor_single_scalar(out=tmpo2[0:1, :], in_=ps[3][0:1, 512:1024],
                                           scalar=1.0 / D, op=ALU.mult)
                    h.tensor_mul(out=castb[0:1, :], in0=lnsF[0:1, 0, :],
                                 in1=lnsF[0:1, 0, :])
                    h.tensor_sub(out=tmpo2[0:1, :], in0=tmpo2[0:1, :],
                                 in1=castb[0:1, :])
                    r = h.tensor_single_scalar(out=tmpo2[0:1, :],
                                               in_=tmpo2[0:1, :],
                                               scalar=1e-5, op=ALU.add)
                    r.then_inc(sems["vd"], 1)
                R.v(_v1)
                INC("vd")
                W("a", "vd")
                R.a(lambda h: h.activation(out=tmpo2[0:1, :],
                                           in_=tmpo2[0:1, :],
                                           func=AF.Sqrt).then_inc(sems["va"], 1))
                INC("va")
                W("v", "va")
                R.v(lambda h: h.reciprocal(out=lnsF[0:1, 1, :], in_=tmpo2[0:1, :])
                    .then_inc(sems["vd"], 1))
                INC("vd")
                W("t", "vd")

                def _t2(h):
                    h.matmul(ps[2][:, 0:512], ones_rowf[:], lnsF[0:1, 0, :],
                             start=True, stop=True)
                    r = h.matmul(ps[2][:, 512:1024], ones_rowf[:], lnsF[0:1, 1, :],
                                 start=True, stop=True)
                    r.then_inc(sems["pe"], 1)
                R.t(_t2)
                INC("pe")
                W("v", "pe")

                def _v2(h):
                    for jc in range(8):
                        h.tensor_sub(out=castb, in0=xbuf[:, jc, :], in1=ps[2][:, 0:512])
                        r = h.tensor_mul(out=ln2c[:, jc, :], in0=castb,
                                         in1=ps[2][:, 512:1024])
                    r.then_inc(sems["vd"], 1)
                R.v(_v2)
                INC("vd")
                if DBG and also_out and c == 0:
                    W("g", "vd", C["vd"])
                    def _gdf(h):
                        h.dma_start(out=dbg_ln2, in_=ln2c[:]).then_inc(sems["so"], 16)
                        h.dma_start(out=dbg_xb, in_=xbuf[:]).then_inc(sems["so"], 16)
                        al = dbg_al
                        h.dma_start(out=al[:, 32:40], in_=alpha2[:]).then_inc(sems["so"], 16)
                        h.dma_start(out=al[:, 40:44], in_=b2cols[:, 0:4]).then_inc(sems["so"], 16)
                        h.dma_start(out=al[:, 44:48], in_=bias_q[:, 0:4]).then_inc(sems["so"], 16)
                    R.g(_gdf)
                    INC("so", 80)
                    W("v", "so", C["so"])
                h_bf = arC_b.rearrange("p (m t) -> p m t", m=32)
                W("t", "vd")
                for Mt in range(32):
                    pst = ps[0] if Mt % 2 == 0 else ps[1]
                    W("t", "va", C["va"])

                    def _t3(h, Mt=Mt, pst=pst):
                        for jc in range(8):
                            r = h.matmul(pst[:, 0:512], s1T[:, jc, 128 * Mt:128 * (Mt + 1)],
                                         ln2c[:, jc, :], start=(jc == 0), stop=(jc == 7))
                        r.then_inc(sems["pe"], 1)
                    R.t(_t3)
                    INC("pe")
                    W("a", "pe")
                    R.a(lambda h, Mt=Mt, pst=pst: h.activation(
                        out=h_bf[:, Mt, :], in_=pst[:, 0:512], func=AF.Gelu,
                        scale=alpha1[:, Mt:Mt + 1], bias=b1cols[:, Mt:Mt + 1])
                        .then_inc(sems["va"], 1))
                    INC("va")
                if DBG and also_out and c == 0:
                    W("g", "va", C["va"])
                    R.g(lambda h: h.dma_start(out=dbg_h, in_=h_bf[:, 0:4, :])
                        .then_inc(sems["so"], 16))
                    INC("so", 16)
                    W("t", "so", C["so"])
                for Mt in range(8):
                    pst = ps[0] if Mt % 2 == 0 else ps[1]
                    W("t", "va", C["va"])

                    def _t4(h, Mt=Mt, pst=pst):
                        for hc in range(32):
                            r = h.matmul(pst[:, 0:512], s2T[:, hc, 128 * Mt:128 * (Mt + 1)],
                                         h_bf[:, hc, :], start=(hc == 0), stop=(hc == 31))
                        r.then_inc(sems["pe"], 1)
                    R.t(_t4)
                    INC("pe")
                    W("a", "pe")
                    W("a", "vd", C["vd"])
                    R.a(lambda h, Mt=Mt, pst=pst: h.activation(
                        out=tmpo2[:], in_=pst[:, 0:512], func=AF.Identity,
                        scale=alpha2[:, Mt:Mt + 1], bias=b2cols[:, Mt:Mt + 1])
                        .then_inc(sems["va"], 1))
                    INC("va")
                    W("v", "va")
                    R.v(lambda h, Mt=Mt: h.tensor_add(
                        out=xbuf[:, Mt, :], in0=xbuf[:, Mt, :], in1=tmpo2[:])
                        .then_inc(sems["vd"], 1))
                    INC("vd")
                    W("g", "vd")

                    def _g(h, Mt=Mt, c=c):
                        r = h.dma_start(out=xdR[:, Mt, 512 * c:512 * (c + 1)],
                                        in_=xbuf[:, Mt, :])
                        r.then_inc(sems["sg"], 16)
                        if also_out:
                            h.dma_start(out=outR2[:, Mt, 512 * c:512 * (c + 1)],
                                        in_=xbuf[:, Mt, :]).then_inc(sems["so"], 16)
                    R.g(_g)
                    INC("sg", 16)
                    if also_out:
                        INC("so", 16)
                    W("v", "sg", C["sg"])

        def p6():
            xsR = x2_d.rearrange("(j p) t -> p j t", p=128)
            l3R = ln3_loc.rearrange("(j p) t -> p j t", p=128)
            W("s", "sg", C["sg"])
            for c in range(4):
                W("s", "vd", C["vd"])
                W("s", "sg", C["sg"])
                R.s(lambda h, c=c: h.dma_start(
                    out=xbuf[:], in_=xsR[:, :, 512 * c:512 * (c + 1)])
                    .then_inc(sems["sd"], 16))
                INC("sd", 16)
                W("a", "sd")
                for jc in range(8):
                    if jc > 0:
                        W("a", "pe", C["pe"])

                    def _a(h, jc=jc):
                        h.activation(out=castb, in_=xbuf[:, jc, :], func=AF.Identity)
                        r = h.activation(out=sqb, in_=xbuf[:, jc, :], func=AF.Square)
                        r.then_inc(sems["va"], 1)
                    R.a(_a)
                    INC("va")
                    W("t", "va")

                    def _t(h, jc=jc):
                        h.matmul(ps[3][0:1, 0:512], ones_col[:], castb,
                                 start=(jc == 0), stop=(jc == 7))
                        r = h.matmul(ps[3][0:1, 512:1024], ones_col[:], sqb,
                                     start=(jc == 0), stop=(jc == 7))
                        r.then_inc(sems["pe"], 1)
                    R.t(_t)
                    INC("pe")
                W("v", "pe")

                def _v1(h):
                    h.tensor_single_scalar(out=lnsF[0:1, 0, :], in_=ps[3][0:1, 0:512],
                                           scalar=1.0 / D, op=ALU.mult)
                    h.tensor_single_scalar(out=tmpo2[0:1, :], in_=ps[3][0:1, 512:1024],
                                           scalar=1.0 / D, op=ALU.mult)
                    h.tensor_mul(out=castb[0:1, :], in0=lnsF[0:1, 0, :],
                                 in1=lnsF[0:1, 0, :])
                    h.tensor_sub(out=tmpo2[0:1, :], in0=tmpo2[0:1, :], in1=castb[0:1, :])
                    r = h.tensor_single_scalar(out=tmpo2[0:1, :], in_=tmpo2[0:1, :],
                                               scalar=1e-5, op=ALU.add)
                    r.then_inc(sems["vd"], 1)
                R.v(_v1)
                INC("vd")
                W("a", "vd")
                R.a(lambda h: h.activation(out=tmpo2[0:1, :], in_=tmpo2[0:1, :],
                                           func=AF.Sqrt).then_inc(sems["va"], 1))
                INC("va")
                W("v", "va")
                R.v(lambda h: h.reciprocal(out=lnsF[0:1, 1, :], in_=tmpo2[0:1, :])
                    .then_inc(sems["vd"], 1))
                INC("vd")
                W("t", "vd")

                def _t2(h):
                    h.matmul(ps[2][:, 0:512], ones_rowf[:], lnsF[0:1, 0, :],
                             start=True, stop=True)
                    r = h.matmul(ps[2][:, 512:1024], ones_rowf[:], lnsF[0:1, 1, :],
                                 start=True, stop=True)
                    r.then_inc(sems["pe"], 1)
                R.t(_t2)
                INC("pe")
                W("v", "pe")

                def _v2(h):
                    for jc in range(8):
                        h.tensor_sub(out=castb, in0=xbuf[:, jc, :], in1=ps[2][:, 0:512])
                        r = h.tensor_mul(out=ln2c[:, jc, :], in0=castb,
                                         in1=ps[2][:, 512:1024])
                    r.then_inc(sems["vd"], 1)
                R.v(_v2)
                INC("vd")
                W("g", "vd")
                R.g(lambda h, c=c: h.dma_start(
                    out=l3R[:, :, 512 * c:512 * (c + 1)], in_=ln2c[:])
                    .then_inc(sems["sg"], 16))
                INC("sg", 16)
                W("v", "sg", C["sg"])
            W("g", "sg", C["sg"])
            R.g(lambda h: h.collective_compute(
                "AllGather", ALU.bypass, replica_groups=RG,
                ins=[ln3_loc], outs=[ln3_all]).then_inc(sems["vc"], 1))
            INC("vc")

        def p7():
            for (wsrc, off) in ((w_inT_x, 0), (w_inT_z, 256)):
                wr = wsrc.rearrange("(j p) e -> p j e", p=128)
                W("s", "va", C["va"])
                R.s(lambda h, wr=wr: h.dma_start(
                    out=arC_f[:, 0:2048].rearrange("p (j e) -> p j e", j=8),
                    in_=wr).then_inc(sems["sd"], 16))
                INC("sd", 16)
                W("a", "sd")

                def _a(h, off=off):
                    src2 = arC_f[:, 0:2048].rearrange("p (j e) -> p j e", j=8)
                    for jc in range(8):
                        r = h.activation(out=wioT[:, jc, off:off + 256],
                                         in_=src2[:, jc, :], func=AF.Identity)
                    r.then_inc(sems["va"], 1)
                R.a(_a)
                INC("va")
            wxr = wxT_sl.rearrange("(k p) n -> p k n", p=128)
            W("s", "va", C["va"])
            R.s(lambda h: h.dma_start(
                out=arC_f[:, 0:66].rearrange("p (k n) -> p k n", k=2), in_=wxr)
                .then_inc(sems["sd"], 16))
            INC("sd", 16)
            W("a", "sd")

            def _awx(h):
                src2 = arC_f[:, 0:66].rearrange("p (k n) -> p k n", k=2)
                h.activation(out=wxTb[:, 0, :], in_=src2[:, 0, :], func=AF.Identity)
                r = h.activation(out=wxTb[:, 1, :], in_=src2[:, 1, :], func=AF.Identity)
                r.then_inc(sems["va"], 1)
            R.a(_awx)
            INC("va")

            lnall = ln3_all.rearrange("(bk j p) t -> bk p j t", p=128, j=8)
            uR = u_d.rearrange("(k p) t -> p k t", p=128)
            zR = zs_d.rearrange("(k p) t -> p k t", p=128)
            bpR = bcd_part
            rhsT = arA_b[:, 0:16384].rearrange("p (j t) -> p j t", j=8)
            ublk = arC_f[:, 0:4102].rearrange("p (k t) -> p k t", k=2)
            zblk = arB.ap()[:, 0:4096].rearrange("p (k t) -> p k t", k=2)
            uS = arA.ap()[:, 8192:12288].rearrange("p (k t) -> p k t", k=2)
            halo = arA.ap()[:, 12288:12294].rearrange("p (k t) -> p k t", k=2)
            xpev = arA.ap()[:, 12300:14348][0:33, :]
            W("v", "vc", C["vc"])
            for tcb in range(8):
                seq_start = (tcb % 2 == 0)
                W("s", "vc", C["vc"])
                W("s", "va", C["va"])
                W("s", "pe", C["pe"])
                R.s(lambda h, tcb=tcb: h.dma_start(
                    out=rhsT[:], in_=lnall[tcb]).then_inc(sems["sd"], 16))
                INC("sd", 16)
                for th in range(2):
                    W("t", "sd")
                    W("t", "va", C["va"])

                    def _t(h, th=th):
                        for jc in range(8):
                            for Mt in range(4):
                                for tc in range(2):
                                    r = h.matmul(
                                        ps[Mt][:, 512 * tc:512 * (tc + 1)],
                                        wioT[:, jc, 128 * Mt:128 * (Mt + 1)],
                                        rhsT[:, jc, 1024 * th + 512 * tc:
                                             1024 * th + 512 * tc + 512],
                                        start=(jc == 0), stop=(jc == 7))
                        r.then_inc(sems["pe"], 1)
                    R.t(_t)
                    INC("pe")
                    W("a", "pe")

                    def _a(h, th=th):
                        for Mt in range(2):
                            h.activation(out=ublk[:, Mt, 3 + 1024 * th:3 + 1024 * th + 1024],
                                         in_=ps[Mt][:, 0:1024], func=AF.Identity)
                        for Mt in range(2):
                            r = h.activation(out=zblk[:, Mt, 1024 * th:1024 * th + 1024],
                                             in_=ps[Mt + 2][:, 0:1024], func=AF.Silu)
                        r.then_inc(sems["va"], 1)
                    R.a(_a)
                    INC("va")
                W("v", "va")
                if seq_start:
                    R.v(lambda h: h.memset(ublk[:, :, 0:3], 0.0).then_inc(sems["vd"], 1))
                else:
                    R.v(lambda h: h.tensor_copy(out=ublk[:, :, 0:3], in_=halo[:])
                        .then_inc(sems["vd"], 1))
                INC("vd")
                W("v", "vd")

                def _vc(h):
                    for k in range(2):
                        h.tensor_scalar(out=uS[:, k, :], in0=ublk[:, k, 3:2051],
                                        scalar1=conv_sb[:, k, 3:4],
                                        scalar2=convb_sb[:, k:k + 1],
                                        op0=ALU.mult, op1=ALU.add)
                        for sh in (1, 2, 3):
                            r = h.scalar_tensor_tensor(
                                out=uS[:, k, :], in0=ublk[:, k, 3 - sh:2051 - sh],
                                scalar=conv_sb[:, k, 3 - sh:4 - sh],
                                in1=uS[:, k, :], op0=ALU.mult, op1=ALU.add)
                    r = h.tensor_copy(out=halo[:], in_=ublk[:, :, 2048:2051])
                    r.then_inc(sems["vd"], 1)
                R.v(_vc)
                INC("vd")
                W("a", "vd")
                R.a(lambda h: h.activation(out=uS[:, :, :].rearrange("p a b -> p (a b)"),
                                           in_=uS[:, :, :].rearrange("p a b -> p (a b)"),
                                           func=AF.Silu).then_inc(sems["va"], 1))
                INC("va")
                ubf = arA_b[:, 28672:32768].rearrange("p (k t) -> p k t", k=2)
                W("v", "va")
                R.v(lambda h: h.tensor_copy(out=ubf[:, :, :], in_=uS[:, :, :])
                    .then_inc(sems["vd"], 1))
                INC("vd")
                for tc in range(4):
                    W("t", "vd", C["vd"])
                    W("t", "va", C["va"])

                    def _tx(h, tc=tc):
                        for k in range(2):
                            r = h.matmul(ps[0][0:33, 512 * (tc % 2):512 * (tc % 2) + 512],
                                         wxTb[:, k, :],
                                         ubf[:, k, 512 * tc:512 * (tc + 1)],
                                         start=(k == 0), stop=(k == 1))
                        r.then_inc(sems["pe"], 1)
                    R.t(_tx)
                    INC("pe")
                    W("a", "pe")
                    R.a(lambda h, tc=tc: h.activation(
                        out=xpev[:, 512 * tc:512 * (tc + 1)],
                        in_=ps[0][0:33, 512 * (tc % 2):512 * (tc % 2) + 512],
                        func=AF.Identity).then_inc(sems["va"], 1))
                    INC("va")
                W("g", "va", C["va"])
                W("g", "vd", C["vd"])

                def _g(h, tcb=tcb):
                    h.dma_start(out=uR[:, :, 2048 * tcb:2048 * (tcb + 1)],
                                in_=uS[:, :, :]).then_inc(sems["sg"], 16)
                    h.dma_start(out=zR[:, :, 2048 * tcb:2048 * (tcb + 1)],
                                in_=zblk[:, :, :]).then_inc(sems["sg"], 16)
                    h.dma_start(out=bpR[:, 2048 * tcb:2048 * (tcb + 1)],
                                in_=xpev[:]).then_inc(sems["sg"], 16)
                R.g(_g)
                INC("sg", 48)
                W("a", "sg", C["sg"])
                W("v", "sg", C["sg"])
            W("g", "sg", C["sg"])
            R.g(lambda h: h.collective_compute(
                "AllReduce", ALU.add, replica_groups=RG,
                ins=[bcd_part], outs=[bcd_all]).then_inc(sems["vc"], 1))
            INC("vc")

        def p8():
            gR = gated_loc.rearrange("(k p) t -> p k t", p=128)
            uR = u_d.rearrange("(k p) t -> p k t", p=128)
            zR = zs_d.rearrange("(k p) t -> p k t", p=128)
            bcdc = arB.ap()[0:33, 0:2048]
            slotA = arB.ap()[0:1, 14336:16384]
            slotB = arB.ap()[32:33, 14336:16384]
            slotC = arB.ap()[64:65, 14336:16384]
            slotD = arB.ap()[64:65, 0:2048]
            scr1 = arB.ap()[:, 2048:6144].rearrange("p (k t) -> p k t", k=2)
            scr2 = arB.ap()[:, 6144:10240].rearrange("p (k t) -> p k t", k=2)
            tmpy = arB.ap()[:, 10240:14336].rearrange("p (k t) -> p k t", k=2)
            dt_v = arD_f[:, 0:4096].rearrange("p (k t) -> p k t", k=2)
            dtu_v = arD_f[:, 4096:8192].rearrange("p (k t) -> p k t", k=2)
            dA_v = arC_f[:, 0:4096].rearrange("p (k t) -> p k t", k=2)
            hbuf = arC_f[:, 4096:8192].rearrange("p (k t) -> p k t", k=2)
            uc_v = arA.ap()[:, 0:4096].rearrange("p (k t) -> p k t", k=2)
            zc_v = arA.ap()[:, 4096:8192].rearrange("p (k t) -> p k t", k=2)
            yacc = arA.ap()[:, 8192:12288].rearrange("p (k t) -> p k t", k=2)
            gbf = arA_b[:, 24576:28672].rearrange("p (k t) -> p k t", k=2)

            for sc in range(8):
                seq_start = (sc % 2 == 0)
                W("s", "vc", C["vc"])
                W("s", "vd", C["vd"])
                W("s", "va", C["va"])

                def _s(h, sc=sc):
                    h.dma_start(out=bcdc, in_=bcd_all[:, 2048 * sc:2048 * (sc + 1)])                         .then_inc(sems["sd"], 16)
                    h.dma_start(out=uc_v[:, :, :], in_=uR[:, :, 2048 * sc:2048 * (sc + 1)])                         .then_inc(sems["sd"], 16)
                    h.dma_start(out=zc_v[:, :, :], in_=zR[:, :, 2048 * sc:2048 * (sc + 1)])                         .then_inc(sems["sd"], 16)
                R.s(_s)
                INC("sd", 48)
                W("s", "sd", C["sd"])
                R.s(lambda h: h.dma_start(out=slotA, in_=bcdc[32:33, :])
                    .then_inc(sems["sd"], 16))
                INC("sd", 16)
                W("t", "sd")
                W("t", "va", C["va"])

                def _tb(h):
                    for tc in range(4):
                        pst = ps[2] if tc < 2 else ps[3]
                        r = h.matmul(pst[:, 512 * (tc % 2):512 * (tc % 2) + 512],
                                     ones_rowf[:], slotA[0:1, 512 * tc:512 * (tc + 1)],
                                     start=True, stop=True)
                    r.then_inc(sems["pe"], 1)
                R.t(_tb)
                INC("pe")
                W("a", "pe")

                def _asp(h):
                    for k in range(2):
                        for half in range(2):
                            pst = ps[2] if half == 0 else ps[3]
                            h.activation(out=scr1[:, k, 1024 * half:1024 * (half + 1)],
                                         in_=pst[:, 0:1024], func=AF.Identity,
                                         scale=dtw_sb[:, k:k + 1], bias=dtb_sb[:, k:k + 1])
                    h.activation(out=scr2.rearrange("p a b -> p (a b)"),
                                 in_=scr1.rearrange("p a b -> p (a b)"), func=AF.Abs)
                    h.activation(out=scr2.rearrange("p a b -> p (a b)"),
                                 in_=scr2.rearrange("p a b -> p (a b)"), func=AF.Exp,
                                 scale=-1.0)
                    r = h.activation(out=scr2.rearrange("p a b -> p (a b)"),
                                     in_=scr2.rearrange("p a b -> p (a b)"), func=AF.Ln,
                                     bias=1.0)
                    r.then_inc(sems["va"], 1)
                R.a(_asp)
                INC("va")
                W("v", "va")

                def _vsp(h):
                    h.scalar_tensor_tensor(out=dt_v.rearrange("p a b -> p (a b)"),
                                           in0=scr1.rearrange("p a b -> p (a b)"),
                                           scalar=0.0, in1=scr2.rearrange("p a b -> p (a b)"),
                                           op0=ALU.max, op1=ALU.add)
                    r = h.tensor_mul(out=dtu_v.rearrange("p a b -> p (a b)"),
                                     in0=dt_v.rearrange("p a b -> p (a b)"),
                                     in1=uc_v.rearrange("p a b -> p (a b)"))
                    r.then_inc(sems["vd"], 1)
                R.v(_vsp)
                INC("vd")
                for n in range(NST):
                    sB = slotB if n % 2 == 0 else slotA
                    sC = slotC if n % 2 == 0 else slotD
                    obB = onesq[32:33, :] if n % 2 == 0 else ones_rowf[:]
                    obC = onesq[64:65, :]
                    W("s", "pe", C["pe"])

                    def _sn(h, n=n, sB=sB, sC=sC):
                        h.dma_start(out=sB, in_=bcdc[n:n + 1, :]).then_inc(sems["sd"], 16)
                        h.dma_start(out=sC, in_=bcdc[NST + n:NST + n + 1, :]).then_inc(sems["sd"], 16)
                    R.s(_sn)
                    INC("sd", 32)
                    W("t", "vd", C["vd"])
                    W("t", "va", C["va"])
                    W("t", "sd", C["sd"])

                    def _tn(h, n=n, sB=sB, sC=sC, obB=obB, obC=obC):
                        for tc in range(4):
                            pst = ps[0] if tc < 2 else ps[1]
                            h.matmul(pst[:, 512 * (tc % 2):512 * (tc % 2) + 512],
                                     obB, sB[0:1, 512 * tc:512 * (tc + 1)],
                                     start=True, stop=True)
                        for tc in range(4):
                            pst = ps[2] if tc < 2 else ps[3]
                            r = h.matmul(pst[:, 512 * (tc % 2):512 * (tc % 2) + 512],
                                         obC, sC[0:1, 512 * tc:512 * (tc + 1)],
                                         start=True, stop=True)
                        r.then_inc(sems["pe"], 1)
                    R.t(_tn)
                    INC("pe")
                    W("a", "pe", C["pe"])
                    W("a", "vd", C["vd"])

                    def _an(h, n=n):
                        for k in range(2):
                            r = h.activation(out=dA_v[:, k, :], in_=dt_v[:, k, :],
                                             func=AF.Exp, scale=A_sb[:, k, n:n + 1])
                        r.then_inc(sems["va"], 1)
                    R.a(_an)
                    INC("va")
                    W("v", "va")
                    W("v", "pe", C["pe"])

                    def _vn(h, n=n, sc=sc, seq_start=seq_start):
                        for k in range(2):
                            for half in range(2):
                                pst = ps[0] if half == 0 else ps[1]
                                h.tensor_mul(out=hbuf[:, k, 1024 * half:1024 * (half + 1)],
                                             in0=dtu_v[:, k, 1024 * half:1024 * (half + 1)],
                                             in1=pst[:, 0:1024])
                        for k in range(2):
                            ini = 0.0 if seq_start else carry[:, k, n:n + 1]
                            h.tensor_tensor_scan(out=hbuf[:, k, :], data0=dA_v[:, k, :],
                                                 data1=hbuf[:, k, :], initial=ini,
                                                 op0=ALU.mult, op1=ALU.add)
                        h.tensor_copy(out=carry[:, 0, n:n + 1], in_=hbuf[:, 0, 2047:2048])
                        h.tensor_copy(out=carry[:, 1, n:n + 1], in_=hbuf[:, 1, 2047:2048])
                        for k in range(2):
                            for half in range(2):
                                pst = ps[2] if half == 0 else ps[3]
                                h.tensor_mul(out=tmpy[:, k, 1024 * half:1024 * (half + 1)],
                                             in0=hbuf[:, k, 1024 * half:1024 * (half + 1)],
                                             in1=pst[:, 0:1024])
                        if n == 0:
                            r = h.tensor_copy(out=yacc.rearrange("p a b -> p (a b)"),
                                              in_=tmpy.rearrange("p a b -> p (a b)"))
                        else:
                            r = h.tensor_add(out=yacc.rearrange("p a b -> p (a b)"),
                                             in0=yacc.rearrange("p a b -> p (a b)"),
                                             in1=tmpy.rearrange("p a b -> p (a b)"))
                        r.then_inc(sems["vd"], 1)
                    R.v(_vn)
                    INC("vd")
                W("v", "vd")

                def _vg(h):
                    for k in range(2):
                        h.scalar_tensor_tensor(out=yacc[:, k, :], in0=uc_v[:, k, :],
                                               scalar=dpar_sb[:, k:k + 1],
                                               in1=yacc[:, k, :],
                                               op0=ALU.mult, op1=ALU.add)
                    r = h.tensor_mul(out=gbf.rearrange("p a b -> p (a b)"),
                                     in0=yacc.rearrange("p a b -> p (a b)"),
                                     in1=zc_v.rearrange("p a b -> p (a b)"))
                    r.then_inc(sems["vd"], 1)
                R.v(_vg)
                INC("vd")
                W("g", "vd")
                R.g(lambda h, sc=sc: h.dma_start(
                    out=gR[:, :, 2048 * sc:2048 * (sc + 1)], in_=gbf[:, :, :])
                    .then_inc(sems["sg"], 16))
                INC("sg", 16)
                W("v", "sg", C["sg"])
            W("g", "sg", C["sg"])
            R.g(lambda h: h.collective_compute(
                "AllGather", ALU.bypass, replica_groups=RG,
                ins=[gated_loc], outs=[gated_all]).then_inc(sems["vc"], 1))
            INC("vc")

        def p9():
            woR = w_outT_in.rearrange("(k p) dd -> p k dd", p=128)
            woT = arC_b[:, 0:16384].rearrange("p (k dd) -> p k dd", k=16)
            for half in range(2):
                stg9 = arD_f[:, 0:8192] if half == 0 else arB.ap()[:, 0:8192]
                W("s", "va", C["va"])
                W("s", "vd", C["vd"])
                R.s(lambda h, half=half, stg9=stg9: h.dma_start(
                    out=stg9.rearrange("p (k dd) -> p k dd", k=8),
                    in_=woR[:, 8 * half:8 * (half + 1), :]).then_inc(sems["sd"], 16))
                INC("sd", 16)
                W("a", "sd")
                W("a", "vd", C["vd"])
                W("a", "sg", C["sg"])

                def _a(h, half=half, stg9=stg9):
                    src2 = stg9.rearrange("p (k dd) -> p k dd", k=8)
                    for k in range(8):
                        r = h.activation(out=woT[:, 8 * half + k, :], in_=src2[:, k, :],
                                         func=AF.Identity)
                    r.then_inc(sems["va"], 1)
                R.a(_a)
                INC("va")
            W("g", "va", C["va"])
            roff = nc.gpsimd.alloc_register("roff")
            ga_h = gated_all.tensor
            x2R = x2_d.rearrange("(j p) t -> p j t", p=128)
            x3R = x3_d.rearrange("(j p) t -> p j t", p=128)
            outR3 = outT.rearrange("(j p) t -> p j t", p=128)
            grhs = arD_b[:, 0:8192].rearrange("p (k t) -> p k t", k=16)
            x2c = arD_f[:, 4096:4608]
            tmpr = arD_f[:, 4608:5120]
            W("g", "vc", C["vc"])
            for f in range(4):
                W("g", "pe", C["pe"])
                W("g", "vd", C["vd"])
                W("g", "va", C["va"])

                def _s(h, f=f):
                    pid = h.partition_id()
                    h.reg_alu(roff, pid, TOK, ALU.mult)
                    h.reg_alu(roff, roff, 512 * f, ALU.add)
                    h.dma_start(out=grhs[:, :, :],
                                in_=bass.AP(ga_h, roff,
                                            [[T16, 128], [128 * T16, 16], [1, 512]]))                         .then_inc(sems["sg"], 16)
                R.g(_s)
                INC("sg", 16)
                if DBG and f == 0:
                    W("g", "sg", C["sg"])
                    W("g", "va", C["va"])
                    def _gdw(h):
                        h.dma_start(out=dbg_gr, in_=grhs[:, :, :]).then_inc(sems["so"], 16)
                        h.dma_start(out=dbg_wo, in_=woT[:, :, :]).then_inc(sems["so"], 16)
                    R.g(_gdw)
                    INC("so", 32)
                    W("t", "so", C["so"])
                for Mt in range(8):
                    pst = ps[0] if Mt % 2 == 0 else ps[1]
                    W("t", "sg", C["sg"])
                    W("t", "va", C["va"])

                    def _t(h, Mt=Mt, pst=pst):
                        for ec in range(16):
                            r = h.matmul(pst[:, 0:512], woT[:, ec, 128 * Mt:128 * (Mt + 1)],
                                         grhs[:, ec, :], start=(ec == 0), stop=(ec == 15))
                        r.then_inc(sems["pe"], 1)
                    R.t(_t)
                    INC("pe")
                    W("s", "vd", C["vd"])
                    W("s", "sg", C["sg"])
                    R.s(lambda h, Mt=Mt, f=f: h.dma_start(
                        out=x2c, in_=x2R[:, Mt, 512 * f:512 * (f + 1)])
                        .then_inc(sems["sd"], 16))
                    INC("sd", 16)
                    W("a", "pe")
                    W("a", "vd", C["vd"])
                    R.a(lambda h, pst=pst: h.activation(
                        out=tmpr, in_=pst[:, 0:512], func=AF.Identity)
                        .then_inc(sems["va"], 1))
                    INC("va")
                    W("v", "va")
                    W("v", "sd")
                    R.v(lambda h: h.tensor_add(out=x2c, in0=x2c, in1=tmpr)
                        .then_inc(sems["vd"], 1))
                    INC("vd")
                    W("g", "vd")

                    def _g(h, Mt=Mt, f=f):
                        r = h.dma_start(out=x3R[:, Mt, 512 * f:512 * (f + 1)], in_=x2c)
                        r.then_inc(sems["sg"], 16)
                        if STAGE == 3:
                            h.dma_start(out=outR3[:, Mt, 512 * f:512 * (f + 1)],
                                        in_=x2c).then_inc(sems["so"], 16)
                    R.g(_g)
                    INC("sg", 16)
                    if STAGE == 3:
                        INC("so", 16)
                    W("v", "sg", C["sg"])

        def p10():
            s1R = s1T_d.rearrange("(j p) t -> p j t", p=128)
            s2R = s2T_d.rearrange("(j p) t -> p j t", p=128)
            W("s", "va", C["va"])
            W("s", "vd", C["vd"])
            W("s", "pe", C["pe"])
            W("s", "sg", C["sg"])

            def _s(h):
                h.dma_start(out=s1T[:], in_=s1R).then_inc(sems["sd"], 16)
                h.dma_start(out=s2T[:], in_=s2R).then_inc(sems["sd"], 16)
            R.s(_s)
            INC("sd", 32)
            W("t", "sd")
            ffn_chunks(x3_d, x1_d, STAGE == 4)

        p0()
        p1()
        if DBG:
            W("g", "vd", C["vd"])
            def _gdbg(h):
                h.dma_start(out=dbg_ln1.rearrange("(j p) t -> p j t", p=128),
                            in_=ln1[:]).then_inc(sems["so"], 16)
            R.g(_gdbg)
            INC("so", 16)
        p2()
        p3()
        if STAGE >= 2:
            p4()
            ffn_chunks(x1_d, x2_d, STAGE == 2)
        if STAGE >= 3:
            p6()
            p7()
            p8()
            p9()
        if STAGE >= 4:
            p10()

        if DBG and STAGE >= 3:
            W("g", "sg", C["sg"])
            W("g", "vc", C["vc"])

            def _gm(h):
                h.dma_start(out=dbg_u, in_=u_d[:, 0:2048]).then_inc(sems["so"], 16)
                h.dma_start(out=dbg_z, in_=zs_d[:, 0:2048]).then_inc(sems["so"], 16)
                h.dma_start(out=dbg_bcd, in_=bcd_all[:, 0:2048]).then_inc(sems["so"], 16)
                h.dma_start(out=dbg_g, in_=gated_all[:, 0:2048]).then_inc(sems["so"], 16)
                h.dma_start(out=dbg_l3, in_=ln3_all[0:D, 0:2048]).then_inc(sems["so"], 16)
            R.g(_gm)
            INC("so", 80)
        W("g", "sg", C["sg"])
        if C["so"]:
            W("g", "so", C["so"])
        R.g(lambda h: h.nop())

        with nc.Block() as blk:
            @blk.sync
            def _(h):
                for fn in R.ops["s"]:
                    fn(h)

            @blk.scalar
            def _(h):
                for fn in R.ops["a"]:
                    fn(h)

            @blk.vector
            def _(h):
                for fn in R.ops["v"]:
                    fn(h)

            @blk.tensor
            def _(h):
                for fn in R.ops["t"]:
                    fn(h)

            @blk.gpsimd
            def _(h):
                for fn in R.ops["g"]:
                    fn(h)

    return nc


_NC_CACHE = None


def _get_nc():
    global _NC_CACHE
    if _NC_CACHE is None:
        _NC_CACHE = build_nc()
    return _NC_CACHE


def _prep_inmaps(inputs):
    f32 = lambda a: np.ascontiguousarray(np.asarray(a, dtype=np.float32))
    x = f32(inputs["L0_attn_a_x"])
    circ = {k: np.tile(f32(inputs[f"L0_attn_a_w{k}_c"]), CTILE // D) for k in "qkvo"}
    bias_cols = {k: np.ascontiguousarray(
        f32(inputs[f"L0_attn_a_w{k}_b"]).reshape(8, 128).T) for k in "qkvo"}
    wvb_row = f32(inputs["L0_attn_a_wv_b"]).reshape(1, D)
    w1T = np.ascontiguousarray(f32(inputs["L0_ffn_w1"]).T)
    w2T = np.ascontiguousarray(f32(inputs["L0_ffn_w2"]).T)
    b1c = np.ascontiguousarray(f32(inputs["L0_ffn_b1"]).reshape(HDIM // 128, 128).T)
    b2c = np.ascontiguousarray(f32(inputs["L0_ffn_b2"]).reshape(D // 128, 128).T)
    inw = f32(inputs["L0_attn_b_in_proj_w"])
    convw = f32(inputs["L0_attn_b_conv_w"])
    convb = f32(inputs["L0_attn_b_conv_b"])
    xpw = f32(inputs["L0_attn_b_x_proj_w"])
    dtw = f32(inputs["L0_attn_b_dt_proj_w"])[:, 0]
    dtb = f32(inputs["L0_attn_b_dt_proj_b"])
    alog = f32(inputs["L0_attn_b_A_log"])
    dpar = f32(inputs["L0_attn_b_D_param"])
    w_outT = np.ascontiguousarray(f32(inputs["L0_attn_b_out_proj_w"]).T)

    def cols2(v256):
        return np.ascontiguousarray(v256.reshape(2, 128).T)

    in_maps = []
    for c in range(NCORE):
        b, hh = c // 2, c % 2
        sl = slice(DSL * c, DSL * (c + 1))
        m = dict(
            xT_full=np.ascontiguousarray(np.concatenate(
                [x[b, TOK * hh:TOK * (hh + 1), :],
                 x[b, TOK * (1 - hh):TOK * (2 - hh), :]], axis=0).T),
            x_ownT=np.ascontiguousarray(x[b, TOK * hh:TOK * (hh + 1), :].T),
            w1T=w1T, w2T=w2T, b1_c=b1c, b2_c=b2c,
            w1N=f32(inputs["L0_ffn_w1"]), w2N=f32(inputs["L0_ffn_w2"]),
            wvb_row=wvb_row,
            w_inT_x=np.ascontiguousarray(inw[sl, :].T),
            w_inT_z=np.ascontiguousarray(inw[DI + DSL * c:DI + DSL * (c + 1), :].T),
            conv_w_c=np.ascontiguousarray(
                convw[sl].reshape(2, 128, 4).transpose(1, 0, 2)),
            conv_b_c=cols2(convb[sl]),
            wxT_sl=np.ascontiguousarray(xpw[:, sl].T),
            dtw_c=cols2(dtw[sl]),
            dtb_c=cols2(dtb[sl]),
            alog_c=np.ascontiguousarray(
                alog[sl].reshape(2, 128, NST).transpose(1, 0, 2)),
            dpar_c=cols2(dpar[sl]),
            w_outT=w_outT,
        )
        for k in "qkvo":
            m[f"c{k}2"] = circ[k]
            m[f"w{k}b_c"] = bias_cols[k]
        in_maps.append(m)
    return in_maps


def kernel(**inputs):
    nc = _get_nc()
    in_maps = _prep_inmaps(inputs)
    res = run_bass_kernel_spmd(nc, in_maps, core_ids=list(range(NCORE)))
    out = np.empty((4, S, D), dtype=np.float32)
    for c in range(NCORE):
        b, hh = c // 2, c % 2
        out[b, TOK * hh:TOK * (hh + 1), :] = res.results[c]["outT"].T
    return out


# version 06
# speedup vs baseline: 1.1675x; 1.1634x over previous
import numpy as np
import ml_dtypes
from contextlib import ExitStack

import concourse.bass as bass
import concourse.mybir as mybir
from concourse.bass_utils import run_bass_kernel_spmd

F32 = mybir.dt.float32
BF16 = mybir.dt.bfloat16
AF = mybir.ActivationFunctionType
ALU = mybir.AluOpType

D = 1024
S = 4096
TOK = 2048
H, HD = 16, 64
HDIM = 4096
DI = 2048
DSL = 256
NST = 16
T16 = 16384
NCORE = 8
CTILE = 132 * D

STAGE = 4

RG = [[0, 1, 2, 3, 4, 5, 6, 7]]


class Rec:
    def __init__(self):
        self.ops = {e: [] for e in "savtg"}

    def s(self, fn): self.ops["s"].append(fn)
    def a(self, fn): self.ops["a"].append(fn)
    def v(self, fn): self.ops["v"].append(fn)
    def t(self, fn): self.ops["t"].append(fn)
    def g(self, fn): self.ops["g"].append(fn)


def build_nc():
    nc = bass.Bass("TRN2", target_bir_lowering=False)

    def inp(name, shape, dt=F32):
        return nc.dram_tensor(name, shape, dt, kind="ExternalInput").ap()

    xT_full = inp("xT_full", [D, S])
    x_ownT = inp("x_ownT", [D, TOK])
    circ2 = {k: inp(f"c{k}2", [CTILE]) for k in "qkvo"}
    bias_in = {k: inp(f"w{k}b_c", [128, 8]) for k in "qkvo"}
    wvb_row_in = inp("wvb_row", [1, D])
    w1T_in = inp("w1T", [D, HDIM])
    w2T_in = inp("w2T", [HDIM, D])
    w1N_in = inp("w1N", [HDIM, D])
    w2N_in = inp("w2N", [D, HDIM])
    b1_c = inp("b1_c", [128, HDIM // 128])
    b2_c = inp("b2_c", [128, D // 128])
    w_inT_x = inp("w_inT_x", [D, DSL])
    w_inT_z = inp("w_inT_z", [D, DSL])
    conv_w_c = inp("conv_w_c", [128, 2, 4])
    conv_b_c = inp("conv_b_c", [128, 2])
    wxT_sl = inp("wxT_sl", [DSL, 33])
    dtw_c = inp("dtw_c", [128, 2])
    dtb_c = inp("dtb_c", [128, 2])
    alog_c = inp("alog_c", [128, 2, NST])
    dpar_c = inp("dpar_c", [128, 2])
    w_outT_in = inp("w_outT", [DI, D])

    outT = nc.dram_tensor("outT", [D, TOK], F32, kind="ExternalOutput").ap()
    DBG = False
    if DBG:
        dbg_ln1 = nc.dram_tensor("dbg_ln1", [D, S], BF16, kind="ExternalOutput").ap()
        dbg_om = nc.dram_tensor("dbg_om", [D, TOK], BF16, kind="ExternalOutput").ap()
        dbg_kT = nc.dram_tensor("dbg_kT", [128, S], BF16, kind="ExternalOutput").ap()
        dbg_qT = nc.dram_tensor("dbg_qT", [128, TOK], BF16, kind="ExternalOutput").ap()
        dbg_vtm = nc.dram_tensor("dbg_vtm", [128, 32 * 130], BF16, kind="ExternalOutput").ap()
        dbg_neg = nc.dram_tensor("dbg_neg", [128, 2], F32, kind="ExternalOutput").ap()
        dbg_sc = nc.dram_tensor("dbg_sc", [2, 2 * NST], F32, kind="ExternalOutput").ap()
        dbg_m = nc.dram_tensor("dbg_m", [128, 2], BF16, kind="ExternalOutput").ap()
        dbg_al = nc.dram_tensor("dbg_al", [128, 48], F32, kind="ExternalOutput").ap()
        dbg_ln2 = nc.dram_tensor("dbg_ln2", [128, 8, 512], BF16, kind="ExternalOutput").ap()
        dbg_h = nc.dram_tensor("dbg_h", [128, 4, 512], BF16, kind="ExternalOutput").ap()
        dbg_xb = nc.dram_tensor("dbg_xb", [128, 8, 512], F32, kind="ExternalOutput").ap()
        dbg_u = nc.dram_tensor("dbg_u", [DSL, 2048], F32, kind="ExternalOutput").ap()
        dbg_z = nc.dram_tensor("dbg_z", [DSL, 2048], F32, kind="ExternalOutput").ap()
        dbg_bcd = nc.dram_tensor("dbg_bcd", [33, 2048], F32, kind="ExternalOutput").ap()
        dbg_g = nc.dram_tensor("dbg_g", [DI, 2048], BF16, kind="ExternalOutput").ap()
        dbg_l3 = nc.dram_tensor("dbg_l3", [D, 2048], BF16, kind="ExternalOutput").ap()
        dbg_gr = nc.dram_tensor("dbg_gr", [128, 16, 512], BF16, kind="ExternalOutput").ap()
        dbg_wo = nc.dram_tensor("dbg_wo", [128, 16, 1024], BF16, kind="ExternalOutput").ap()
        dbg_wo2 = nc.dram_tensor("dbg_wo2", [128, 16, 1024], BF16, kind="ExternalOutput").ap()
        dbg_sq = nc.dram_tensor("dbg_sq", [128, 4096], BF16, kind="ExternalOutput").ap()

    def scratch(name, shape, dt=F32):
        return nc.dram_tensor(name, shape, dt, kind="Internal").ap()

    x1_d = scratch("x1_d", [D, TOK])
    x2_d = scratch("x2_d", [D, TOK])
    x3_d = scratch("x3_d", [D, TOK])
    s1T_d = scratch("s1T_d", [D, HDIM], BF16)
    s2T_d = scratch("s2T_d", [HDIM, D], BF16)
    ln3_loc = scratch("ln3_loc", [D, TOK], BF16)
    ln3_all = scratch("ln3_all", [NCORE * D, TOK], BF16)
    bcd_part = scratch("bcd_part", [33, T16])
    bcd_all = scratch("bcd_all", [33, T16])
    u_d = scratch("u_d", [DSL, T16])
    zs_d = scratch("zs_d", [DSL, T16])
    gated_loc = scratch("gated_loc", [DSL, T16], BF16)
    gated_all = scratch("gated_all", [DI, T16], BF16)

    es = ExitStack()
    with es:
        sbt = lambda name, shape, dt: es.enter_context(nc.sbuf_tensor(name, shape, dt))

        bias_q = sbt("bias_q", [128, 8], F32)
        bias_k = sbt("bias_k", [128, 8], F32)
        bias_v = sbt("bias_v", [128, 8], F32)
        bias_o = sbt("bias_o", [128, 8], F32)
        vbrow = sbt("vbrow", [1, D], BF16)
        ones_col = sbt("ones_col", [128, 1], BF16)
        ones_row = sbt("ones_row", [1, 128], BF16)
        ones_rowf = sbt("ones_rowf", [1, 128], F32)
        onesq = sbt("onesq", [128, 128], F32)
        onesq_b = sbt("onesq_b", [128, 128], BF16)
        maskAB = sbt("maskAB", [128, 2], BF16)
        iota_col = sbt("iota_col", [128, 1], mybir.dt.int32)
        negMrow = sbt("negMrow", [1, 2], F32)
        negMb = sbt("negMb", [128, 2], F32)
        nsc = sbt("nsc", [1, 8], F32)
        carry = sbt("carry", [128, 2, NST], F32)
        A_sb = sbt("A_sb", [128, 2, NST], F32)
        alpha1 = sbt("alpha1", [128, HDIM // 128], F32)
        alpha2 = sbt("alpha2", [128, D // 128], F32)
        b1cols = sbt("b1cols", [128, HDIM // 128], F32)
        b2cols = sbt("b2cols", [128, D // 128], F32)
        conv_sb = sbt("conv_sb", [128, 2, 4], F32)
        convb_sb = sbt("convb_sb", [128, 2], F32)
        dtw_sb = sbt("dtw_sb", [128, 2], F32)
        dtb_sb = sbt("dtb_sb", [128, 2], F32)
        dpar_sb = sbt("dpar_sb", [128, 2], F32)
        wioT = sbt("wioT", [128, 8, 512], BF16)
        wxTb = sbt("wxTb", [128, 2, 33], BF16)

        arA = sbt("arA", [128, 16384], F32)
        arB = sbt("arB", [128, 16384], F32)
        arC = sbt("arC", [128, 8192], F32)
        arD = sbt("arD", [128, 8192], F32)

        ps = [es.enter_context(nc.psum_tensor(f"ps{i}", [128, 1024], F32))
              for i in range(4)]

        sems = {}
        for name in ("sd", "sg", "ssc", "pe", "va", "vd", "vg", "vc", "so"):
            sems[name] = es.enter_context(nc.semaphore(name))
        C = {k: 0 for k in sems}

        R = Rec()

        def W(eng, name, val=None):
            v = C[name] if val is None else val
            sem = sems[name]
            getattr(R, eng)(lambda h: h.wait_ge(sem, v))

        def INC(name, n=1):
            C[name] += n
            return C[name]

        arA_b = arA.ap().bitcast(BF16)
        arB_b = arB.ap().bitcast(BF16)
        arC_b = arC.ap().bitcast(BF16)
        arD_b = arD.ap().bitcast(BF16)
        arD_f = arD.ap()

        ln1 = arA_b.rearrange("p (j t) -> p j t", j=8)
        CT = {k: arB_b[:, 8192 * i:8192 * (i + 1)].rearrange("p (j i) -> p j i", j=8)
              for i, k in enumerate("qkvo")}
        om = arC_b.rearrange("p (j t) -> p j t", j=8)
        stage = arC.ap().rearrange("p (r j t) -> p r j t", r=2, j=8)

        ln_xb = arD_b[:, 0:4096].rearrange("p (j t) -> p j t", j=8)
        ln_x2 = arD_b[:, 4096:8192].rearrange("p (j t) -> p j t", j=8)
        lns_ln = arD_f[:, 4096:6144].rearrange("p (k t) -> p k t", k=4)
        tmp_ln = arD_f[:, 6144:6656]

        kTv = arD_b[:, 0:4096]
        vtm = arD_b[:, 4096:8256].rearrange("p (tt c) -> p tt c", c=130)
        qTv = arD_b[:, 8256:10304]
        pT = arD_b[:, 10304:14400].rearrange("p (r t) -> p r t", r=4)
        rb64 = arD_f[:, 5152:6176]
        lns_at = arD_f[:, 6176:7200].rearrange("p (k t) -> p k t", k=2)
        lns_at2 = arD_f[:, 5152:6176].rearrange("p (k t) -> p k t", k=2)
        x0c = arD_f[:, 0:512]
        x1c = arD_f[:, 512:1024]
        tmpo = arD_f[:, 1024:1536]

        def p0():
            def _s(h):
                for t, src in ((bias_q, bias_in["q"]), (bias_k, bias_in["k"]),
                               (bias_v, bias_in["v"]), (bias_o, bias_in["o"]),
                               (arD_f[0:1, 0:1024], wvb_row_in), (b1cols, b1_c), (b2cols, b2_c),
                               (conv_sb, conv_w_c), (convb_sb, conv_b_c),
                               (dtw_sb, dtw_c), (dtb_sb, dtb_c), (dpar_sb, dpar_c),
                               (A_sb, alog_c)):
                    h.dma_start(out=t[:], in_=src).then_inc(sems["sd"], 16)
            R.s(_s)
            INC("sd", 16 * 13)

            R.g(lambda h: h.iota(iota_col[:], pattern=[[0, 1]], base=0,
                                 channel_multiplier=1).then_inc(sems["vg"], 1))
            INC("vg")
            W("v", "sd")
            W("v", "vg")

            def _v(h):
                h.memset(ones_col[:], 1.0)
                h.memset(ones_row[:], 1.0)
                h.memset(ones_rowf[:], 1.0)
                h.memset(onesq[:], 1.0)
                h.memset(onesq_b[:], 1.0)
                h.memset(nsc[:], 0.0)
                h.tensor_single_scalar(out=maskAB[:, 0:1], in_=iota_col[:], scalar=64, op=ALU.is_lt)
                h.tensor_single_scalar(out=maskAB[:, 1:2], in_=iota_col[:], scalar=64, op=ALU.is_ge)
                h.tensor_copy(out=vbrow[:], in_=arD_f[0:1, 0:1024]).then_inc(sems["vd"], 1)
            R.v(_v)
            INC("vd")
            W("a", "sd")
            R.a(lambda h: h.activation(out=A_sb[:], in_=A_sb.ap() if False else A_sb[:],
                                       func=AF.Exp).then_inc(sems["va"], 1))
            INC("va")
            W("v", "va")
            R.v(lambda h: h.tensor_single_scalar(out=A_sb[:], in_=A_sb[:],
                                                 scalar=-1.0, op=ALU.mult)
                .then_inc(sems["vd"], 1))
            INC("vd")

        def ln_rec(src_dram, dst_store, ntok, keep_store=None):
            nchunks = ntok // 512
            srcR = src_dram.rearrange("(j p) t -> p j t", p=128)
            for c in range(nchunks):
                st = stage[:, c % 2]
                W("s", "vd")
                R.s(lambda h, st=st, c=c: h.dma_start(
                    out=st[:], in_=srcR[:, :, 512 * c:512 * (c + 1)])
                    .then_inc(sems["sd"], 16))
                INC("sd", 16)
                W("a", "sd")
                W("a", "pe", C["pe"])

                def _a(h, st=st):
                    for jc in range(8):
                        h.activation(out=ln_xb[:, jc, :], in_=st[:, jc, :], func=AF.Identity)
                    for jc in range(8):
                        r = h.activation(out=ln_x2[:, jc, :], in_=st[:, jc, :], func=AF.Square)
                    r.then_inc(sems["va"], 1)
                R.a(_a)
                INC("va")
                W("t", "va")
                W("t", "vd", C["vd"])

                def _t(h):
                    for jc in range(8):
                        h.matmul(ps[3][0:1, 0:512], ones_col[:], ln_xb[:, jc, :],
                                 start=(jc == 0), stop=(jc == 7))
                    for jc in range(8):
                        r = h.matmul(ps[3][32:33, 512:1024], ones_col[:], ln_x2[:, jc, :],
                                     start=(jc == 0), stop=(jc == 7))
                    r.then_inc(sems["pe"], 1)
                R.t(_t)
                INC("pe")
                W("v", "pe")

                def _v1(h):
                    h.tensor_single_scalar(out=lns_ln[0:1, 0, :], in_=ps[3][0:1, 0:512],
                                           scalar=1.0 / D, op=ALU.mult)
                    h.tensor_single_scalar(out=lns_ln[0:1, 1, :], in_=ps[3][32:33, 512:1024],
                                           scalar=1.0 / D, op=ALU.mult)
                    h.tensor_mul(out=lns_ln[0:1, 2, :], in0=lns_ln[0:1, 0, :],
                                 in1=lns_ln[0:1, 0, :])
                    h.tensor_sub(out=lns_ln[0:1, 3, :], in0=lns_ln[0:1, 1, :],
                                 in1=lns_ln[0:1, 2, :])
                    r = h.tensor_single_scalar(out=lns_ln[0:1, 3, :], in_=lns_ln[0:1, 3, :],
                                               scalar=1e-5, op=ALU.add)
                    r.then_inc(sems["vd"], 1)
                R.v(_v1)
                INC("vd")
                W("a", "vd")
                R.a(lambda h: h.activation(out=lns_ln[0:1, 3, :], in_=lns_ln[0:1, 3, :],
                                           func=AF.Sqrt).then_inc(sems["va"], 1))
                INC("va")
                W("v", "va")
                R.v(lambda h: h.reciprocal(out=lns_ln[0:1, 1, :], in_=lns_ln[0:1, 3, :])
                    .then_inc(sems["vd"], 1))
                INC("vd")
                W("t", "vd")

                def _t2(h):
                    h.matmul(ps[2][:, 0:512], ones_rowf[:], lns_ln[0:1, 0, :],
                             start=True, stop=True)
                    h.matmul(ps[2][:, 512:1024], ones_rowf[:], lns_ln[0:1, 1, :],
                             start=True, stop=True).then_inc(sems["pe"], 1)
                R.t(_t2)
                INC("pe")
                W("v", "pe")

                def _v2(h, st=st, c=c):
                    for jc in range(8):
                        h.tensor_sub(out=tmp_ln[:], in0=st[:, jc, :], in1=ps[2][:, 0:512])
                        r = h.tensor_mul(out=dst_store(jc, c), in0=tmp_ln[:],
                                         in1=ps[2][:, 512:1024])
                        if keep_store is not None:
                            r = h.tensor_copy(out=keep_store(jc, c), in_=st[:, jc, :])
                    r.then_inc(sems["vd"], 1)
                R.v(_v2)
                INC("vd")

        def p1():
            ln_rec(xT_full, lambda jc, c: ln1[:, jc, 512 * c:512 * (c + 1)], S)

        def p2():
            for k in "qkvo":
                for jc in range(8):
                    st2 = stage[:, jc % 2, 0:2]
                    W("s", "va")
                    W("s", "vd")
                    off = 2 * D - 128 * jc
                    R.s(lambda h, off=off, k=k, st2=st2: h.dma_start(
                        out=st2[:],
                        in_=bass.AP(circ2[k].tensor, off, [[D - 1, 128], [1, D]]))
                        .then_inc(sems["sd"], 16))
                    INC("sd", 16)
                    W("a", "sd")
                    R.a(lambda h, k=k, jc=jc, st2=st2: h.activation(
                        out=CT[k][:, jc, :], in_=st2.rearrange("p a t -> p (a t)"),
                        func=AF.Identity).then_inc(sems["va"], 1))
                    INC("va")

        def p3():
            dP = 16384
            qT_h = qTv.tensor
            QOFF0 = qTv.offset

            prj = [ps[2][:, 0:512], ps[2][:, 512:1024],
                   ps[3][:, 0:512], ps[3][:, 512:1024]]
            prj_free = [C["va"]] * 4

            W("v", "vd", C["vd"])
            R.v(lambda h: h.memset(
                bass.AP(vtm.tensor, vtm.offset + 64, [[dP, 128], [130, 32], [65, 2]]),
                1.0).then_inc(sems["vd"], 1))
            INC("vd")
            vtm_ones_vd = C["vd"]

            for hp in range(8):
                for f in range(8):
                    slot = f % 4
                    W("t", "va", prj_free[slot])

                    def _t(h, hp=hp, f=f, slot=slot):
                        for jc in range(8):
                            r = h.matmul(prj[slot],
                                         CT["k"][:, jc, 128 * hp:128 * (hp + 1)],
                                         ln1[:, jc, 512 * f:512 * (f + 1)],
                                         start=(jc == 0), stop=(jc == 7))
                        r.then_inc(sems["pe"], 1)
                    R.t(_t)
                    INC("pe")
                    W("a", "pe")
                    R.a(lambda h, f=f, slot=slot, hp=hp: h.activation(
                        out=kTv[:, 512 * f:512 * (f + 1)], in_=prj[slot],
                        func=AF.Identity, bias=bias_k[:, hp:hp + 1])
                        .then_inc(sems["va"], 1))
                    INC("va")
                    prj_free[slot] = C["va"]

                for tt in range(32):
                    slot = tt % 4
                    W("t", "va", prj_free[slot])

                    def _t(h, hp=hp, tt=tt, slot=slot):
                        for jc in range(8):
                            h.matmul(prj[slot][:, 0:128],
                                     ln1[:, jc, 128 * tt:128 * (tt + 1)],
                                     CT["v"][:, jc, 128 * hp:128 * (hp + 1)],
                                     start=(jc == 0), stop=False)
                        r = h.matmul(prj[slot][:, 0:128], ones_row[:],
                                     vbrow[0:1, 128 * hp:128 * (hp + 1)],
                                     start=False, stop=True)
                        r.then_inc(sems["pe"], 1)
                    R.t(_t)
                    INC("pe")
                    W("a", "pe")

                    def _a(h, tt=tt, slot=slot):
                        h.activation(out=vtm[:, tt, 0:64], in_=prj[slot][:, 0:64],
                                     func=AF.Identity)
                        r = h.activation(out=vtm[:, tt, 65:129], in_=prj[slot][:, 64:128],
                                         func=AF.Identity)
                        r.then_inc(sems["va"], 1)
                    R.a(_a)
                    INC("va")
                    prj_free[slot] = C["va"]

                for f in range(4):
                    slot = f % 4
                    W("t", "va", prj_free[slot])

                    def _t(h, hp=hp, f=f, slot=slot):
                        for jc in range(8):
                            r = h.matmul(prj[slot],
                                         CT["q"][:, jc, 128 * hp:128 * (hp + 1)],
                                         ln1[:, jc, 512 * f:512 * (f + 1)],
                                         start=(jc == 0), stop=(jc == 7))
                        r.then_inc(sems["pe"], 1)
                    R.t(_t)
                    INC("pe")
                    W("a", "pe")
                    R.a(lambda h, f=f, slot=slot, hp=hp: h.activation(
                        out=qTv[:, 512 * f:512 * (f + 1)], in_=prj[slot],
                        func=AF.Identity, bias=bias_q[:, hp:hp + 1])
                        .then_inc(sems["va"], 1))
                    INC("va")
                    prj_free[slot] = C["va"]

                sqscr = pT.rearrange("p r t -> p (r t)")
                qacc = carry[0:2, 0, 0:1]
                kacc = carry[0:2, 0, 1:2]
                ntmp = carry[0:2, 0, 2:3]
                negM2 = carry[0:2, 0, 3:4]
                qmaxs = carry[0:2, 1, 0:4]
                kmaxs = carry[0:2, 1, 4:12]
                for (srcv, nch, maxs, acc) in ((qTv, 4, qmaxs, qacc),
                                               (kTv, 8, kmaxs, kacc)):
                    W("a", "vd", C["vd"])
                    R.a(lambda h, srcv=srcv, nch=nch: h.activation(
                        out=sqscr[:, 0:512 * nch], in_=srcv, func=AF.Square)
                        .then_inc(sems["va"], 1))
                    INC("va")
                    W("t", "va")
                    for ch in range(nch):
                        pst = ps[0] if ch % 2 == 0 else ps[1]
                        W("t", "vd")
                        R.t(lambda h, ch=ch, pst=pst: h.matmul(
                            pst[0:2, 0:512], maskAB[:],
                            sqscr[:, 512 * ch:512 * (ch + 1)],
                            start=True, stop=True).then_inc(sems["pe"], 1))
                        INC("pe")
                        W("v", "pe")
                        R.v(lambda h, ch=ch, pst=pst, maxs=maxs: h.reduce_max(
                            out=maxs[:, ch:ch + 1], in_=pst[0:2, 0:512],
                            axis=mybir.AxisListType.X).then_inc(sems["vd"], 1))
                        INC("vd")
                    W("v", "vd")
                    R.v(lambda h, maxs=maxs, acc=acc, nch=nch: h.reduce_max(
                        out=acc, in_=maxs[:, 0:nch],
                        axis=mybir.AxisListType.X).then_inc(sems["vd"], 1))
                    INC("vd")
                W("v", "vd")
                R.v(lambda h: h.tensor_mul(out=ntmp, in0=qacc, in1=kacc)
                    .then_inc(sems["vd"], 1))
                INC("vd")
                W("a", "vd")
                R.a(lambda h: h.activation(out=ntmp, in_=ntmp, func=AF.Sqrt)
                    .then_inc(sems["va"], 1))
                INC("va")
                W("v", "va")
                R.v(lambda h: h.tensor_single_scalar(
                    out=negM2, in_=ntmp, scalar=-0.125 * 1.02, op=ALU.mult)
                    .then_inc(sems["vd"], 1))
                INC("vd")
                W("s", "vd")
                R.s(lambda h: h.dma_start(out=negMrow[0:1, 0:2], in_=negM2)
                    .then_inc(sems["sd"], 16))
                INC("sd", 16)
                W("t", "sd")
                R.t(lambda h: h.matmul(ps[0][:, 0:2], ones_rowf[:], negMrow[0:1, 0:2],
                                       start=True, stop=True).then_inc(sems["pe"], 1))
                INC("pe")
                W("a", "pe")
                R.a(lambda h: h.activation(out=negMb[:], in_=ps[0][:, 0:2],
                                           func=AF.Identity).then_inc(sems["va"], 1))
                INC("va")

                if DBG and hp == 7:
                    W("g", "va", C["va"])
                    W("g", "vd", C["vd"])
                    def _gd(h):
                        h.dma_start(out=dbg_kT, in_=kTv).then_inc(sems["so"], 16)
                        h.dma_start(out=dbg_qT, in_=qTv).then_inc(sems["so"], 16)
                        h.dma_start(out=dbg_vtm, in_=vtm.rearrange("p a b -> p (a b)")).then_inc(sems["so"], 16)
                        h.dma_start(out=dbg_neg, in_=negMb[:]).then_inc(sems["so"], 16)
                        h.dma_start(out=dbg_sc, in_=carry[0:2, :, :].rearrange("p a b -> p (a b)")).then_inc(sems["so"], 16)
                        h.dma_start(out=dbg_m, in_=maskAB[:]).then_inc(sems["so"], 16)
                        h.dma_start(out=dbg_sq, in_=pT.rearrange("p r t -> p (r t)")).then_inc(sems["so"], 16)
                    R.g(_gd)
                    INC("so", 112)
                    W("t", "so", C["so"])
                W("t", "vd", vtm_ones_vd)
                for qc in range(2):
                    for kc in range(32):
                        W("t", "va")

                        def _t(h, qc=qc, kc=kc):
                            for fh in range(2):
                                for head in (0, 1):
                                    pst = ps[0] if head == 0 else ps[1]
                                    rhs = bass.AP(
                                        qT_h,
                                        64 * head * dP + QOFF0 + 1024 * qc + 512 * fh,
                                        [[dP, 64], [1, 512]])
                                    r = h.matmul(
                                        pst[:, 512 * fh:512 * (fh + 1)],
                                        kTv[64 * head:64 * (head + 1),
                                            128 * kc:128 * (kc + 1)],
                                        rhs, start=True, stop=True,
                                        tile_position=(64 * head, 0))
                            r.then_inc(sems["pe"], 1)
                        R.t(_t)
                        INC("pe")
                        W("a", "pe")

                        def _a(h, kc=kc):
                            h.activation(out=pT[:, 2 * (kc % 2) + 0, :], in_=ps[0][:],
                                         func=AF.Exp, scale=0.125, bias=negMb[:, 0:1])
                            r = h.activation(out=pT[:, 2 * (kc % 2) + 1, :], in_=ps[1][:],
                                             func=AF.Exp, scale=0.125, bias=negMb[:, 1:2])
                            r.then_inc(sems["va"], 1)
                        R.a(_a)
                        INC("va")
                        W("t", "va")

                        def _t2(h, kc=kc):
                            for head in (0, 1):
                                pst = ps[2] if head == 0 else ps[3]
                                for fh in range(2):
                                    r = h.matmul(
                                        pst[0:65, 512 * fh:512 * (fh + 1)],
                                        vtm[:, kc, 65 * head:65 * head + 65],
                                        pT[:, 2 * (kc % 2) + head,
                                           512 * fh:512 * (fh + 1)],
                                        start=(kc == 0), stop=(kc == 31))
                            r.then_inc(sems["pe"], 1)
                        R.t(_t2)
                        INC("pe")

                    W("v", "pe")

                    def _v(h):
                        h.reciprocal(out=lns_at[0:1, 0, :], in_=ps[2][64:65, 0:512])
                        h.reciprocal(out=lns_at[0:1, 1, :], in_=ps[2][64:65, 512:1024])
                        h.reciprocal(out=lns_at2[0:1, 0, :], in_=ps[3][64:65, 0:512])
                        r = h.reciprocal(out=lns_at2[0:1, 1, :], in_=ps[3][64:65, 512:1024])
                        r.then_inc(sems["vd"], 1)
                    R.v(_v)
                    INC("vd")
                    W("t", "vd")

                    def _t3(h):
                        h.matmul(ps[0][0:64, 0:512], ones_rowf[0:1, 0:64],
                                 lns_at[0:1, 0, :], start=True, stop=True)
                        h.matmul(ps[0][0:64, 512:1024], ones_rowf[0:1, 0:64],
                                 lns_at[0:1, 1, :], start=True, stop=True)
                        h.matmul(ps[1][0:64, 0:512], ones_rowf[0:1, 0:64],
                                 lns_at2[0:1, 0, :], start=True, stop=True)
                        r = h.matmul(ps[1][0:64, 512:1024], ones_rowf[0:1, 0:64],
                                     lns_at2[0:1, 1, :], start=True, stop=True)
                        r.then_inc(sems["pe"], 1)
                    R.t(_t3)
                    INC("pe")
                    W("a", "pe")

                    def _a2(h):
                        h.activation(out=rb64[0:64, :], in_=ps[0][0:64, :], func=AF.Identity)
                        r = h.activation(out=rb64[64:128, :], in_=ps[1][0:64, :],
                                         func=AF.Identity)
                        r.then_inc(sems["va"], 1)
                    R.a(_a2)
                    INC("va")
                    W("v", "va")

                    def _v2(h, hp=hp, qc=qc):
                        h.tensor_mul(out=om[0:64, hp, 1024 * qc:1024 * (qc + 1)],
                                     in0=ps[2][0:64, :], in1=rb64[0:64, :])
                        r = h.tensor_mul(out=om[64:128, hp, 1024 * qc:1024 * (qc + 1)],
                                         in0=ps[3][0:64, :], in1=rb64[64:128, :])
                        r.then_inc(sems["vd"], 1)
                    R.v(_v2)
                    INC("vd")
                    W("t", "vd")

            if DBG:
                W("g", "vd", C["vd"])
                def _gd2(h):
                    h.dma_start(out=dbg_om.rearrange("(j p) t -> p j t", p=128),
                                in_=om[:]).then_inc(sems["so"], 16)
                R.g(_gd2)
                INC("so", 16)
            W("t", "vd")
            x_ownR = x_ownT.rearrange("(j p) t -> p j t", p=128)
            x1R = x1_d.rearrange("(j p) t -> p j t", p=128)
            outR = outT.rearrange("(j p) t -> p j t", p=128)
            for mt in range(8):
                for f in range(4):
                    pst = ps[0] if (mt * 4 + f) % 2 == 0 else ps[1]
                    W("s", "vd", C["vd"])
                    R.s(lambda h, mt=mt, f=f: h.dma_start(
                        out=x0c[:], in_=x_ownR[:, mt, 512 * f:512 * (f + 1)])
                        .then_inc(sems["sd"], 16))
                    INC("sd", 16)
                    W("t", "va", C["va"])

                    def _t(h, mt=mt, f=f, pst=pst):
                        for jc in range(8):
                            r = h.matmul(pst[:, 0:512],
                                         CT["o"][:, jc, 128 * mt:128 * (mt + 1)],
                                         om[:, jc, 512 * f:512 * (f + 1)],
                                         start=(jc == 0), stop=(jc == 7))
                        r.then_inc(sems["pe"], 1)
                    R.t(_t)
                    INC("pe")
                    W("a", "pe")
                    W("a", "vd", C["vd"])
                    R.a(lambda h, mt=mt, pst=pst: h.activation(
                        out=tmpo[:], in_=pst[:, 0:512], func=AF.Identity,
                        bias=bias_o[:, mt:mt + 1]).then_inc(sems["va"], 1))
                    INC("va")
                    W("v", "va")
                    W("v", "sd")
                    R.v(lambda h: h.tensor_add(out=x1c[:], in0=x0c[:], in1=tmpo[:])
                        .then_inc(sems["vd"], 1))
                    INC("vd")
                    W("g", "vd")

                    def _g(h, mt=mt, f=f):
                        r = h.dma_start(out=x1R[:, mt, 512 * f:512 * (f + 1)], in_=x1c[:])
                        r.then_inc(sems["sg"], 16)
                        if STAGE == 1:
                            h.dma_start(out=outR[:, mt, 512 * f:512 * (f + 1)],
                                        in_=x1c[:]).then_inc(sems["so"], 16)
                    R.g(_g)
                    INC("sg", 16)
                    if STAGE == 1:
                        INC("so", 16)
                    W("v", "sg", C["sg"])

        s1T = arA_b.rearrange("p (j t) -> p j t", j=8)
        s2T = arB_b.rearrange("p (j t) -> p j t", j=32)
        w1R = w1T_in.rearrange("(j p) t -> p j t", p=128)
        w2R = w2T_in.rearrange("(j p) t -> p j t", p=128)
        arC_f = arC.ap()

        def p4():
            for jc in range(8):
                stg = arC_f[:, 4096 * (jc % 2):4096 * (jc % 2) + 4096]
                W("s", "va", C["va"])
                R.s(lambda h, jc=jc, stg=stg: h.dma_start(
                    out=stg, in_=w1R[:, jc, :]).then_inc(sems["sd"], 16))
                INC("sd", 16)
                W("a", "sd")
                R.a(lambda h, jc=jc, stg=stg: h.activation(
                    out=s1T[:, jc, :], in_=stg, func=AF.Sign).then_inc(sems["va"], 1))
                INC("va")
            for hc in range(32):
                stg = arC_f[:, 4096 * (hc % 2):4096 * (hc % 2) + 1024]
                W("s", "va", C["va"])
                R.s(lambda h, hc=hc, stg=stg: h.dma_start(
                    out=stg, in_=w2R[:, hc, :]).then_inc(sems["sd"], 16))
                INC("sd", 16)
                W("a", "sd")
                R.a(lambda h, hc=hc, stg=stg: h.activation(
                    out=s2T[:, hc, :], in_=stg, func=AF.Sign).then_inc(sems["va"], 1))
                INC("va")
            w1NR = w1N_in.rearrange("(m p) t -> p m t", p=128)
            w2NR = w2N_in.rearrange("(m p) t -> p m t", p=128)
            for (wR, ntile, alout, dd) in ((w1NR, 32, alpha1, D), (w2NR, 8, alpha2, HDIM)):
                for Mt in range(ntile):
                    stg = arC_f[:, dd * (Mt % 2):dd * (Mt % 2) + dd]
                    W("s", "vd", C["vd"])
                    W("s", "va", C["va"])
                    R.s(lambda h, wR=wR, Mt=Mt, stg=stg: h.dma_start(
                        out=stg, in_=wR[:, Mt, :]).then_inc(sems["sd"], 16))
                    INC("sd", 16)
                    W("a", "sd")
                    W("a", "vd", C["vd"])
                    absdst = arD_f[:, 0:dd]
                    R.a(lambda h, stg=stg, absdst=absdst, dd=dd: h.activation(
                        out=absdst, in_=stg, func=AF.Abs, scale=1.0 / dd)
                        .then_inc(sems["va"], 1))
                    INC("va")
                    W("v", "va")

                    def _v(h, absdst=absdst, alout=alout, Mt=Mt):
                        r = h.tensor_reduce(out=alout[:, Mt:Mt + 1], in_=absdst,
                                            axis=mybir.AxisListType.X, op=ALU.add)
                        r.then_inc(sems["vd"], 1)
                    R.v(_v)
                    INC("vd")
            if DBG:
                W("g", "vd", C["vd"])
                R.g(lambda h: h.dma_start(out=dbg_al[:, 0:32], in_=alpha1[:])
                    .then_inc(sems["so"], 16))
                INC("so", 16)
                W("v", "so", C["so"])
            W("g", "va", C["va"])
            s1R = s1T_d.rearrange("(j p) t -> p j t", p=128)
            s2R = s2T_d.rearrange("(j p) t -> p j t", p=128)

            def _g(h):
                h.dma_start(out=s1R[:, :, :], in_=s1T[:]).then_inc(sems["sg"], 16)
                h.dma_start(out=s2R[:, :, :], in_=s2T[:]).then_inc(sems["sg"], 16)
            R.g(_g)
            INC("sg", 32)

        xbuf = arD_f[:, 0:4096].rearrange("p (j t) -> p j t", j=8)
        lnsF = arD_f[:, 4096:5120].rearrange("p (k t) -> p k t", k=2)
        tmpo2 = arD_f[:, 5120:5632]
        ln2c = arD_b[:, 11264:15360].rearrange("p (j t) -> p j t", j=8)
        castb = arD_b[:, 15360:15872]
        sqb = arD_b[:, 15872:16384]

        def ffn_chunks(xsrc_d, xdst_d, also_out):
            xsR = xsrc_d.rearrange("(j p) t -> p j t", p=128)
            xdR = xdst_d.rearrange("(j p) t -> p j t", p=128)
            outR2 = outT.rearrange("(j p) t -> p j t", p=128)
            W("s", "sg", C["sg"])
            for c in range(4):
                W("s", "vd", C["vd"])
                W("s", "sg", C["sg"])
                R.s(lambda h, c=c: h.dma_start(
                    out=xbuf[:], in_=xsR[:, :, 512 * c:512 * (c + 1)])
                    .then_inc(sems["sd"], 16))
                INC("sd", 16)
                W("a", "sd")
                for jc in range(8):
                    if jc > 0:
                        W("a", "pe", C["pe"])

                    def _a(h, jc=jc):
                        h.activation(out=castb, in_=xbuf[:, jc, :], func=AF.Identity)
                        r = h.activation(out=sqb, in_=xbuf[:, jc, :], func=AF.Square)
                        r.then_inc(sems["va"], 1)
                    R.a(_a)
                    INC("va")
                    W("t", "va")

                    def _t(h, jc=jc):
                        h.matmul(ps[3][0:1, 0:512], ones_col[:], castb,
                                 start=(jc == 0), stop=(jc == 7))
                        r = h.matmul(ps[3][0:1, 512:1024], ones_col[:], sqb,
                                     start=(jc == 0), stop=(jc == 7))
                        r.then_inc(sems["pe"], 1)
                    R.t(_t)
                    INC("pe")
                W("v", "pe")

                def _v1(h):
                    h.tensor_single_scalar(out=lnsF[0:1, 0, :], in_=ps[3][0:1, 0:512],
                                           scalar=1.0 / D, op=ALU.mult)
                    h.tensor_single_scalar(out=tmpo2[0:1, :], in_=ps[3][0:1, 512:1024],
                                           scalar=1.0 / D, op=ALU.mult)
                    h.tensor_mul(out=castb[0:1, :], in0=lnsF[0:1, 0, :],
                                 in1=lnsF[0:1, 0, :])
                    h.tensor_sub(out=tmpo2[0:1, :], in0=tmpo2[0:1, :],
                                 in1=castb[0:1, :])
                    r = h.tensor_single_scalar(out=tmpo2[0:1, :],
                                               in_=tmpo2[0:1, :],
                                               scalar=1e-5, op=ALU.add)
                    r.then_inc(sems["vd"], 1)
                R.v(_v1)
                INC("vd")
                W("a", "vd")
                R.a(lambda h: h.activation(out=tmpo2[0:1, :],
                                           in_=tmpo2[0:1, :],
                                           func=AF.Sqrt).then_inc(sems["va"], 1))
                INC("va")
                W("v", "va")
                R.v(lambda h: h.reciprocal(out=lnsF[0:1, 1, :], in_=tmpo2[0:1, :])
                    .then_inc(sems["vd"], 1))
                INC("vd")
                W("t", "vd")

                def _t2(h):
                    h.matmul(ps[2][:, 0:512], ones_rowf[:], lnsF[0:1, 0, :],
                             start=True, stop=True)
                    r = h.matmul(ps[2][:, 512:1024], ones_rowf[:], lnsF[0:1, 1, :],
                                 start=True, stop=True)
                    r.then_inc(sems["pe"], 1)
                R.t(_t2)
                INC("pe")
                W("v", "pe")

                def _v2(h):
                    for jc in range(8):
                        h.tensor_sub(out=castb, in0=xbuf[:, jc, :], in1=ps[2][:, 0:512])
                        r = h.tensor_mul(out=ln2c[:, jc, :], in0=castb,
                                         in1=ps[2][:, 512:1024])
                    r.then_inc(sems["vd"], 1)
                R.v(_v2)
                INC("vd")
                if DBG and also_out and c == 0:
                    W("g", "vd", C["vd"])
                    def _gdf(h):
                        h.dma_start(out=dbg_ln2, in_=ln2c[:]).then_inc(sems["so"], 16)
                        h.dma_start(out=dbg_xb, in_=xbuf[:]).then_inc(sems["so"], 16)
                        al = dbg_al
                        h.dma_start(out=al[:, 32:40], in_=alpha2[:]).then_inc(sems["so"], 16)
                        h.dma_start(out=al[:, 40:44], in_=b2cols[:, 0:4]).then_inc(sems["so"], 16)
                        h.dma_start(out=al[:, 44:48], in_=bias_q[:, 0:4]).then_inc(sems["so"], 16)
                    R.g(_gdf)
                    INC("so", 80)
                    W("v", "so", C["so"])
                h_bf = arC_b.rearrange("p (m t) -> p m t", m=32)
                W("t", "vd")
                for Mt in range(32):
                    pst = ps[0] if Mt % 2 == 0 else ps[1]
                    W("t", "va", C["va"])

                    def _t3(h, Mt=Mt, pst=pst):
                        for jc in range(8):
                            r = h.matmul(pst[:, 0:512], s1T[:, jc, 128 * Mt:128 * (Mt + 1)],
                                         ln2c[:, jc, :], start=(jc == 0), stop=(jc == 7))
                        r.then_inc(sems["pe"], 1)
                    R.t(_t3)
                    INC("pe")
                    W("a", "pe")
                    R.a(lambda h, Mt=Mt, pst=pst: h.activation(
                        out=h_bf[:, Mt, :], in_=pst[:, 0:512], func=AF.Gelu,
                        scale=alpha1[:, Mt:Mt + 1], bias=b1cols[:, Mt:Mt + 1])
                        .then_inc(sems["va"], 1))
                    INC("va")
                if DBG and also_out and c == 0:
                    W("g", "va", C["va"])
                    R.g(lambda h: h.dma_start(out=dbg_h, in_=h_bf[:, 0:4, :])
                        .then_inc(sems["so"], 16))
                    INC("so", 16)
                    W("t", "so", C["so"])
                for Mt in range(8):
                    pst = ps[0] if Mt % 2 == 0 else ps[1]
                    W("t", "va", C["va"])

                    def _t4(h, Mt=Mt, pst=pst):
                        for hc in range(32):
                            r = h.matmul(pst[:, 0:512], s2T[:, hc, 128 * Mt:128 * (Mt + 1)],
                                         h_bf[:, hc, :], start=(hc == 0), stop=(hc == 31))
                        r.then_inc(sems["pe"], 1)
                    R.t(_t4)
                    INC("pe")
                    W("a", "pe")
                    W("a", "vd", C["vd"])
                    R.a(lambda h, Mt=Mt, pst=pst: h.activation(
                        out=tmpo2[:], in_=pst[:, 0:512], func=AF.Identity,
                        scale=alpha2[:, Mt:Mt + 1], bias=b2cols[:, Mt:Mt + 1])
                        .then_inc(sems["va"], 1))
                    INC("va")
                    W("v", "va")
                    R.v(lambda h, Mt=Mt: h.tensor_add(
                        out=xbuf[:, Mt, :], in0=xbuf[:, Mt, :], in1=tmpo2[:])
                        .then_inc(sems["vd"], 1))
                    INC("vd")
                    W("g", "vd")

                    def _g(h, Mt=Mt, c=c):
                        r = h.dma_start(out=xdR[:, Mt, 512 * c:512 * (c + 1)],
                                        in_=xbuf[:, Mt, :])
                        r.then_inc(sems["sg"], 16)
                        if also_out:
                            h.dma_start(out=outR2[:, Mt, 512 * c:512 * (c + 1)],
                                        in_=xbuf[:, Mt, :]).then_inc(sems["so"], 16)
                    R.g(_g)
                    INC("sg", 16)
                    if also_out:
                        INC("so", 16)
                    W("v", "sg", C["sg"])

        def p6():
            xsR = x2_d.rearrange("(j p) t -> p j t", p=128)
            l3R = ln3_loc.rearrange("(j p) t -> p j t", p=128)
            W("s", "sg", C["sg"])
            for c in range(4):
                W("s", "vd", C["vd"])
                W("s", "sg", C["sg"])
                R.s(lambda h, c=c: h.dma_start(
                    out=xbuf[:], in_=xsR[:, :, 512 * c:512 * (c + 1)])
                    .then_inc(sems["sd"], 16))
                INC("sd", 16)
                W("a", "sd")
                for jc in range(8):
                    if jc > 0:
                        W("a", "pe", C["pe"])

                    def _a(h, jc=jc):
                        h.activation(out=castb, in_=xbuf[:, jc, :], func=AF.Identity)
                        r = h.activation(out=sqb, in_=xbuf[:, jc, :], func=AF.Square)
                        r.then_inc(sems["va"], 1)
                    R.a(_a)
                    INC("va")
                    W("t", "va")

                    def _t(h, jc=jc):
                        h.matmul(ps[3][0:1, 0:512], ones_col[:], castb,
                                 start=(jc == 0), stop=(jc == 7))
                        r = h.matmul(ps[3][0:1, 512:1024], ones_col[:], sqb,
                                     start=(jc == 0), stop=(jc == 7))
                        r.then_inc(sems["pe"], 1)
                    R.t(_t)
                    INC("pe")
                W("v", "pe")

                def _v1(h):
                    h.tensor_single_scalar(out=lnsF[0:1, 0, :], in_=ps[3][0:1, 0:512],
                                           scalar=1.0 / D, op=ALU.mult)
                    h.tensor_single_scalar(out=tmpo2[0:1, :], in_=ps[3][0:1, 512:1024],
                                           scalar=1.0 / D, op=ALU.mult)
                    h.tensor_mul(out=castb[0:1, :], in0=lnsF[0:1, 0, :],
                                 in1=lnsF[0:1, 0, :])
                    h.tensor_sub(out=tmpo2[0:1, :], in0=tmpo2[0:1, :], in1=castb[0:1, :])
                    r = h.tensor_single_scalar(out=tmpo2[0:1, :], in_=tmpo2[0:1, :],
                                               scalar=1e-5, op=ALU.add)
                    r.then_inc(sems["vd"], 1)
                R.v(_v1)
                INC("vd")
                W("a", "vd")
                R.a(lambda h: h.activation(out=tmpo2[0:1, :], in_=tmpo2[0:1, :],
                                           func=AF.Sqrt).then_inc(sems["va"], 1))
                INC("va")
                W("v", "va")
                R.v(lambda h: h.reciprocal(out=lnsF[0:1, 1, :], in_=tmpo2[0:1, :])
                    .then_inc(sems["vd"], 1))
                INC("vd")
                W("t", "vd")

                def _t2(h):
                    h.matmul(ps[2][:, 0:512], ones_rowf[:], lnsF[0:1, 0, :],
                             start=True, stop=True)
                    r = h.matmul(ps[2][:, 512:1024], ones_rowf[:], lnsF[0:1, 1, :],
                                 start=True, stop=True)
                    r.then_inc(sems["pe"], 1)
                R.t(_t2)
                INC("pe")
                W("v", "pe")

                def _v2(h):
                    for jc in range(8):
                        h.tensor_sub(out=castb, in0=xbuf[:, jc, :], in1=ps[2][:, 0:512])
                        r = h.tensor_mul(out=ln2c[:, jc, :], in0=castb,
                                         in1=ps[2][:, 512:1024])
                    r.then_inc(sems["vd"], 1)
                R.v(_v2)
                INC("vd")
                W("g", "vd")
                R.g(lambda h, c=c: h.dma_start(
                    out=l3R[:, :, 512 * c:512 * (c + 1)], in_=ln2c[:])
                    .then_inc(sems["sg"], 16))
                INC("sg", 16)
                W("v", "sg", C["sg"])
            W("g", "sg", C["sg"])
            R.g(lambda h: h.collective_compute(
                "AllGather", ALU.bypass, replica_groups=RG,
                ins=[ln3_loc], outs=[ln3_all]).then_inc(sems["vc"], 1))
            INC("vc")

        def p7():
            for (wsrc, off) in ((w_inT_x, 0), (w_inT_z, 256)):
                wr = wsrc.rearrange("(j p) e -> p j e", p=128)
                W("s", "va", C["va"])
                R.s(lambda h, wr=wr: h.dma_start(
                    out=arC_f[:, 0:2048].rearrange("p (j e) -> p j e", j=8),
                    in_=wr).then_inc(sems["sd"], 16))
                INC("sd", 16)
                W("a", "sd")

                def _a(h, off=off):
                    src2 = arC_f[:, 0:2048].rearrange("p (j e) -> p j e", j=8)
                    for jc in range(8):
                        r = h.activation(out=wioT[:, jc, off:off + 256],
                                         in_=src2[:, jc, :], func=AF.Identity)
                    r.then_inc(sems["va"], 1)
                R.a(_a)
                INC("va")
            wxr = wxT_sl.rearrange("(k p) n -> p k n", p=128)
            W("s", "va", C["va"])
            R.s(lambda h: h.dma_start(
                out=arC_f[:, 0:66].rearrange("p (k n) -> p k n", k=2), in_=wxr)
                .then_inc(sems["sd"], 16))
            INC("sd", 16)
            W("a", "sd")

            def _awx(h):
                src2 = arC_f[:, 0:66].rearrange("p (k n) -> p k n", k=2)
                h.activation(out=wxTb[:, 0, :], in_=src2[:, 0, :], func=AF.Identity)
                r = h.activation(out=wxTb[:, 1, :], in_=src2[:, 1, :], func=AF.Identity)
                r.then_inc(sems["va"], 1)
            R.a(_awx)
            INC("va")

            lnall = ln3_all.rearrange("(bk j p) t -> bk p j t", p=128, j=8)
            uR = u_d.rearrange("(k p) t -> p k t", p=128)
            zR = zs_d.rearrange("(k p) t -> p k t", p=128)
            bpR = bcd_part
            rhsT = arA_b[:, 0:16384].rearrange("p (j t) -> p j t", j=8)
            ublk = arC_f[:, 0:4102].rearrange("p (k t) -> p k t", k=2)
            zblk = arB.ap()[:, 0:4096].rearrange("p (k t) -> p k t", k=2)
            uS = arA.ap()[:, 8192:12288].rearrange("p (k t) -> p k t", k=2)
            halo = arA.ap()[:, 12288:12294].rearrange("p (k t) -> p k t", k=2)
            xpev = arA.ap()[:, 12300:14348][0:33, :]
            W("v", "vc", C["vc"])
            for tcb in range(8):
                seq_start = (tcb % 2 == 0)
                W("s", "vc", C["vc"])
                W("s", "va", C["va"])
                W("s", "pe", C["pe"])
                R.s(lambda h, tcb=tcb: h.dma_start(
                    out=rhsT[:], in_=lnall[tcb]).then_inc(sems["sd"], 16))
                INC("sd", 16)
                for th in range(2):
                    W("t", "sd")
                    W("t", "va", C["va"])

                    def _t(h, th=th):
                        for jc in range(8):
                            for Mt in range(4):
                                for tc in range(2):
                                    r = h.matmul(
                                        ps[Mt][:, 512 * tc:512 * (tc + 1)],
                                        wioT[:, jc, 128 * Mt:128 * (Mt + 1)],
                                        rhsT[:, jc, 1024 * th + 512 * tc:
                                             1024 * th + 512 * tc + 512],
                                        start=(jc == 0), stop=(jc == 7))
                        r.then_inc(sems["pe"], 1)
                    R.t(_t)
                    INC("pe")
                    W("a", "pe")

                    def _a(h, th=th):
                        for Mt in range(2):
                            h.activation(out=ublk[:, Mt, 3 + 1024 * th:3 + 1024 * th + 1024],
                                         in_=ps[Mt][:, 0:1024], func=AF.Identity)
                        for Mt in range(2):
                            r = h.activation(out=zblk[:, Mt, 1024 * th:1024 * th + 1024],
                                             in_=ps[Mt + 2][:, 0:1024], func=AF.Silu)
                        r.then_inc(sems["va"], 1)
                    R.a(_a)
                    INC("va")
                W("v", "va")
                if seq_start:
                    R.v(lambda h: h.memset(ublk[:, :, 0:3], 0.0).then_inc(sems["vd"], 1))
                else:
                    R.v(lambda h: h.tensor_copy(out=ublk[:, :, 0:3], in_=halo[:])
                        .then_inc(sems["vd"], 1))
                INC("vd")
                W("v", "vd")

                def _vc(h):
                    for k in range(2):
                        h.tensor_scalar(out=uS[:, k, :], in0=ublk[:, k, 3:2051],
                                        scalar1=conv_sb[:, k, 3:4],
                                        scalar2=convb_sb[:, k:k + 1],
                                        op0=ALU.mult, op1=ALU.add)
                        for sh in (1, 2, 3):
                            r = h.scalar_tensor_tensor(
                                out=uS[:, k, :], in0=ublk[:, k, 3 - sh:2051 - sh],
                                scalar=conv_sb[:, k, 3 - sh:4 - sh],
                                in1=uS[:, k, :], op0=ALU.mult, op1=ALU.add)
                    r = h.tensor_copy(out=halo[:], in_=ublk[:, :, 2048:2051])
                    r.then_inc(sems["vd"], 1)
                R.v(_vc)
                INC("vd")
                W("a", "vd")
                R.a(lambda h: h.activation(out=uS[:, :, :].rearrange("p a b -> p (a b)"),
                                           in_=uS[:, :, :].rearrange("p a b -> p (a b)"),
                                           func=AF.Silu).then_inc(sems["va"], 1))
                INC("va")
                ubf = arA_b[:, 28672:32768].rearrange("p (k t) -> p k t", k=2)
                W("v", "va")
                R.v(lambda h: h.tensor_copy(out=ubf[:, :, :], in_=uS[:, :, :])
                    .then_inc(sems["vd"], 1))
                INC("vd")
                for tc in range(4):
                    W("t", "vd", C["vd"])
                    W("t", "va", C["va"])

                    def _tx(h, tc=tc):
                        for k in range(2):
                            r = h.matmul(ps[0][0:33, 512 * (tc % 2):512 * (tc % 2) + 512],
                                         wxTb[:, k, :],
                                         ubf[:, k, 512 * tc:512 * (tc + 1)],
                                         start=(k == 0), stop=(k == 1))
                        r.then_inc(sems["pe"], 1)
                    R.t(_tx)
                    INC("pe")
                    W("a", "pe")
                    R.a(lambda h, tc=tc: h.activation(
                        out=xpev[:, 512 * tc:512 * (tc + 1)],
                        in_=ps[0][0:33, 512 * (tc % 2):512 * (tc % 2) + 512],
                        func=AF.Identity).then_inc(sems["va"], 1))
                    INC("va")
                W("g", "va", C["va"])
                W("g", "vd", C["vd"])

                def _g(h, tcb=tcb):
                    h.dma_start(out=uR[:, :, 2048 * tcb:2048 * (tcb + 1)],
                                in_=uS[:, :, :]).then_inc(sems["sg"], 16)
                    h.dma_start(out=zR[:, :, 2048 * tcb:2048 * (tcb + 1)],
                                in_=zblk[:, :, :]).then_inc(sems["sg"], 16)
                    h.dma_start(out=bpR[:, 2048 * tcb:2048 * (tcb + 1)],
                                in_=xpev[:]).then_inc(sems["sg"], 16)
                R.g(_g)
                INC("sg", 48)
                W("a", "sg", C["sg"])
                W("v", "sg", C["sg"])
            W("g", "sg", C["sg"])
            R.g(lambda h: h.collective_compute(
                "AllReduce", ALU.add, replica_groups=RG,
                ins=[bcd_part], outs=[bcd_all]).then_inc(sems["vc"], 1))
            INC("vc")

        def p8():
            gR = gated_loc.rearrange("(k p) t -> p k t", p=128)
            uR = u_d.rearrange("(k p) t -> p k t", p=128)
            zR = zs_d.rearrange("(k p) t -> p k t", p=128)
            bcdc = arB.ap()[0:33, 0:2048]
            slotA = arB.ap()[0:1, 14336:16384]
            arBb = arB.ap().bitcast(BF16)
            bcdb = arBb[0:33, 4096:6144]
            sl_b = [arBb[0:1, 28672:30720], arBb[32:33, 28672:30720]]
            sl_c = [arBb[64:65, 28672:30720], arBb[0:1, 30720:32768]]
            ob_b = [None, None]
            ps_bB = [None, None]
            ps_bC = [None, None]
            scr1 = arB.ap()[:, 2048:6144].rearrange("p (k t) -> p k t", k=2)
            scr2 = arB.ap()[:, 6144:10240].rearrange("p (k t) -> p k t", k=2)
            tmpy = arB.ap()[:, 10240:14336].rearrange("p (k t) -> p k t", k=2)
            dt_v = arD_f[:, 0:4096].rearrange("p (k t) -> p k t", k=2)
            dtu_v = arD_f[:, 4096:8192].rearrange("p (k t) -> p k t", k=2)
            dA_v = arC_f[:, 0:4096].rearrange("p (k t) -> p k t", k=2)
            dA_v2 = arC_f[:, 4096:8192].rearrange("p (k t) -> p k t", k=2)
            hbuf = arB.ap()[:, 6144:10240].rearrange("p (k t) -> p k t", k=2)
            uc_v = arA.ap()[:, 0:4096].rearrange("p (k t) -> p k t", k=2)
            zc_v = arA.ap()[:, 4096:8192].rearrange("p (k t) -> p k t", k=2)
            yacc = arA.ap()[:, 8192:12288].rearrange("p (k t) -> p k t", k=2)
            gbf = arA_b[:, 24576:28672].rearrange("p (k t) -> p k t", k=2)

            for sc in range(8):
                seq_start = (sc % 2 == 0)
                W("s", "vc", C["vc"])
                W("s", "vd", C["vd"])
                W("s", "va", C["va"])

                def _s(h, sc=sc):
                    h.dma_start(out=bcdc, in_=bcd_all[:, 2048 * sc:2048 * (sc + 1)])                         .then_inc(sems["sd"], 16)
                    h.dma_start(out=uc_v[:, :, :], in_=uR[:, :, 2048 * sc:2048 * (sc + 1)])                         .then_inc(sems["sd"], 16)
                    h.dma_start(out=zc_v[:, :, :], in_=zR[:, :, 2048 * sc:2048 * (sc + 1)])                         .then_inc(sems["sd"], 16)
                R.s(_s)
                INC("sd", 48)
                W("s", "sd", C["sd"])
                W("s", "pe", C["pe"])
                R.s(lambda h: h.dma_start(out=slotA, in_=bcdc[32:33, :])
                    .then_inc(sems["sd"], 16))
                INC("sd", 16)
                W("t", "sd")
                W("t", "va", C["va"])
                W("t", "vd", C["vd"])

                def _tb(h):
                    for tc in range(4):
                        pst = ps[2] if tc < 2 else ps[3]
                        r = h.matmul(pst[:, 512 * (tc % 2):512 * (tc % 2) + 512],
                                     ones_rowf[:], slotA[0:1, 512 * tc:512 * (tc + 1)],
                                     start=True, stop=True)
                    r.then_inc(sems["pe"], 1)
                R.t(_tb)
                INC("pe")
                W("a", "pe")

                def _asp(h):
                    for k in range(2):
                        for half in range(2):
                            pst = ps[2] if half == 0 else ps[3]
                            h.activation(out=scr1[:, k, 1024 * half:1024 * (half + 1)],
                                         in_=pst[:, 0:1024], func=AF.Identity,
                                         scale=dtw_sb[:, k:k + 1], bias=dtb_sb[:, k:k + 1])
                    h.activation(out=scr2.rearrange("p a b -> p (a b)"),
                                 in_=scr1.rearrange("p a b -> p (a b)"), func=AF.Abs)
                    h.activation(out=scr2.rearrange("p a b -> p (a b)"),
                                 in_=scr2.rearrange("p a b -> p (a b)"), func=AF.Exp,
                                 scale=-1.0)
                    r = h.activation(out=scr2.rearrange("p a b -> p (a b)"),
                                     in_=scr2.rearrange("p a b -> p (a b)"), func=AF.Ln,
                                     bias=1.0)
                    r.then_inc(sems["va"], 1)
                R.a(_asp)
                INC("va")
                W("v", "va")

                def _vsp(h):
                    h.scalar_tensor_tensor(out=dt_v.rearrange("p a b -> p (a b)"),
                                           in0=scr1.rearrange("p a b -> p (a b)"),
                                           scalar=0.0, in1=scr2.rearrange("p a b -> p (a b)"),
                                           op0=ALU.max, op1=ALU.add)
                    r = h.tensor_mul(out=dtu_v.rearrange("p a b -> p (a b)"),
                                     in0=dt_v.rearrange("p a b -> p (a b)"),
                                     in1=uc_v.rearrange("p a b -> p (a b)"))
                    r.then_inc(sems["vd"], 1)
                R.v(_vsp)
                INC("vd")
                W("a", "sd", C["sd"])
                W("a", "va", C["va"])
                R.a(lambda h: h.activation(out=bcdb, in_=bcdc, func=AF.Identity)
                    .then_inc(sems["va"], 1))
                INC("va")
                vd_hist = {}
                for n in range(NST):
                    par = n % 2
                    pA = ps[0] if par == 0 else ps[2]
                    pB = ps[1] if par == 0 else ps[3]
                    sB = sl_b[par]
                    sC = sl_c[par]
                    obB = ones_row[:] if par == 0 else onesq_b[32:33, :]
                    obC = onesq_b[64:65, :] if par == 0 else ones_row[:]
                    W("s", "pe", C["pe"])
                    W("s", "va", C["va"])

                    def _sn(h, n=n, sB=sB, sC=sC):
                        h.dma_start(out=sB, in_=bcdb[n:n + 1, :]).then_inc(sems["sd"], 16)
                        h.dma_start(out=sC, in_=bcdb[NST + n:NST + n + 1, :]) \
                            .then_inc(sems["sd"], 16)
                    R.s(_sn)
                    INC("sd", 32)
                    W("t", "vd", vd_hist.get((n - 2, "y"), C["vd"]))
                    W("t", "va", C["va"])
                    W("t", "sd", C["sd"])

                    def _tnB(h, pA=pA, pB=pB, sB=sB, obB=obB):
                        for tc in range(4):
                            pst = pA if tc < 2 else pB
                            r = h.matmul(pst[:, 512 * (tc % 2):512 * (tc % 2) + 512],
                                         obB, sB[0:1, 512 * tc:512 * (tc + 1)],
                                         start=True, stop=True)
                        r.then_inc(sems["pe"], 1)
                    R.t(_tnB)
                    INC("pe")
                    peB = C["pe"]
                    W("a", "pe", C["pe"])
                    W("a", "vd", vd_hist.get((n - 2, "y"), C["vd"]))
                    dAp = dA_v if par == 0 else dA_v2

                    def _an(h, n=n, dAp=dAp):
                        for k in range(2):
                            r = h.activation(out=dAp[:, k, :], in_=dt_v[:, k, :],
                                             func=AF.Exp, scale=A_sb[:, k, n:n + 1])
                        r.then_inc(sems["va"], 1)
                    R.a(_an)
                    INC("va")
                    W("v", "pe", peB)

                    def _v1(h, pA=pA, pB=pB):
                        for k in range(2):
                            for half in range(2):
                                pst = pA if half == 0 else pB
                                r = h.tensor_mul(
                                    out=hbuf[:, k, 1024 * half:1024 * (half + 1)],
                                    in0=dtu_v[:, k, 1024 * half:1024 * (half + 1)],
                                    in1=pst[:, 0:1024])
                        r.then_inc(sems["vd"], 1)
                    R.v(_v1)
                    INC("vd")
                    vd_hist[(n, "b")] = C["vd"]
                    W("t", "vd", vd_hist[(n, "b")])

                    def _tnC(h, pA=pA, pB=pB, sC=sC, obC=obC):
                        for tc in range(4):
                            pst = pA if tc < 2 else pB
                            r = h.matmul(pst[:, 512 * (tc % 2):512 * (tc % 2) + 512],
                                         obC, sC[0:1, 512 * tc:512 * (tc + 1)],
                                         start=True, stop=True)
                        r.then_inc(sems["pe"], 1)
                    R.t(_tnC)
                    INC("pe")
                    peC = C["pe"]
                    W("v", "va")

                    def _v2(h, n=n, seq_start=seq_start, dAp=dAp, pA=pA, pB=pB, peC=peC):
                        for k in range(2):
                            ini = 0.0 if seq_start else carry[:, k, n:n + 1]
                            h.tensor_tensor_scan(out=hbuf[:, k, :], data0=dAp[:, k, :],
                                                 data1=hbuf[:, k, :], initial=ini,
                                                 op0=ALU.mult, op1=ALU.add)
                        h.tensor_copy(out=carry[:, 0, n:n + 1], in_=hbuf[:, 0, 2047:2048])
                        h.tensor_copy(out=carry[:, 1, n:n + 1], in_=hbuf[:, 1, 2047:2048])
                        h.wait_ge(sems["pe"], peC)
                        for k in range(2):
                            for half in range(2):
                                pst = pA if half == 0 else pB
                                h.tensor_mul(out=tmpy[:, k, 1024 * half:1024 * (half + 1)],
                                             in0=hbuf[:, k, 1024 * half:1024 * (half + 1)],
                                             in1=pst[:, 0:1024])
                        if n == 0:
                            r = h.tensor_copy(out=yacc.rearrange("p a b -> p (a b)"),
                                              in_=tmpy.rearrange("p a b -> p (a b)"))
                        else:
                            r = h.tensor_add(out=yacc.rearrange("p a b -> p (a b)"),
                                             in0=yacc.rearrange("p a b -> p (a b)"),
                                             in1=tmpy.rearrange("p a b -> p (a b)"))
                        r.then_inc(sems["vd"], 1)
                    R.v(_v2)
                    INC("vd")
                    vd_hist[(n, "y")] = C["vd"]
                W("v", "vd")

                def _vg(h):
                    for k in range(2):
                        h.scalar_tensor_tensor(out=yacc[:, k, :], in0=uc_v[:, k, :],
                                               scalar=dpar_sb[:, k:k + 1],
                                               in1=yacc[:, k, :],
                                               op0=ALU.mult, op1=ALU.add)
                    r = h.tensor_mul(out=gbf.rearrange("p a b -> p (a b)"),
                                     in0=yacc.rearrange("p a b -> p (a b)"),
                                     in1=zc_v.rearrange("p a b -> p (a b)"))
                    r.then_inc(sems["vd"], 1)
                R.v(_vg)
                INC("vd")
                W("g", "vd")
                R.g(lambda h, sc=sc: h.dma_start(
                    out=gR[:, :, 2048 * sc:2048 * (sc + 1)], in_=gbf[:, :, :])
                    .then_inc(sems["sg"], 16))
                INC("sg", 16)
                W("v", "sg", C["sg"])
            W("g", "sg", C["sg"])
            R.g(lambda h: h.collective_compute(
                "AllGather", ALU.bypass, replica_groups=RG,
                ins=[gated_loc], outs=[gated_all]).then_inc(sems["vc"], 1))
            INC("vc")

        def p9():
            woR = w_outT_in.rearrange("(k p) dd -> p k dd", p=128)
            woT = arC_b[:, 0:16384].rearrange("p (k dd) -> p k dd", k=16)
            for half in range(2):
                stg9 = arD_f[:, 0:8192] if half == 0 else arB.ap()[:, 0:8192]
                W("s", "va", C["va"])
                W("s", "vd", C["vd"])
                R.s(lambda h, half=half, stg9=stg9: h.dma_start(
                    out=stg9.rearrange("p (k dd) -> p k dd", k=8),
                    in_=woR[:, 8 * half:8 * (half + 1), :]).then_inc(sems["sd"], 16))
                INC("sd", 16)
                W("a", "sd")
                W("a", "vd", C["vd"])
                W("a", "sg", C["sg"])

                def _a(h, half=half, stg9=stg9):
                    src2 = stg9.rearrange("p (k dd) -> p k dd", k=8)
                    for k in range(8):
                        r = h.activation(out=woT[:, 8 * half + k, :], in_=src2[:, k, :],
                                         func=AF.Identity)
                    r.then_inc(sems["va"], 1)
                R.a(_a)
                INC("va")
            W("g", "va", C["va"])
            roff = nc.gpsimd.alloc_register("roff")
            ga_h = gated_all.tensor
            x2R = x2_d.rearrange("(j p) t -> p j t", p=128)
            x3R = x3_d.rearrange("(j p) t -> p j t", p=128)
            outR3 = outT.rearrange("(j p) t -> p j t", p=128)
            grhs = arD_b[:, 0:8192].rearrange("p (k t) -> p k t", k=16)
            x2c = arD_f[:, 4096:4608]
            tmpr = arD_f[:, 4608:5120]
            W("g", "vc", C["vc"])
            for f in range(4):
                W("g", "pe", C["pe"])
                W("g", "vd", C["vd"])
                W("g", "va", C["va"])

                def _s(h, f=f):
                    pid = h.partition_id()
                    h.reg_alu(roff, pid, TOK, ALU.mult)
                    h.reg_alu(roff, roff, 512 * f, ALU.add)
                    h.dma_start(out=grhs[:, :, :],
                                in_=bass.AP(ga_h, roff,
                                            [[T16, 128], [128 * T16, 16], [1, 512]]))                         .then_inc(sems["sg"], 16)
                R.g(_s)
                INC("sg", 16)
                if DBG and f == 0:
                    W("g", "sg", C["sg"])
                    W("g", "va", C["va"])
                    def _gdw(h):
                        h.dma_start(out=dbg_gr, in_=grhs[:, :, :]).then_inc(sems["so"], 16)
                        h.dma_start(out=dbg_wo, in_=woT[:, :, :]).then_inc(sems["so"], 16)
                    R.g(_gdw)
                    INC("so", 32)
                    W("t", "so", C["so"])
                for Mt in range(8):
                    pst = ps[0] if Mt % 2 == 0 else ps[1]
                    W("t", "sg", C["sg"])
                    W("t", "va", C["va"])

                    def _t(h, Mt=Mt, pst=pst):
                        for ec in range(16):
                            r = h.matmul(pst[:, 0:512], woT[:, ec, 128 * Mt:128 * (Mt + 1)],
                                         grhs[:, ec, :], start=(ec == 0), stop=(ec == 15))
                        r.then_inc(sems["pe"], 1)
                    R.t(_t)
                    INC("pe")
                    W("s", "vd", C["vd"])
                    W("s", "sg", C["sg"])
                    R.s(lambda h, Mt=Mt, f=f: h.dma_start(
                        out=x2c, in_=x2R[:, Mt, 512 * f:512 * (f + 1)])
                        .then_inc(sems["sd"], 16))
                    INC("sd", 16)
                    W("a", "pe")
                    W("a", "vd", C["vd"])
                    R.a(lambda h, pst=pst: h.activation(
                        out=tmpr, in_=pst[:, 0:512], func=AF.Identity)
                        .then_inc(sems["va"], 1))
                    INC("va")
                    W("v", "va")
                    W("v", "sd")
                    R.v(lambda h: h.tensor_add(out=x2c, in0=x2c, in1=tmpr)
                        .then_inc(sems["vd"], 1))
                    INC("vd")
                    W("g", "vd")

                    def _g(h, Mt=Mt, f=f):
                        r = h.dma_start(out=x3R[:, Mt, 512 * f:512 * (f + 1)], in_=x2c)
                        r.then_inc(sems["sg"], 16)
                        if STAGE == 3:
                            h.dma_start(out=outR3[:, Mt, 512 * f:512 * (f + 1)],
                                        in_=x2c).then_inc(sems["so"], 16)
                    R.g(_g)
                    INC("sg", 16)
                    if STAGE == 3:
                        INC("so", 16)
                    W("v", "sg", C["sg"])

        def p10():
            s1R = s1T_d.rearrange("(j p) t -> p j t", p=128)
            s2R = s2T_d.rearrange("(j p) t -> p j t", p=128)
            W("s", "va", C["va"])
            W("s", "vd", C["vd"])
            W("s", "pe", C["pe"])
            W("s", "sg", C["sg"])

            def _s(h):
                h.dma_start(out=s1T[:], in_=s1R).then_inc(sems["sd"], 16)
                h.dma_start(out=s2T[:], in_=s2R).then_inc(sems["sd"], 16)
            R.s(_s)
            INC("sd", 32)
            W("t", "sd")
            ffn_chunks(x3_d, x1_d, STAGE == 4)

        p0()
        p1()
        if DBG:
            W("g", "vd", C["vd"])
            def _gdbg(h):
                h.dma_start(out=dbg_ln1.rearrange("(j p) t -> p j t", p=128),
                            in_=ln1[:]).then_inc(sems["so"], 16)
            R.g(_gdbg)
            INC("so", 16)
        p2()
        p3()
        if STAGE >= 2:
            p4()
            ffn_chunks(x1_d, x2_d, STAGE == 2)
        if STAGE >= 3:
            p6()
            p7()
            p8()
            p9()
        if STAGE >= 4:
            p10()

        if DBG and STAGE >= 3:
            W("g", "sg", C["sg"])
            W("g", "vc", C["vc"])

            def _gm(h):
                h.dma_start(out=dbg_u, in_=u_d[:, 0:2048]).then_inc(sems["so"], 16)
                h.dma_start(out=dbg_z, in_=zs_d[:, 0:2048]).then_inc(sems["so"], 16)
                h.dma_start(out=dbg_bcd, in_=bcd_all[:, 0:2048]).then_inc(sems["so"], 16)
                h.dma_start(out=dbg_g, in_=gated_all[:, 0:2048]).then_inc(sems["so"], 16)
                h.dma_start(out=dbg_l3, in_=ln3_all[0:D, 0:2048]).then_inc(sems["so"], 16)
            R.g(_gm)
            INC("so", 80)
        W("g", "sg", C["sg"])
        if C["so"]:
            W("g", "so", C["so"])
        R.g(lambda h: h.nop())

        with nc.Block() as blk:
            @blk.sync
            def _(h):
                for fn in R.ops["s"]:
                    fn(h)

            @blk.scalar
            def _(h):
                for fn in R.ops["a"]:
                    fn(h)

            @blk.vector
            def _(h):
                for fn in R.ops["v"]:
                    fn(h)

            @blk.tensor
            def _(h):
                for fn in R.ops["t"]:
                    fn(h)

            @blk.gpsimd
            def _(h):
                for fn in R.ops["g"]:
                    fn(h)

    return nc


_NC_CACHE = None


def _get_nc():
    global _NC_CACHE
    if _NC_CACHE is None:
        _NC_CACHE = build_nc()
    return _NC_CACHE


def _prep_inmaps(inputs):
    f32 = lambda a: np.ascontiguousarray(np.asarray(a, dtype=np.float32))
    x = f32(inputs["L0_attn_a_x"])
    circ = {k: np.tile(f32(inputs[f"L0_attn_a_w{k}_c"]), CTILE // D) for k in "qkvo"}
    bias_cols = {k: np.ascontiguousarray(
        f32(inputs[f"L0_attn_a_w{k}_b"]).reshape(8, 128).T) for k in "qkvo"}
    wvb_row = f32(inputs["L0_attn_a_wv_b"]).reshape(1, D)
    w1T = np.ascontiguousarray(f32(inputs["L0_ffn_w1"]).T)
    w2T = np.ascontiguousarray(f32(inputs["L0_ffn_w2"]).T)
    b1c = np.ascontiguousarray(f32(inputs["L0_ffn_b1"]).reshape(HDIM // 128, 128).T)
    b2c = np.ascontiguousarray(f32(inputs["L0_ffn_b2"]).reshape(D // 128, 128).T)
    inw = f32(inputs["L0_attn_b_in_proj_w"])
    convw = f32(inputs["L0_attn_b_conv_w"])
    convb = f32(inputs["L0_attn_b_conv_b"])
    xpw = f32(inputs["L0_attn_b_x_proj_w"])
    dtw = f32(inputs["L0_attn_b_dt_proj_w"])[:, 0]
    dtb = f32(inputs["L0_attn_b_dt_proj_b"])
    alog = f32(inputs["L0_attn_b_A_log"])
    dpar = f32(inputs["L0_attn_b_D_param"])
    w_outT = np.ascontiguousarray(f32(inputs["L0_attn_b_out_proj_w"]).T)

    def cols2(v256):
        return np.ascontiguousarray(v256.reshape(2, 128).T)

    in_maps = []
    for c in range(NCORE):
        b, hh = c // 2, c % 2
        sl = slice(DSL * c, DSL * (c + 1))
        m = dict(
            xT_full=np.ascontiguousarray(np.concatenate(
                [x[b, TOK * hh:TOK * (hh + 1), :],
                 x[b, TOK * (1 - hh):TOK * (2 - hh), :]], axis=0).T),
            x_ownT=np.ascontiguousarray(x[b, TOK * hh:TOK * (hh + 1), :].T),
            w1T=w1T, w2T=w2T, b1_c=b1c, b2_c=b2c,
            w1N=f32(inputs["L0_ffn_w1"]), w2N=f32(inputs["L0_ffn_w2"]),
            wvb_row=wvb_row,
            w_inT_x=np.ascontiguousarray(inw[sl, :].T),
            w_inT_z=np.ascontiguousarray(inw[DI + DSL * c:DI + DSL * (c + 1), :].T),
            conv_w_c=np.ascontiguousarray(
                convw[sl].reshape(2, 128, 4).transpose(1, 0, 2)),
            conv_b_c=cols2(convb[sl]),
            wxT_sl=np.ascontiguousarray(xpw[:, sl].T),
            dtw_c=cols2(dtw[sl]),
            dtb_c=cols2(dtb[sl]),
            alog_c=np.ascontiguousarray(
                alog[sl].reshape(2, 128, NST).transpose(1, 0, 2)),
            dpar_c=cols2(dpar[sl]),
            w_outT=w_outT,
        )
        for k in "qkvo":
            m[f"c{k}2"] = circ[k]
            m[f"w{k}b_c"] = bias_cols[k]
        in_maps.append(m)
    return in_maps


def kernel(**inputs):
    nc = _get_nc()
    in_maps = _prep_inmaps(inputs)
    res = run_bass_kernel_spmd(nc, in_maps, core_ids=list(range(NCORE)))
    out = np.empty((4, S, D), dtype=np.float32)
    for c in range(NCORE):
        b, hh = c // 2, c % 2
        out[b, TOK * hh:TOK * (hh + 1), :] = res.results[c]["outT"].T
    return out


# version 07
# speedup vs baseline: 1.1939x; 1.0225x over previous
import numpy as np
import ml_dtypes
from contextlib import ExitStack

import concourse.bass as bass
import concourse.mybir as mybir
from concourse.bass_utils import run_bass_kernel_spmd

F32 = mybir.dt.float32
BF16 = mybir.dt.bfloat16
AF = mybir.ActivationFunctionType
ALU = mybir.AluOpType

D = 1024
S = 4096
TOK = 2048
H, HD = 16, 64
HDIM = 4096
DI = 2048
DSL = 256
NST = 16
T16 = 16384
NCORE = 8
CTILE = 132 * D

STAGE = 4

RG = [[0, 1, 2, 3, 4, 5, 6, 7]]


class Rec:
    def __init__(self):
        self.ops = {e: [] for e in "savtg"}

    def s(self, fn): self.ops["s"].append(fn)
    def a(self, fn): self.ops["a"].append(fn)
    def v(self, fn): self.ops["v"].append(fn)
    def t(self, fn): self.ops["t"].append(fn)
    def g(self, fn): self.ops["g"].append(fn)


def build_nc():
    nc = bass.Bass("TRN2", target_bir_lowering=False)

    def inp(name, shape, dt=F32):
        return nc.dram_tensor(name, shape, dt, kind="ExternalInput").ap()

    xT_full = inp("xT_full", [D, S])
    x_ownT = inp("x_ownT", [D, TOK])
    circ2 = {k: inp(f"c{k}2", [CTILE]) for k in "qkvo"}
    bias_in = {k: inp(f"w{k}b_c", [128, 8]) for k in "qkvo"}
    wvb_row_in = inp("wvb_row", [1, D])
    w1T_in = inp("w1T", [D, HDIM])
    w2T_in = inp("w2T", [HDIM, D])
    w1N_in = inp("w1N", [HDIM, D])
    w2N_in = inp("w2N", [D, HDIM])
    b1_c = inp("b1_c", [128, HDIM // 128])
    b2_c = inp("b2_c", [128, D // 128])
    w_inT_x = inp("w_inT_x", [D, DSL])
    w_inT_z = inp("w_inT_z", [D, DSL])
    conv_w_c = inp("conv_w_c", [128, 2, 4])
    conv_b_c = inp("conv_b_c", [128, 2])
    wxT_sl = inp("wxT_sl", [DSL, 33])
    dtw_c = inp("dtw_c", [128, 2])
    dtb_c = inp("dtb_c", [128, 2])
    alog_c = inp("alog_c", [128, 2, NST])
    dpar_c = inp("dpar_c", [128, 2])
    w_outT_in = inp("w_outT", [DI, D])

    outT = nc.dram_tensor("outT", [D, TOK], F32, kind="ExternalOutput").ap()
    DBG = False
    if DBG:
        dbg_ln1 = nc.dram_tensor("dbg_ln1", [D, S], BF16, kind="ExternalOutput").ap()
        dbg_om = nc.dram_tensor("dbg_om", [D, TOK], BF16, kind="ExternalOutput").ap()
        dbg_kT = nc.dram_tensor("dbg_kT", [128, S], BF16, kind="ExternalOutput").ap()
        dbg_qT = nc.dram_tensor("dbg_qT", [128, TOK], BF16, kind="ExternalOutput").ap()
        dbg_vtm = nc.dram_tensor("dbg_vtm", [128, 32 * 130], BF16, kind="ExternalOutput").ap()
        dbg_neg = nc.dram_tensor("dbg_neg", [128, 2], F32, kind="ExternalOutput").ap()
        dbg_sc = nc.dram_tensor("dbg_sc", [2, 2 * NST], F32, kind="ExternalOutput").ap()
        dbg_m = nc.dram_tensor("dbg_m", [128, 2], BF16, kind="ExternalOutput").ap()
        dbg_al = nc.dram_tensor("dbg_al", [128, 48], F32, kind="ExternalOutput").ap()
        dbg_ln2 = nc.dram_tensor("dbg_ln2", [128, 8, 512], BF16, kind="ExternalOutput").ap()
        dbg_h = nc.dram_tensor("dbg_h", [128, 4, 512], BF16, kind="ExternalOutput").ap()
        dbg_xb = nc.dram_tensor("dbg_xb", [128, 8, 512], F32, kind="ExternalOutput").ap()
        dbg_u = nc.dram_tensor("dbg_u", [DSL, 2048], F32, kind="ExternalOutput").ap()
        dbg_z = nc.dram_tensor("dbg_z", [DSL, 2048], F32, kind="ExternalOutput").ap()
        dbg_bcd = nc.dram_tensor("dbg_bcd", [33, 2048], F32, kind="ExternalOutput").ap()
        dbg_g = nc.dram_tensor("dbg_g", [DI, 2048], BF16, kind="ExternalOutput").ap()
        dbg_l3 = nc.dram_tensor("dbg_l3", [D, 2048], BF16, kind="ExternalOutput").ap()
        dbg_gr = nc.dram_tensor("dbg_gr", [128, 16, 512], BF16, kind="ExternalOutput").ap()
        dbg_wo = nc.dram_tensor("dbg_wo", [128, 16, 1024], BF16, kind="ExternalOutput").ap()
        dbg_wo2 = nc.dram_tensor("dbg_wo2", [128, 16, 1024], BF16, kind="ExternalOutput").ap()
        dbg_sq = nc.dram_tensor("dbg_sq", [128, 4096], BF16, kind="ExternalOutput").ap()

    def scratch(name, shape, dt=F32):
        return nc.dram_tensor(name, shape, dt, kind="Internal").ap()

    x1_d = scratch("x1_d", [D, TOK])
    x2_d = scratch("x2_d", [D, TOK])
    x3_d = scratch("x3_d", [D, TOK])
    s1T_d = scratch("s1T_d", [D, HDIM], BF16)
    s2T_d = scratch("s2T_d", [HDIM, D], BF16)
    ln3_loc = scratch("ln3_loc", [D, TOK], BF16)
    ln3_all = scratch("ln3_all", [NCORE * D, TOK], BF16)
    bcd_part = scratch("bcd_part", [33, T16])
    bcd_all = scratch("bcd_all", [33, T16])
    u_d = scratch("u_d", [DSL, T16])
    zs_d = scratch("zs_d", [DSL, T16])
    gated_loc = scratch("gated_loc", [DSL, T16], BF16)
    gated_all = scratch("gated_all", [DI, T16], BF16)

    es = ExitStack()
    with es:
        sbt = lambda name, shape, dt: es.enter_context(nc.sbuf_tensor(name, shape, dt))

        bias_q = sbt("bias_q", [128, 8], F32)
        bias_k = sbt("bias_k", [128, 8], F32)
        bias_v = sbt("bias_v", [128, 8], F32)
        bias_o = sbt("bias_o", [128, 8], F32)
        vbrow = sbt("vbrow", [1, D], BF16)
        ones_col = sbt("ones_col", [128, 1], BF16)
        ones_row = sbt("ones_row", [1, 128], BF16)
        ones_rowf = sbt("ones_rowf", [1, 128], F32)
        onesq = sbt("onesq", [128, 128], F32)
        onesq_b = sbt("onesq_b", [128, 128], BF16)
        maskAB = sbt("maskAB", [128, 2], BF16)
        iota_col = sbt("iota_col", [128, 1], mybir.dt.int32)
        negMrow = sbt("negMrow", [1, 2], F32)
        negMb = sbt("negMb", [128, 2], F32)
        nsc = sbt("nsc", [1, 8], F32)
        carry = sbt("carry", [128, 2, NST], F32)
        A_sb = sbt("A_sb", [128, 2, NST], F32)
        alpha1 = sbt("alpha1", [128, HDIM // 128], F32)
        alpha2 = sbt("alpha2", [128, D // 128], F32)
        b1cols = sbt("b1cols", [128, HDIM // 128], F32)
        b2cols = sbt("b2cols", [128, D // 128], F32)
        conv_sb = sbt("conv_sb", [128, 2, 4], F32)
        convb_sb = sbt("convb_sb", [128, 2], F32)
        dtw_sb = sbt("dtw_sb", [128, 2], F32)
        dtb_sb = sbt("dtb_sb", [128, 2], F32)
        dpar_sb = sbt("dpar_sb", [128, 2], F32)
        wioT = sbt("wioT", [128, 8, 512], BF16)
        wxTb = sbt("wxTb", [128, 2, 33], BF16)

        arA = sbt("arA", [128, 16384], F32)
        arB = sbt("arB", [128, 16384], F32)
        arC = sbt("arC", [128, 8192], F32)
        arD = sbt("arD", [128, 8192], F32)

        ps = [es.enter_context(nc.psum_tensor(f"ps{i}", [128, 1024], F32))
              for i in range(4)]

        sems = {}
        for name in ("sd", "sg", "ssc", "pe", "va", "vd", "vg", "vc", "so"):
            sems[name] = es.enter_context(nc.semaphore(name))
        C = {k: 0 for k in sems}

        R = Rec()

        def W(eng, name, val=None):
            v = C[name] if val is None else val
            sem = sems[name]
            getattr(R, eng)(lambda h: h.wait_ge(sem, v))

        def INC(name, n=1):
            C[name] += n
            return C[name]

        arA_b = arA.ap().bitcast(BF16)
        arB_b = arB.ap().bitcast(BF16)
        arC_b = arC.ap().bitcast(BF16)
        arD_b = arD.ap().bitcast(BF16)
        arD_f = arD.ap()

        ln1 = arA_b.rearrange("p (j t) -> p j t", j=8)
        CT = {k: arB_b[:, 8192 * i:8192 * (i + 1)].rearrange("p (j i) -> p j i", j=8)
              for i, k in enumerate("qkvo")}
        om = arC_b.rearrange("p (j t) -> p j t", j=8)
        stage = arC.ap().rearrange("p (r j t) -> p r j t", r=2, j=8)

        ln_xb = arD_b[:, 0:4096].rearrange("p (j t) -> p j t", j=8)
        ln_x2 = arD_b[:, 4096:8192].rearrange("p (j t) -> p j t", j=8)
        lns_ln = arD_f[:, 4096:6144].rearrange("p (k t) -> p k t", k=4)
        tmp_ln = arD_f[:, 6144:6656]

        kTv = arD_b[:, 0:4096]
        vtm = arD_b[:, 4096:8256].rearrange("p (tt c) -> p tt c", c=130)
        qTv = arD_b[:, 8256:10304]
        pT = arD_b[:, 10304:14400].rearrange("p (r t) -> p r t", r=4)
        rb64 = arD_f[:, 5152:6176]
        lns_at = arD_f[:, 6176:7200].rearrange("p (k t) -> p k t", k=2)
        lns_at2 = arD_f[:, 5152:6176].rearrange("p (k t) -> p k t", k=2)
        x0c = arD_f[:, 0:512]
        x1c = arD_f[:, 512:1024]
        tmpo = arD_f[:, 1024:1536]

        def p0():
            def _s(h):
                for t, src in ((bias_q, bias_in["q"]), (bias_k, bias_in["k"]),
                               (bias_v, bias_in["v"]), (bias_o, bias_in["o"]),
                               (arD_f[0:1, 0:1024], wvb_row_in), (b1cols, b1_c), (b2cols, b2_c),
                               (conv_sb, conv_w_c), (convb_sb, conv_b_c),
                               (dtw_sb, dtw_c), (dtb_sb, dtb_c), (dpar_sb, dpar_c),
                               (A_sb, alog_c)):
                    h.dma_start(out=t[:], in_=src).then_inc(sems["sd"], 16)
            R.s(_s)
            INC("sd", 16 * 13)

            R.g(lambda h: h.iota(iota_col[:], pattern=[[0, 1]], base=0,
                                 channel_multiplier=1).then_inc(sems["vg"], 1))
            INC("vg")
            W("v", "sd")
            W("v", "vg")

            def _v(h):
                h.memset(ones_col[:], 1.0)
                h.memset(ones_row[:], 1.0)
                h.memset(ones_rowf[:], 1.0)
                h.memset(onesq[:], 1.0)
                h.memset(onesq_b[:], 1.0)
                h.memset(nsc[:], 0.0)
                h.tensor_single_scalar(out=maskAB[:, 0:1], in_=iota_col[:], scalar=64, op=ALU.is_lt)
                h.tensor_single_scalar(out=maskAB[:, 1:2], in_=iota_col[:], scalar=64, op=ALU.is_ge)
                h.tensor_copy(out=vbrow[:], in_=arD_f[0:1, 0:1024]).then_inc(sems["vd"], 1)
            R.v(_v)
            INC("vd")
            W("a", "sd")
            R.a(lambda h: h.activation(out=A_sb[:], in_=A_sb.ap() if False else A_sb[:],
                                       func=AF.Exp).then_inc(sems["va"], 1))
            INC("va")
            W("v", "va")
            R.v(lambda h: h.tensor_single_scalar(out=A_sb[:], in_=A_sb[:],
                                                 scalar=-1.0, op=ALU.mult)
                .then_inc(sems["vd"], 1))
            INC("vd")

        def ln_rec(src_dram, dst_store, ntok, keep_store=None):
            nchunks = ntok // 512
            srcR = src_dram.rearrange("(j p) t -> p j t", p=128)
            for c in range(nchunks):
                st = stage[:, c % 2]
                W("s", "vd")
                R.s(lambda h, st=st, c=c: h.dma_start(
                    out=st[:], in_=srcR[:, :, 512 * c:512 * (c + 1)])
                    .then_inc(sems["sd"], 16))
                INC("sd", 16)
                W("a", "sd")
                W("a", "pe", C["pe"])

                def _a(h, st=st):
                    for jc in range(8):
                        h.activation(out=ln_xb[:, jc, :], in_=st[:, jc, :], func=AF.Identity)
                    for jc in range(8):
                        r = h.activation(out=ln_x2[:, jc, :], in_=st[:, jc, :], func=AF.Square)
                    r.then_inc(sems["va"], 1)
                R.a(_a)
                INC("va")
                W("t", "va")
                W("t", "vd", C["vd"])

                def _t(h):
                    for jc in range(8):
                        h.matmul(ps[3][0:1, 0:512], ones_col[:], ln_xb[:, jc, :],
                                 start=(jc == 0), stop=(jc == 7))
                    for jc in range(8):
                        r = h.matmul(ps[3][32:33, 512:1024], ones_col[:], ln_x2[:, jc, :],
                                     start=(jc == 0), stop=(jc == 7))
                    r.then_inc(sems["pe"], 1)
                R.t(_t)
                INC("pe")
                W("v", "pe")

                def _v1(h):
                    h.tensor_single_scalar(out=lns_ln[0:1, 0, :], in_=ps[3][0:1, 0:512],
                                           scalar=1.0 / D, op=ALU.mult)
                    h.tensor_single_scalar(out=lns_ln[0:1, 1, :], in_=ps[3][32:33, 512:1024],
                                           scalar=1.0 / D, op=ALU.mult)
                    h.tensor_mul(out=lns_ln[0:1, 2, :], in0=lns_ln[0:1, 0, :],
                                 in1=lns_ln[0:1, 0, :])
                    h.tensor_sub(out=lns_ln[0:1, 3, :], in0=lns_ln[0:1, 1, :],
                                 in1=lns_ln[0:1, 2, :])
                    r = h.tensor_single_scalar(out=lns_ln[0:1, 3, :], in_=lns_ln[0:1, 3, :],
                                               scalar=1e-5, op=ALU.add)
                    r.then_inc(sems["vd"], 1)
                R.v(_v1)
                INC("vd")
                W("a", "vd")
                R.a(lambda h: h.activation(out=lns_ln[0:1, 3, :], in_=lns_ln[0:1, 3, :],
                                           func=AF.Sqrt).then_inc(sems["va"], 1))
                INC("va")
                W("v", "va")
                R.v(lambda h: h.reciprocal(out=lns_ln[0:1, 1, :], in_=lns_ln[0:1, 3, :])
                    .then_inc(sems["vd"], 1))
                INC("vd")
                W("t", "vd")

                def _t2(h):
                    h.matmul(ps[2][:, 0:512], ones_rowf[:], lns_ln[0:1, 0, :],
                             start=True, stop=True)
                    h.matmul(ps[2][:, 512:1024], ones_rowf[:], lns_ln[0:1, 1, :],
                             start=True, stop=True).then_inc(sems["pe"], 1)
                R.t(_t2)
                INC("pe")
                W("v", "pe")

                def _v2(h, st=st, c=c):
                    for jc in range(8):
                        h.tensor_sub(out=tmp_ln[:], in0=st[:, jc, :], in1=ps[2][:, 0:512])
                        r = h.tensor_mul(out=dst_store(jc, c), in0=tmp_ln[:],
                                         in1=ps[2][:, 512:1024])
                        if keep_store is not None:
                            r = h.tensor_copy(out=keep_store(jc, c), in_=st[:, jc, :])
                    r.then_inc(sems["vd"], 1)
                R.v(_v2)
                INC("vd")

        def p1():
            ln_rec(xT_full, lambda jc, c: ln1[:, jc, 512 * c:512 * (c + 1)], S)

        def p2():
            for k in "qkvo":
                for jc in range(8):
                    st2 = stage[:, jc % 2, 0:2]
                    W("s", "va")
                    W("s", "vd")
                    off = 2 * D - 128 * jc
                    R.s(lambda h, off=off, k=k, st2=st2: h.dma_start(
                        out=st2[:],
                        in_=bass.AP(circ2[k].tensor, off, [[D - 1, 128], [1, D]]))
                        .then_inc(sems["sd"], 16))
                    INC("sd", 16)
                    W("a", "sd")
                    R.a(lambda h, k=k, jc=jc, st2=st2: h.activation(
                        out=CT[k][:, jc, :], in_=st2.rearrange("p a t -> p (a t)"),
                        func=AF.Identity).then_inc(sems["va"], 1))
                    INC("va")

        def p3():
            dP = 16384
            qT_h = qTv.tensor
            QOFF0 = qTv.offset

            prj = [ps[2][:, 0:512], ps[2][:, 512:1024],
                   ps[3][:, 0:512], ps[3][:, 512:1024]]
            prj_free = [C["va"]] * 4

            W("v", "vd", C["vd"])
            R.v(lambda h: h.memset(
                bass.AP(vtm.tensor, vtm.offset + 64, [[dP, 128], [130, 32], [65, 2]]),
                1.0).then_inc(sems["vd"], 1))
            INC("vd")
            vtm_ones_vd = C["vd"]

            for hp in range(8):
                for f in range(8):
                    slot = f % 4
                    W("t", "va", prj_free[slot])

                    def _t(h, hp=hp, f=f, slot=slot):
                        for jc in range(8):
                            r = h.matmul(prj[slot],
                                         CT["k"][:, jc, 128 * hp:128 * (hp + 1)],
                                         ln1[:, jc, 512 * f:512 * (f + 1)],
                                         start=(jc == 0), stop=(jc == 7))
                        r.then_inc(sems["pe"], 1)
                    R.t(_t)
                    INC("pe")
                    W("a", "pe")
                    R.a(lambda h, f=f, slot=slot, hp=hp: h.activation(
                        out=kTv[:, 512 * f:512 * (f + 1)], in_=prj[slot],
                        func=AF.Identity, bias=bias_k[:, hp:hp + 1])
                        .then_inc(sems["va"], 1))
                    INC("va")
                    prj_free[slot] = C["va"]

                for tt in range(32):
                    slot = tt % 4
                    W("t", "va", prj_free[slot])

                    def _t(h, hp=hp, tt=tt, slot=slot):
                        for jc in range(8):
                            h.matmul(prj[slot][:, 0:128],
                                     ln1[:, jc, 128 * tt:128 * (tt + 1)],
                                     CT["v"][:, jc, 128 * hp:128 * (hp + 1)],
                                     start=(jc == 0), stop=False)
                        r = h.matmul(prj[slot][:, 0:128], ones_row[:],
                                     vbrow[0:1, 128 * hp:128 * (hp + 1)],
                                     start=False, stop=True)
                        r.then_inc(sems["pe"], 1)
                    R.t(_t)
                    INC("pe")
                    W("a", "pe")

                    def _a(h, tt=tt, slot=slot):
                        h.activation(out=vtm[:, tt, 0:64], in_=prj[slot][:, 0:64],
                                     func=AF.Identity)
                        r = h.activation(out=vtm[:, tt, 65:129], in_=prj[slot][:, 64:128],
                                         func=AF.Identity)
                        r.then_inc(sems["va"], 1)
                    R.a(_a)
                    INC("va")
                    prj_free[slot] = C["va"]

                for f in range(4):
                    slot = f % 4
                    W("t", "va", prj_free[slot])

                    def _t(h, hp=hp, f=f, slot=slot):
                        for jc in range(8):
                            r = h.matmul(prj[slot],
                                         CT["q"][:, jc, 128 * hp:128 * (hp + 1)],
                                         ln1[:, jc, 512 * f:512 * (f + 1)],
                                         start=(jc == 0), stop=(jc == 7))
                        r.then_inc(sems["pe"], 1)
                    R.t(_t)
                    INC("pe")
                    W("a", "pe")
                    R.a(lambda h, f=f, slot=slot, hp=hp: h.activation(
                        out=qTv[:, 512 * f:512 * (f + 1)], in_=prj[slot],
                        func=AF.Identity, bias=bias_q[:, hp:hp + 1])
                        .then_inc(sems["va"], 1))
                    INC("va")
                    prj_free[slot] = C["va"]

                sqscr = pT.rearrange("p r t -> p (r t)")
                qacc = carry[0:2, 0, 0:1]
                kacc = carry[0:2, 0, 1:2]
                ntmp = carry[0:2, 0, 2:3]
                negM2 = carry[0:2, 0, 3:4]
                qmaxs = carry[0:2, 1, 0:4]
                kmaxs = carry[0:2, 1, 4:12]
                for (srcv, nch, maxs, acc) in ((qTv, 4, qmaxs, qacc),
                                               (kTv, 8, kmaxs, kacc)):
                    W("a", "vd", C["vd"])
                    R.a(lambda h, srcv=srcv, nch=nch: h.activation(
                        out=sqscr[:, 0:512 * nch], in_=srcv, func=AF.Square)
                        .then_inc(sems["va"], 1))
                    INC("va")
                    W("t", "va")
                    for ch in range(nch):
                        pst = ps[0] if ch % 2 == 0 else ps[1]
                        W("t", "vd")
                        R.t(lambda h, ch=ch, pst=pst: h.matmul(
                            pst[0:2, 0:512], maskAB[:],
                            sqscr[:, 512 * ch:512 * (ch + 1)],
                            start=True, stop=True).then_inc(sems["pe"], 1))
                        INC("pe")
                        W("v", "pe")
                        R.v(lambda h, ch=ch, pst=pst, maxs=maxs: h.reduce_max(
                            out=maxs[:, ch:ch + 1], in_=pst[0:2, 0:512],
                            axis=mybir.AxisListType.X).then_inc(sems["vd"], 1))
                        INC("vd")
                    W("v", "vd")
                    R.v(lambda h, maxs=maxs, acc=acc, nch=nch: h.reduce_max(
                        out=acc, in_=maxs[:, 0:nch],
                        axis=mybir.AxisListType.X).then_inc(sems["vd"], 1))
                    INC("vd")
                W("v", "vd")
                R.v(lambda h: h.tensor_mul(out=ntmp, in0=qacc, in1=kacc)
                    .then_inc(sems["vd"], 1))
                INC("vd")
                W("a", "vd")
                R.a(lambda h: h.activation(out=ntmp, in_=ntmp, func=AF.Sqrt)
                    .then_inc(sems["va"], 1))
                INC("va")
                W("v", "va")
                R.v(lambda h: h.tensor_single_scalar(
                    out=negM2, in_=ntmp, scalar=-0.125 * 1.02, op=ALU.mult)
                    .then_inc(sems["vd"], 1))
                INC("vd")
                W("s", "vd")
                R.s(lambda h: h.dma_start(out=negMrow[0:1, 0:2], in_=negM2)
                    .then_inc(sems["sd"], 16))
                INC("sd", 16)
                W("t", "sd")
                R.t(lambda h: h.matmul(ps[0][:, 0:2], ones_rowf[:], negMrow[0:1, 0:2],
                                       start=True, stop=True).then_inc(sems["pe"], 1))
                INC("pe")
                W("a", "pe")
                R.a(lambda h: h.activation(out=negMb[:], in_=ps[0][:, 0:2],
                                           func=AF.Identity).then_inc(sems["va"], 1))
                INC("va")

                if DBG and hp == 7:
                    W("g", "va", C["va"])
                    W("g", "vd", C["vd"])
                    def _gd(h):
                        h.dma_start(out=dbg_kT, in_=kTv).then_inc(sems["so"], 16)
                        h.dma_start(out=dbg_qT, in_=qTv).then_inc(sems["so"], 16)
                        h.dma_start(out=dbg_vtm, in_=vtm.rearrange("p a b -> p (a b)")).then_inc(sems["so"], 16)
                        h.dma_start(out=dbg_neg, in_=negMb[:]).then_inc(sems["so"], 16)
                        h.dma_start(out=dbg_sc, in_=carry[0:2, :, :].rearrange("p a b -> p (a b)")).then_inc(sems["so"], 16)
                        h.dma_start(out=dbg_m, in_=maskAB[:]).then_inc(sems["so"], 16)
                        h.dma_start(out=dbg_sq, in_=pT.rearrange("p r t -> p (r t)")).then_inc(sems["so"], 16)
                    R.g(_gd)
                    INC("so", 112)
                    W("t", "so", C["so"])
                W("t", "vd", vtm_ones_vd)
                for qc in range(2):
                    for kc in range(32):
                        W("t", "va")

                        def _t(h, qc=qc, kc=kc):
                            for fh in range(2):
                                for head in (0, 1):
                                    pst = ps[0] if head == 0 else ps[1]
                                    rhs = bass.AP(
                                        qT_h,
                                        64 * head * dP + QOFF0 + 1024 * qc + 512 * fh,
                                        [[dP, 64], [1, 512]])
                                    r = h.matmul(
                                        pst[:, 512 * fh:512 * (fh + 1)],
                                        kTv[64 * head:64 * (head + 1),
                                            128 * kc:128 * (kc + 1)],
                                        rhs, start=True, stop=True,
                                        tile_position=(64 * head, 0))
                            r.then_inc(sems["pe"], 1)
                        R.t(_t)
                        INC("pe")
                        W("a", "pe")

                        def _a(h, kc=kc):
                            h.activation(out=pT[:, 2 * (kc % 2) + 0, :], in_=ps[0][:],
                                         func=AF.Exp, scale=0.125, bias=negMb[:, 0:1])
                            r = h.activation(out=pT[:, 2 * (kc % 2) + 1, :], in_=ps[1][:],
                                             func=AF.Exp, scale=0.125, bias=negMb[:, 1:2])
                            r.then_inc(sems["va"], 1)
                        R.a(_a)
                        INC("va")
                        W("t", "va")

                        def _t2(h, kc=kc):
                            for head in (0, 1):
                                pst = ps[2] if head == 0 else ps[3]
                                for fh in range(2):
                                    r = h.matmul(
                                        pst[0:65, 512 * fh:512 * (fh + 1)],
                                        vtm[:, kc, 65 * head:65 * head + 65],
                                        pT[:, 2 * (kc % 2) + head,
                                           512 * fh:512 * (fh + 1)],
                                        start=(kc == 0), stop=(kc == 31))
                            r.then_inc(sems["pe"], 1)
                        R.t(_t2)
                        INC("pe")

                    W("v", "pe")

                    def _v(h):
                        h.reciprocal(out=lns_at[0:1, 0, :], in_=ps[2][64:65, 0:512])
                        h.reciprocal(out=lns_at[0:1, 1, :], in_=ps[2][64:65, 512:1024])
                        h.reciprocal(out=lns_at2[0:1, 0, :], in_=ps[3][64:65, 0:512])
                        r = h.reciprocal(out=lns_at2[0:1, 1, :], in_=ps[3][64:65, 512:1024])
                        r.then_inc(sems["vd"], 1)
                    R.v(_v)
                    INC("vd")
                    W("t", "vd")

                    def _t3(h):
                        h.matmul(ps[0][0:64, 0:512], ones_rowf[0:1, 0:64],
                                 lns_at[0:1, 0, :], start=True, stop=True)
                        h.matmul(ps[0][0:64, 512:1024], ones_rowf[0:1, 0:64],
                                 lns_at[0:1, 1, :], start=True, stop=True)
                        h.matmul(ps[1][0:64, 0:512], ones_rowf[0:1, 0:64],
                                 lns_at2[0:1, 0, :], start=True, stop=True)
                        r = h.matmul(ps[1][0:64, 512:1024], ones_rowf[0:1, 0:64],
                                     lns_at2[0:1, 1, :], start=True, stop=True)
                        r.then_inc(sems["pe"], 1)
                    R.t(_t3)
                    INC("pe")
                    W("a", "pe")

                    def _a2(h):
                        h.activation(out=rb64[0:64, :], in_=ps[0][0:64, :], func=AF.Identity)
                        r = h.activation(out=rb64[64:128, :], in_=ps[1][0:64, :],
                                         func=AF.Identity)
                        r.then_inc(sems["va"], 1)
                    R.a(_a2)
                    INC("va")
                    W("v", "va")

                    def _v2(h, hp=hp, qc=qc):
                        h.tensor_mul(out=om[0:64, hp, 1024 * qc:1024 * (qc + 1)],
                                     in0=ps[2][0:64, :], in1=rb64[0:64, :])
                        r = h.tensor_mul(out=om[64:128, hp, 1024 * qc:1024 * (qc + 1)],
                                         in0=ps[3][0:64, :], in1=rb64[64:128, :])
                        r.then_inc(sems["vd"], 1)
                    R.v(_v2)
                    INC("vd")
                    W("t", "vd")

            if DBG:
                W("g", "vd", C["vd"])
                def _gd2(h):
                    h.dma_start(out=dbg_om.rearrange("(j p) t -> p j t", p=128),
                                in_=om[:]).then_inc(sems["so"], 16)
                R.g(_gd2)
                INC("so", 16)
            W("t", "vd")
            x_ownR = x_ownT.rearrange("(j p) t -> p j t", p=128)
            x1R = x1_d.rearrange("(j p) t -> p j t", p=128)
            outR = outT.rearrange("(j p) t -> p j t", p=128)
            for mt in range(8):
                for f in range(4):
                    pst = ps[0] if (mt * 4 + f) % 2 == 0 else ps[1]
                    W("s", "vd", C["vd"])
                    R.s(lambda h, mt=mt, f=f: h.dma_start(
                        out=x0c[:], in_=x_ownR[:, mt, 512 * f:512 * (f + 1)])
                        .then_inc(sems["sd"], 16))
                    INC("sd", 16)
                    W("t", "va", C["va"])

                    def _t(h, mt=mt, f=f, pst=pst):
                        for jc in range(8):
                            r = h.matmul(pst[:, 0:512],
                                         CT["o"][:, jc, 128 * mt:128 * (mt + 1)],
                                         om[:, jc, 512 * f:512 * (f + 1)],
                                         start=(jc == 0), stop=(jc == 7))
                        r.then_inc(sems["pe"], 1)
                    R.t(_t)
                    INC("pe")
                    W("a", "pe")
                    W("a", "vd", C["vd"])
                    R.a(lambda h, mt=mt, pst=pst: h.activation(
                        out=tmpo[:], in_=pst[:, 0:512], func=AF.Identity,
                        bias=bias_o[:, mt:mt + 1]).then_inc(sems["va"], 1))
                    INC("va")
                    W("v", "va")
                    W("v", "sd")
                    R.v(lambda h: h.tensor_add(out=x1c[:], in0=x0c[:], in1=tmpo[:])
                        .then_inc(sems["vd"], 1))
                    INC("vd")
                    W("g", "vd")

                    def _g(h, mt=mt, f=f):
                        r = h.dma_start(out=x1R[:, mt, 512 * f:512 * (f + 1)], in_=x1c[:])
                        r.then_inc(sems["sg"], 16)
                        if STAGE == 1:
                            h.dma_start(out=outR[:, mt, 512 * f:512 * (f + 1)],
                                        in_=x1c[:]).then_inc(sems["so"], 16)
                    R.g(_g)
                    INC("sg", 16)
                    if STAGE == 1:
                        INC("so", 16)
                    W("v", "sg", C["sg"])

        s1T = arA_b.rearrange("p (j t) -> p j t", j=8)
        s2T = arB_b.rearrange("p (j t) -> p j t", j=32)
        w1R = w1T_in.rearrange("(j p) t -> p j t", p=128)
        w2R = w2T_in.rearrange("(j p) t -> p j t", p=128)
        arC_f = arC.ap()

        def p4():
            for jc in range(8):
                stg = arC_f[:, 4096 * (jc % 2):4096 * (jc % 2) + 4096]
                W("s", "va", C["va"])
                R.s(lambda h, jc=jc, stg=stg: h.dma_start(
                    out=stg, in_=w1R[:, jc, :]).then_inc(sems["sd"], 16))
                INC("sd", 16)
                W("a", "sd")
                R.a(lambda h, jc=jc, stg=stg: h.activation(
                    out=s1T[:, jc, :], in_=stg, func=AF.Sign).then_inc(sems["va"], 1))
                INC("va")
            for hc in range(32):
                stg = arC_f[:, 4096 * (hc % 2):4096 * (hc % 2) + 1024]
                W("s", "va", C["va"])
                R.s(lambda h, hc=hc, stg=stg: h.dma_start(
                    out=stg, in_=w2R[:, hc, :]).then_inc(sems["sd"], 16))
                INC("sd", 16)
                W("a", "sd")
                R.a(lambda h, hc=hc, stg=stg: h.activation(
                    out=s2T[:, hc, :], in_=stg, func=AF.Sign).then_inc(sems["va"], 1))
                INC("va")
            w1NR = w1N_in.rearrange("(m p) t -> p m t", p=128)
            w2NR = w2N_in.rearrange("(m p) t -> p m t", p=128)
            for (wR, ntile, alout, dd) in ((w1NR, 32, alpha1, D), (w2NR, 8, alpha2, HDIM)):
                for Mt in range(ntile):
                    stg = arC_f[:, dd * (Mt % 2):dd * (Mt % 2) + dd]
                    W("s", "vd", C["vd"])
                    W("s", "va", C["va"])
                    R.s(lambda h, wR=wR, Mt=Mt, stg=stg: h.dma_start(
                        out=stg, in_=wR[:, Mt, :]).then_inc(sems["sd"], 16))
                    INC("sd", 16)
                    W("a", "sd")
                    W("a", "vd", C["vd"])
                    absdst = arD_f[:, 0:dd]
                    R.a(lambda h, stg=stg, absdst=absdst, dd=dd: h.activation(
                        out=absdst, in_=stg, func=AF.Abs, scale=1.0 / dd)
                        .then_inc(sems["va"], 1))
                    INC("va")
                    W("v", "va")

                    def _v(h, absdst=absdst, alout=alout, Mt=Mt):
                        r = h.tensor_reduce(out=alout[:, Mt:Mt + 1], in_=absdst,
                                            axis=mybir.AxisListType.X, op=ALU.add)
                        r.then_inc(sems["vd"], 1)
                    R.v(_v)
                    INC("vd")
            if DBG:
                W("g", "vd", C["vd"])
                R.g(lambda h: h.dma_start(out=dbg_al[:, 0:32], in_=alpha1[:])
                    .then_inc(sems["so"], 16))
                INC("so", 16)
                W("v", "so", C["so"])
            W("g", "va", C["va"])
            s1R = s1T_d.rearrange("(j p) t -> p j t", p=128)
            s2R = s2T_d.rearrange("(j p) t -> p j t", p=128)

            def _g(h):
                h.dma_start(out=s1R[:, :, :], in_=s1T[:]).then_inc(sems["sg"], 16)
                h.dma_start(out=s2R[:, :, :], in_=s2T[:]).then_inc(sems["sg"], 16)
            R.g(_g)
            INC("sg", 32)

        xbuf = arD_f[:, 0:4096].rearrange("p (j t) -> p j t", j=8)
        lnsF = arD_f[:, 4096:5120].rearrange("p (k t) -> p k t", k=2)
        tmpo2 = arD_f[:, 5120:5632]
        ln2c = arD_b[:, 11264:15360].rearrange("p (j t) -> p j t", j=8)
        castb = arD_b[:, 15360:15872]
        sqb = arD_b[:, 15872:16384]

        def ffn_chunks(xsrc_d, xdst_d, also_out):
            xsR = xsrc_d.rearrange("(j p) t -> p j t", p=128)
            xdR = xdst_d.rearrange("(j p) t -> p j t", p=128)
            outR2 = outT.rearrange("(j p) t -> p j t", p=128)
            W("s", "sg", C["sg"])
            for c in range(4):
                W("s", "vd", C["vd"])
                W("s", "sg", C["sg"])
                R.s(lambda h, c=c: h.dma_start(
                    out=xbuf[:], in_=xsR[:, :, 512 * c:512 * (c + 1)])
                    .then_inc(sems["sd"], 16))
                INC("sd", 16)
                W("a", "sd")
                for jc in range(8):
                    if jc > 0:
                        W("a", "pe", C["pe"])

                    def _a(h, jc=jc):
                        h.activation(out=castb, in_=xbuf[:, jc, :], func=AF.Identity)
                        r = h.activation(out=sqb, in_=xbuf[:, jc, :], func=AF.Square)
                        r.then_inc(sems["va"], 1)
                    R.a(_a)
                    INC("va")
                    W("t", "va")

                    def _t(h, jc=jc):
                        h.matmul(ps[3][0:1, 0:512], ones_col[:], castb,
                                 start=(jc == 0), stop=(jc == 7))
                        r = h.matmul(ps[3][0:1, 512:1024], ones_col[:], sqb,
                                     start=(jc == 0), stop=(jc == 7))
                        r.then_inc(sems["pe"], 1)
                    R.t(_t)
                    INC("pe")
                W("v", "pe")

                def _v1(h):
                    h.tensor_single_scalar(out=lnsF[0:1, 0, :], in_=ps[3][0:1, 0:512],
                                           scalar=1.0 / D, op=ALU.mult)
                    h.tensor_single_scalar(out=tmpo2[0:1, :], in_=ps[3][0:1, 512:1024],
                                           scalar=1.0 / D, op=ALU.mult)
                    h.tensor_mul(out=castb[0:1, :], in0=lnsF[0:1, 0, :],
                                 in1=lnsF[0:1, 0, :])
                    h.tensor_sub(out=tmpo2[0:1, :], in0=tmpo2[0:1, :],
                                 in1=castb[0:1, :])
                    r = h.tensor_single_scalar(out=tmpo2[0:1, :],
                                               in_=tmpo2[0:1, :],
                                               scalar=1e-5, op=ALU.add)
                    r.then_inc(sems["vd"], 1)
                R.v(_v1)
                INC("vd")
                W("a", "vd")
                R.a(lambda h: h.activation(out=tmpo2[0:1, :],
                                           in_=tmpo2[0:1, :],
                                           func=AF.Sqrt).then_inc(sems["va"], 1))
                INC("va")
                W("v", "va")
                R.v(lambda h: h.reciprocal(out=lnsF[0:1, 1, :], in_=tmpo2[0:1, :])
                    .then_inc(sems["vd"], 1))
                INC("vd")
                W("t", "vd")

                def _t2(h):
                    h.matmul(ps[2][:, 0:512], ones_rowf[:], lnsF[0:1, 0, :],
                             start=True, stop=True)
                    r = h.matmul(ps[2][:, 512:1024], ones_rowf[:], lnsF[0:1, 1, :],
                                 start=True, stop=True)
                    r.then_inc(sems["pe"], 1)
                R.t(_t2)
                INC("pe")
                W("v", "pe")

                def _v2(h):
                    for jc in range(8):
                        h.tensor_sub(out=castb, in0=xbuf[:, jc, :], in1=ps[2][:, 0:512])
                        r = h.tensor_mul(out=ln2c[:, jc, :], in0=castb,
                                         in1=ps[2][:, 512:1024])
                    r.then_inc(sems["vd"], 1)
                R.v(_v2)
                INC("vd")
                if DBG and also_out and c == 0:
                    W("g", "vd", C["vd"])
                    def _gdf(h):
                        h.dma_start(out=dbg_ln2, in_=ln2c[:]).then_inc(sems["so"], 16)
                        h.dma_start(out=dbg_xb, in_=xbuf[:]).then_inc(sems["so"], 16)
                        al = dbg_al
                        h.dma_start(out=al[:, 32:40], in_=alpha2[:]).then_inc(sems["so"], 16)
                        h.dma_start(out=al[:, 40:44], in_=b2cols[:, 0:4]).then_inc(sems["so"], 16)
                        h.dma_start(out=al[:, 44:48], in_=bias_q[:, 0:4]).then_inc(sems["so"], 16)
                    R.g(_gdf)
                    INC("so", 80)
                    W("v", "so", C["so"])
                h_bf = arC_b.rearrange("p (m t) -> p m t", m=32)
                W("t", "vd")
                va_g = {}
                va_mm1_0 = C["va"]
                for Mt in range(32):
                    pst = ps[0] if Mt % 2 == 0 else ps[1]
                    W("t", "va", va_g.get(Mt - 2, va_mm1_0))

                    def _t3(h, Mt=Mt, pst=pst):
                        for jc in range(8):
                            r = h.matmul(pst[:, 0:512], s1T[:, jc, 128 * Mt:128 * (Mt + 1)],
                                         ln2c[:, jc, :], start=(jc == 0), stop=(jc == 7))
                        r.then_inc(sems["pe"], 1)
                    R.t(_t3)
                    INC("pe")
                    W("a", "pe")
                    R.a(lambda h, Mt=Mt, pst=pst: h.activation(
                        out=h_bf[:, Mt, :], in_=pst[:, 0:512], func=AF.Gelu,
                        scale=alpha1[:, Mt:Mt + 1], bias=b1cols[:, Mt:Mt + 1])
                        .then_inc(sems["va"], 1))
                    INC("va")
                    va_g[Mt] = C["va"]
                if DBG and also_out and c == 0:
                    W("g", "va", C["va"])
                    R.g(lambda h: h.dma_start(out=dbg_h, in_=h_bf[:, 0:4, :])
                        .then_inc(sems["so"], 16))
                    INC("so", 16)
                    W("t", "so", C["so"])
                tmpoB = lnsF[:, 0, :]
                va_e = {}
                vd_a = {}
                va_mm2_0 = C["va"]
                vd_mm2_0 = C["vd"]
                for Mt in range(8):
                    pst = ps[0] if Mt % 2 == 0 else ps[1]
                    tob = tmpo2 if Mt % 2 == 0 else tmpoB
                    W("t", "va", va_e.get(Mt - 2, va_mm2_0))

                    def _t4(h, Mt=Mt, pst=pst):
                        for hc in range(32):
                            r = h.matmul(pst[:, 0:512], s2T[:, hc, 128 * Mt:128 * (Mt + 1)],
                                         h_bf[:, hc, :], start=(hc == 0), stop=(hc == 31))
                        r.then_inc(sems["pe"], 1)
                    R.t(_t4)
                    INC("pe")
                    W("a", "pe")
                    W("a", "vd", vd_a.get(Mt - 2, vd_mm2_0))
                    R.a(lambda h, Mt=Mt, pst=pst, tob=tob: h.activation(
                        out=tob, in_=pst[:, 0:512], func=AF.Identity,
                        scale=alpha2[:, Mt:Mt + 1], bias=b2cols[:, Mt:Mt + 1])
                        .then_inc(sems["va"], 1))
                    INC("va")
                    va_e[Mt] = C["va"]
                    W("v", "va")
                    R.v(lambda h, Mt=Mt, tob=tob: h.tensor_add(
                        out=xbuf[:, Mt, :], in0=xbuf[:, Mt, :], in1=tob)
                        .then_inc(sems["vd"], 1))
                    INC("vd")
                    vd_a[Mt] = C["vd"]
                    W("g", "vd")

                    def _g(h, Mt=Mt, c=c):
                        r = h.dma_start(out=xdR[:, Mt, 512 * c:512 * (c + 1)],
                                        in_=xbuf[:, Mt, :])
                        r.then_inc(sems["sg"], 16)
                        if also_out:
                            h.dma_start(out=outR2[:, Mt, 512 * c:512 * (c + 1)],
                                        in_=xbuf[:, Mt, :]).then_inc(sems["so"], 16)
                    R.g(_g)
                    INC("sg", 16)
                    if also_out:
                        INC("so", 16)
                    W("v", "sg", C["sg"])

        def p6():
            xsR = x2_d.rearrange("(j p) t -> p j t", p=128)
            l3R = ln3_loc.rearrange("(j p) t -> p j t", p=128)
            W("s", "sg", C["sg"])
            for c in range(4):
                W("s", "vd", C["vd"])
                W("s", "sg", C["sg"])
                R.s(lambda h, c=c: h.dma_start(
                    out=xbuf[:], in_=xsR[:, :, 512 * c:512 * (c + 1)])
                    .then_inc(sems["sd"], 16))
                INC("sd", 16)
                W("a", "sd")
                for jc in range(8):
                    if jc > 0:
                        W("a", "pe", C["pe"])

                    def _a(h, jc=jc):
                        h.activation(out=castb, in_=xbuf[:, jc, :], func=AF.Identity)
                        r = h.activation(out=sqb, in_=xbuf[:, jc, :], func=AF.Square)
                        r.then_inc(sems["va"], 1)
                    R.a(_a)
                    INC("va")
                    W("t", "va")

                    def _t(h, jc=jc):
                        h.matmul(ps[3][0:1, 0:512], ones_col[:], castb,
                                 start=(jc == 0), stop=(jc == 7))
                        r = h.matmul(ps[3][0:1, 512:1024], ones_col[:], sqb,
                                     start=(jc == 0), stop=(jc == 7))
                        r.then_inc(sems["pe"], 1)
                    R.t(_t)
                    INC("pe")
                W("v", "pe")

                def _v1(h):
                    h.tensor_single_scalar(out=lnsF[0:1, 0, :], in_=ps[3][0:1, 0:512],
                                           scalar=1.0 / D, op=ALU.mult)
                    h.tensor_single_scalar(out=tmpo2[0:1, :], in_=ps[3][0:1, 512:1024],
                                           scalar=1.0 / D, op=ALU.mult)
                    h.tensor_mul(out=castb[0:1, :], in0=lnsF[0:1, 0, :],
                                 in1=lnsF[0:1, 0, :])
                    h.tensor_sub(out=tmpo2[0:1, :], in0=tmpo2[0:1, :], in1=castb[0:1, :])
                    r = h.tensor_single_scalar(out=tmpo2[0:1, :], in_=tmpo2[0:1, :],
                                               scalar=1e-5, op=ALU.add)
                    r.then_inc(sems["vd"], 1)
                R.v(_v1)
                INC("vd")
                W("a", "vd")
                R.a(lambda h: h.activation(out=tmpo2[0:1, :], in_=tmpo2[0:1, :],
                                           func=AF.Sqrt).then_inc(sems["va"], 1))
                INC("va")
                W("v", "va")
                R.v(lambda h: h.reciprocal(out=lnsF[0:1, 1, :], in_=tmpo2[0:1, :])
                    .then_inc(sems["vd"], 1))
                INC("vd")
                W("t", "vd")

                def _t2(h):
                    h.matmul(ps[2][:, 0:512], ones_rowf[:], lnsF[0:1, 0, :],
                             start=True, stop=True)
                    r = h.matmul(ps[2][:, 512:1024], ones_rowf[:], lnsF[0:1, 1, :],
                                 start=True, stop=True)
                    r.then_inc(sems["pe"], 1)
                R.t(_t2)
                INC("pe")
                W("v", "pe")

                def _v2(h):
                    for jc in range(8):
                        h.tensor_sub(out=castb, in0=xbuf[:, jc, :], in1=ps[2][:, 0:512])
                        r = h.tensor_mul(out=ln2c[:, jc, :], in0=castb,
                                         in1=ps[2][:, 512:1024])
                    r.then_inc(sems["vd"], 1)
                R.v(_v2)
                INC("vd")
                W("g", "vd")
                R.g(lambda h, c=c: h.dma_start(
                    out=l3R[:, :, 512 * c:512 * (c + 1)], in_=ln2c[:])
                    .then_inc(sems["sg"], 16))
                INC("sg", 16)
                W("v", "sg", C["sg"])
            W("g", "sg", C["sg"])
            R.g(lambda h: h.collective_compute(
                "AllGather", ALU.bypass, replica_groups=RG,
                ins=[ln3_loc], outs=[ln3_all]).then_inc(sems["vc"], 1))
            INC("vc")

        def p7():
            for (wsrc, off) in ((w_inT_x, 0), (w_inT_z, 256)):
                wr = wsrc.rearrange("(j p) e -> p j e", p=128)
                W("s", "va", C["va"])
                R.s(lambda h, wr=wr: h.dma_start(
                    out=arC_f[:, 0:2048].rearrange("p (j e) -> p j e", j=8),
                    in_=wr).then_inc(sems["sd"], 16))
                INC("sd", 16)
                W("a", "sd")

                def _a(h, off=off):
                    src2 = arC_f[:, 0:2048].rearrange("p (j e) -> p j e", j=8)
                    for jc in range(8):
                        r = h.activation(out=wioT[:, jc, off:off + 256],
                                         in_=src2[:, jc, :], func=AF.Identity)
                    r.then_inc(sems["va"], 1)
                R.a(_a)
                INC("va")
            wxr = wxT_sl.rearrange("(k p) n -> p k n", p=128)
            W("s", "va", C["va"])
            R.s(lambda h: h.dma_start(
                out=arC_f[:, 0:66].rearrange("p (k n) -> p k n", k=2), in_=wxr)
                .then_inc(sems["sd"], 16))
            INC("sd", 16)
            W("a", "sd")

            def _awx(h):
                src2 = arC_f[:, 0:66].rearrange("p (k n) -> p k n", k=2)
                h.activation(out=wxTb[:, 0, :], in_=src2[:, 0, :], func=AF.Identity)
                r = h.activation(out=wxTb[:, 1, :], in_=src2[:, 1, :], func=AF.Identity)
                r.then_inc(sems["va"], 1)
            R.a(_awx)
            INC("va")

            lnall = ln3_all.rearrange("(bk j p) t -> bk p j t", p=128, j=8)
            uR = u_d.rearrange("(k p) t -> p k t", p=128)
            zR = zs_d.rearrange("(k p) t -> p k t", p=128)
            bpR = bcd_part
            rhsT = arA_b[:, 0:16384].rearrange("p (j t) -> p j t", j=8)
            ublk = arC_f[:, 0:4102].rearrange("p (k t) -> p k t", k=2)
            zblk = arB.ap()[:, 0:4096].rearrange("p (k t) -> p k t", k=2)
            uS = arA.ap()[:, 8192:12288].rearrange("p (k t) -> p k t", k=2)
            halo = arA.ap()[:, 12288:12294].rearrange("p (k t) -> p k t", k=2)
            xpev = arA.ap()[:, 12300:14348][0:33, :]
            W("v", "vc", C["vc"])
            for tcb in range(8):
                seq_start = (tcb % 2 == 0)
                W("s", "vc", C["vc"])
                W("s", "va", C["va"])
                W("s", "pe", C["pe"])
                R.s(lambda h, tcb=tcb: h.dma_start(
                    out=rhsT[:], in_=lnall[tcb]).then_inc(sems["sd"], 16))
                INC("sd", 16)
                for th in range(2):
                    W("t", "sd")
                    W("t", "va", C["va"])

                    def _t(h, th=th):
                        for jc in range(8):
                            for Mt in range(4):
                                for tc in range(2):
                                    r = h.matmul(
                                        ps[Mt][:, 512 * tc:512 * (tc + 1)],
                                        wioT[:, jc, 128 * Mt:128 * (Mt + 1)],
                                        rhsT[:, jc, 1024 * th + 512 * tc:
                                             1024 * th + 512 * tc + 512],
                                        start=(jc == 0), stop=(jc == 7))
                        r.then_inc(sems["pe"], 1)
                    R.t(_t)
                    INC("pe")
                    W("a", "pe")

                    def _a(h, th=th):
                        for Mt in range(2):
                            h.activation(out=ublk[:, Mt, 3 + 1024 * th:3 + 1024 * th + 1024],
                                         in_=ps[Mt][:, 0:1024], func=AF.Identity)
                        for Mt in range(2):
                            r = h.activation(out=zblk[:, Mt, 1024 * th:1024 * th + 1024],
                                             in_=ps[Mt + 2][:, 0:1024], func=AF.Silu)
                        r.then_inc(sems["va"], 1)
                    R.a(_a)
                    INC("va")
                W("v", "va")
                if seq_start:
                    R.v(lambda h: h.memset(ublk[:, :, 0:3], 0.0).then_inc(sems["vd"], 1))
                else:
                    R.v(lambda h: h.tensor_copy(out=ublk[:, :, 0:3], in_=halo[:])
                        .then_inc(sems["vd"], 1))
                INC("vd")
                W("v", "vd")

                def _vc(h):
                    for k in range(2):
                        h.tensor_scalar(out=uS[:, k, :], in0=ublk[:, k, 3:2051],
                                        scalar1=conv_sb[:, k, 3:4],
                                        scalar2=convb_sb[:, k:k + 1],
                                        op0=ALU.mult, op1=ALU.add)
                        for sh in (1, 2, 3):
                            r = h.scalar_tensor_tensor(
                                out=uS[:, k, :], in0=ublk[:, k, 3 - sh:2051 - sh],
                                scalar=conv_sb[:, k, 3 - sh:4 - sh],
                                in1=uS[:, k, :], op0=ALU.mult, op1=ALU.add)
                    r = h.tensor_copy(out=halo[:], in_=ublk[:, :, 2048:2051])
                    r.then_inc(sems["vd"], 1)
                R.v(_vc)
                INC("vd")
                W("a", "vd")
                R.a(lambda h: h.activation(out=uS[:, :, :].rearrange("p a b -> p (a b)"),
                                           in_=uS[:, :, :].rearrange("p a b -> p (a b)"),
                                           func=AF.Silu).then_inc(sems["va"], 1))
                INC("va")
                ubf = arA_b[:, 28672:32768].rearrange("p (k t) -> p k t", k=2)
                W("v", "va")
                R.v(lambda h: h.tensor_copy(out=ubf[:, :, :], in_=uS[:, :, :])
                    .then_inc(sems["vd"], 1))
                INC("vd")
                for tc in range(4):
                    W("t", "vd", C["vd"])
                    W("t", "va", C["va"])

                    def _tx(h, tc=tc):
                        for k in range(2):
                            r = h.matmul(ps[0][0:33, 512 * (tc % 2):512 * (tc % 2) + 512],
                                         wxTb[:, k, :],
                                         ubf[:, k, 512 * tc:512 * (tc + 1)],
                                         start=(k == 0), stop=(k == 1))
                        r.then_inc(sems["pe"], 1)
                    R.t(_tx)
                    INC("pe")
                    W("a", "pe")
                    R.a(lambda h, tc=tc: h.activation(
                        out=xpev[:, 512 * tc:512 * (tc + 1)],
                        in_=ps[0][0:33, 512 * (tc % 2):512 * (tc % 2) + 512],
                        func=AF.Identity).then_inc(sems["va"], 1))
                    INC("va")
                W("g", "va", C["va"])
                W("g", "vd", C["vd"])

                def _g(h, tcb=tcb):
                    h.dma_start(out=uR[:, :, 2048 * tcb:2048 * (tcb + 1)],
                                in_=uS[:, :, :]).then_inc(sems["sg"], 16)
                    h.dma_start(out=zR[:, :, 2048 * tcb:2048 * (tcb + 1)],
                                in_=zblk[:, :, :]).then_inc(sems["sg"], 16)
                    h.dma_start(out=bpR[:, 2048 * tcb:2048 * (tcb + 1)],
                                in_=xpev[:]).then_inc(sems["sg"], 16)
                R.g(_g)
                INC("sg", 48)
                W("a", "sg", C["sg"])
                W("v", "sg", C["sg"])
            W("g", "sg", C["sg"])
            R.g(lambda h: h.collective_compute(
                "AllReduce", ALU.add, replica_groups=RG,
                ins=[bcd_part], outs=[bcd_all]).then_inc(sems["vc"], 1))
            INC("vc")

        def p8():
            gR = gated_loc.rearrange("(k p) t -> p k t", p=128)
            uR = u_d.rearrange("(k p) t -> p k t", p=128)
            zR = zs_d.rearrange("(k p) t -> p k t", p=128)
            bcdc = arB.ap()[0:33, 0:2048]
            slotA = arB.ap()[0:1, 14336:16384]
            arBb = arB.ap().bitcast(BF16)
            bcdb = arBb[0:33, 4096:6144]
            sl_b = [arBb[0:1, 28672:30720], arBb[32:33, 28672:30720]]
            sl_c = [arBb[64:65, 28672:30720], arBb[0:1, 30720:32768]]
            ob_b = [None, None]
            ps_bB = [None, None]
            ps_bC = [None, None]
            scr1 = arB.ap()[:, 2048:6144].rearrange("p (k t) -> p k t", k=2)
            scr2 = arB.ap()[:, 6144:10240].rearrange("p (k t) -> p k t", k=2)
            tmpy = arB.ap()[:, 10240:14336].rearrange("p (k t) -> p k t", k=2)
            dt_v = arD_f[:, 0:4096].rearrange("p (k t) -> p k t", k=2)
            dtu_v = arD_f[:, 4096:8192].rearrange("p (k t) -> p k t", k=2)
            dA_v = arC_f[:, 0:4096].rearrange("p (k t) -> p k t", k=2)
            dA_v2 = arC_f[:, 4096:8192].rearrange("p (k t) -> p k t", k=2)
            hbuf = arB.ap()[:, 6144:10240].rearrange("p (k t) -> p k t", k=2)
            uc_v = arA.ap()[:, 0:4096].rearrange("p (k t) -> p k t", k=2)
            zc_v = arA.ap()[:, 4096:8192].rearrange("p (k t) -> p k t", k=2)
            yacc = arA.ap()[:, 8192:12288].rearrange("p (k t) -> p k t", k=2)
            gbf = arA_b[:, 24576:28672].rearrange("p (k t) -> p k t", k=2)

            for sc in range(8):
                seq_start = (sc % 2 == 0)
                W("s", "vc", C["vc"])
                W("s", "vd", C["vd"])
                W("s", "va", C["va"])

                def _s(h, sc=sc):
                    h.dma_start(out=bcdc, in_=bcd_all[:, 2048 * sc:2048 * (sc + 1)])                         .then_inc(sems["sd"], 16)
                    h.dma_start(out=uc_v[:, :, :], in_=uR[:, :, 2048 * sc:2048 * (sc + 1)])                         .then_inc(sems["sd"], 16)
                    h.dma_start(out=zc_v[:, :, :], in_=zR[:, :, 2048 * sc:2048 * (sc + 1)])                         .then_inc(sems["sd"], 16)
                R.s(_s)
                INC("sd", 48)
                W("s", "sd", C["sd"])
                W("s", "pe", C["pe"])
                R.s(lambda h: h.dma_start(out=slotA, in_=bcdc[32:33, :])
                    .then_inc(sems["sd"], 16))
                INC("sd", 16)
                W("t", "sd")
                W("t", "va", C["va"])
                W("t", "vd", C["vd"])

                def _tb(h):
                    for tc in range(4):
                        pst = ps[2] if tc < 2 else ps[3]
                        r = h.matmul(pst[:, 512 * (tc % 2):512 * (tc % 2) + 512],
                                     ones_rowf[:], slotA[0:1, 512 * tc:512 * (tc + 1)],
                                     start=True, stop=True)
                    r.then_inc(sems["pe"], 1)
                R.t(_tb)
                INC("pe")
                W("a", "pe")

                def _asp(h):
                    for k in range(2):
                        for half in range(2):
                            pst = ps[2] if half == 0 else ps[3]
                            h.activation(out=scr1[:, k, 1024 * half:1024 * (half + 1)],
                                         in_=pst[:, 0:1024], func=AF.Identity,
                                         scale=dtw_sb[:, k:k + 1], bias=dtb_sb[:, k:k + 1])
                    h.activation(out=scr2.rearrange("p a b -> p (a b)"),
                                 in_=scr1.rearrange("p a b -> p (a b)"), func=AF.Abs)
                    h.activation(out=scr2.rearrange("p a b -> p (a b)"),
                                 in_=scr2.rearrange("p a b -> p (a b)"), func=AF.Exp,
                                 scale=-1.0)
                    r = h.activation(out=scr2.rearrange("p a b -> p (a b)"),
                                     in_=scr2.rearrange("p a b -> p (a b)"), func=AF.Ln,
                                     bias=1.0)
                    r.then_inc(sems["va"], 1)
                R.a(_asp)
                INC("va")
                W("v", "va")

                def _vsp(h):
                    h.scalar_tensor_tensor(out=dt_v.rearrange("p a b -> p (a b)"),
                                           in0=scr1.rearrange("p a b -> p (a b)"),
                                           scalar=0.0, in1=scr2.rearrange("p a b -> p (a b)"),
                                           op0=ALU.max, op1=ALU.add)
                    r = h.tensor_mul(out=dtu_v.rearrange("p a b -> p (a b)"),
                                     in0=dt_v.rearrange("p a b -> p (a b)"),
                                     in1=uc_v.rearrange("p a b -> p (a b)"))
                    r.then_inc(sems["vd"], 1)
                R.v(_vsp)
                INC("vd")
                W("a", "sd", C["sd"])
                W("a", "va", C["va"])
                R.a(lambda h: h.activation(out=bcdb, in_=bcdc, func=AF.Identity)
                    .then_inc(sems["va"], 1))
                INC("va")
                vd_hist = {}
                for n in range(NST):
                    par = n % 2
                    pA = ps[0] if par == 0 else ps[2]
                    pB = ps[1] if par == 0 else ps[3]
                    sB = sl_b[par]
                    sC = sl_c[par]
                    obB = ones_row[:] if par == 0 else onesq_b[32:33, :]
                    obC = onesq_b[64:65, :] if par == 0 else ones_row[:]
                    W("s", "pe", C["pe"])
                    W("s", "va", C["va"])

                    def _sn(h, n=n, sB=sB, sC=sC):
                        h.dma_start(out=sB, in_=bcdb[n:n + 1, :]).then_inc(sems["sd"], 16)
                        h.dma_start(out=sC, in_=bcdb[NST + n:NST + n + 1, :]) \
                            .then_inc(sems["sd"], 16)
                    R.s(_sn)
                    INC("sd", 32)
                    W("t", "vd", vd_hist.get((n - 2, "y"), C["vd"]))
                    W("t", "va", C["va"])
                    W("t", "sd", C["sd"])

                    def _tnB(h, pA=pA, pB=pB, sB=sB, obB=obB):
                        for tc in range(4):
                            pst = pA if tc < 2 else pB
                            r = h.matmul(pst[:, 512 * (tc % 2):512 * (tc % 2) + 512],
                                         obB, sB[0:1, 512 * tc:512 * (tc + 1)],
                                         start=True, stop=True)
                        r.then_inc(sems["pe"], 1)
                    R.t(_tnB)
                    INC("pe")
                    peB = C["pe"]
                    W("a", "pe", C["pe"])
                    W("a", "vd", vd_hist.get((n - 2, "y"), C["vd"]))
                    dAp = dA_v if par == 0 else dA_v2

                    def _an(h, n=n, dAp=dAp):
                        for k in range(2):
                            r = h.activation(out=dAp[:, k, :], in_=dt_v[:, k, :],
                                             func=AF.Exp, scale=A_sb[:, k, n:n + 1])
                        r.then_inc(sems["va"], 1)
                    R.a(_an)
                    INC("va")
                    W("v", "pe", peB)

                    def _v1(h, pA=pA, pB=pB):
                        for k in range(2):
                            for half in range(2):
                                pst = pA if half == 0 else pB
                                r = h.tensor_mul(
                                    out=hbuf[:, k, 1024 * half:1024 * (half + 1)],
                                    in0=dtu_v[:, k, 1024 * half:1024 * (half + 1)],
                                    in1=pst[:, 0:1024])
                        r.then_inc(sems["vd"], 1)
                    R.v(_v1)
                    INC("vd")
                    vd_hist[(n, "b")] = C["vd"]
                    W("t", "vd", vd_hist[(n, "b")])

                    def _tnC(h, pA=pA, pB=pB, sC=sC, obC=obC):
                        for tc in range(4):
                            pst = pA if tc < 2 else pB
                            r = h.matmul(pst[:, 512 * (tc % 2):512 * (tc % 2) + 512],
                                         obC, sC[0:1, 512 * tc:512 * (tc + 1)],
                                         start=True, stop=True)
                        r.then_inc(sems["pe"], 1)
                    R.t(_tnC)
                    INC("pe")
                    peC = C["pe"]
                    W("v", "va")

                    def _v2(h, n=n, seq_start=seq_start, dAp=dAp, pA=pA, pB=pB, peC=peC):
                        for k in range(2):
                            ini = 0.0 if seq_start else carry[:, k, n:n + 1]
                            h.tensor_tensor_scan(out=hbuf[:, k, :], data0=dAp[:, k, :],
                                                 data1=hbuf[:, k, :], initial=ini,
                                                 op0=ALU.mult, op1=ALU.add)
                        h.tensor_copy(out=carry[:, 0, n:n + 1], in_=hbuf[:, 0, 2047:2048])
                        h.tensor_copy(out=carry[:, 1, n:n + 1], in_=hbuf[:, 1, 2047:2048])
                        h.wait_ge(sems["pe"], peC)
                        for k in range(2):
                            for half in range(2):
                                pst = pA if half == 0 else pB
                                h.tensor_mul(out=tmpy[:, k, 1024 * half:1024 * (half + 1)],
                                             in0=hbuf[:, k, 1024 * half:1024 * (half + 1)],
                                             in1=pst[:, 0:1024])
                        if n == 0:
                            r = h.tensor_copy(out=yacc.rearrange("p a b -> p (a b)"),
                                              in_=tmpy.rearrange("p a b -> p (a b)"))
                        else:
                            r = h.tensor_add(out=yacc.rearrange("p a b -> p (a b)"),
                                             in0=yacc.rearrange("p a b -> p (a b)"),
                                             in1=tmpy.rearrange("p a b -> p (a b)"))
                        r.then_inc(sems["vd"], 1)
                    R.v(_v2)
                    INC("vd")
                    vd_hist[(n, "y")] = C["vd"]
                W("v", "vd")

                def _vg(h):
                    for k in range(2):
                        h.scalar_tensor_tensor(out=yacc[:, k, :], in0=uc_v[:, k, :],
                                               scalar=dpar_sb[:, k:k + 1],
                                               in1=yacc[:, k, :],
                                               op0=ALU.mult, op1=ALU.add)
                    r = h.tensor_mul(out=gbf.rearrange("p a b -> p (a b)"),
                                     in0=yacc.rearrange("p a b -> p (a b)"),
                                     in1=zc_v.rearrange("p a b -> p (a b)"))
                    r.then_inc(sems["vd"], 1)
                R.v(_vg)
                INC("vd")
                W("g", "vd")
                R.g(lambda h, sc=sc: h.dma_start(
                    out=gR[:, :, 2048 * sc:2048 * (sc + 1)], in_=gbf[:, :, :])
                    .then_inc(sems["sg"], 16))
                INC("sg", 16)
                W("v", "sg", C["sg"])
            W("g", "sg", C["sg"])
            R.g(lambda h: h.collective_compute(
                "AllGather", ALU.bypass, replica_groups=RG,
                ins=[gated_loc], outs=[gated_all]).then_inc(sems["vc"], 1))
            INC("vc")

        def p9():
            woR = w_outT_in.rearrange("(k p) dd -> p k dd", p=128)
            woT = arC_b[:, 0:16384].rearrange("p (k dd) -> p k dd", k=16)
            for half in range(2):
                stg9 = arD_f[:, 0:8192] if half == 0 else arB.ap()[:, 0:8192]
                W("s", "va", C["va"])
                W("s", "vd", C["vd"])
                R.s(lambda h, half=half, stg9=stg9: h.dma_start(
                    out=stg9.rearrange("p (k dd) -> p k dd", k=8),
                    in_=woR[:, 8 * half:8 * (half + 1), :]).then_inc(sems["sd"], 16))
                INC("sd", 16)
                W("a", "sd")
                W("a", "vd", C["vd"])
                W("a", "sg", C["sg"])

                def _a(h, half=half, stg9=stg9):
                    src2 = stg9.rearrange("p (k dd) -> p k dd", k=8)
                    for k in range(8):
                        r = h.activation(out=woT[:, 8 * half + k, :], in_=src2[:, k, :],
                                         func=AF.Identity)
                    r.then_inc(sems["va"], 1)
                R.a(_a)
                INC("va")
            W("g", "va", C["va"])
            roff = nc.gpsimd.alloc_register("roff")
            ga_h = gated_all.tensor
            x2R = x2_d.rearrange("(j p) t -> p j t", p=128)
            x3R = x3_d.rearrange("(j p) t -> p j t", p=128)
            outR3 = outT.rearrange("(j p) t -> p j t", p=128)
            grhs = arD_b[:, 0:8192].rearrange("p (k t) -> p k t", k=16)
            x2c = arD_f[:, 4096:4608]
            tmpr = arD_f[:, 4608:5120]
            W("g", "vc", C["vc"])
            for f in range(4):
                W("g", "pe", C["pe"])
                W("g", "vd", C["vd"])
                W("g", "va", C["va"])

                def _s(h, f=f):
                    pid = h.partition_id()
                    h.reg_alu(roff, pid, TOK, ALU.mult)
                    h.reg_alu(roff, roff, 512 * f, ALU.add)
                    h.dma_start(out=grhs[:, :, :],
                                in_=bass.AP(ga_h, roff,
                                            [[T16, 128], [128 * T16, 16], [1, 512]]))                         .then_inc(sems["sg"], 16)
                R.g(_s)
                INC("sg", 16)
                if DBG and f == 0:
                    W("g", "sg", C["sg"])
                    W("g", "va", C["va"])
                    def _gdw(h):
                        h.dma_start(out=dbg_gr, in_=grhs[:, :, :]).then_inc(sems["so"], 16)
                        h.dma_start(out=dbg_wo, in_=woT[:, :, :]).then_inc(sems["so"], 16)
                    R.g(_gdw)
                    INC("so", 32)
                    W("t", "so", C["so"])
                for Mt in range(8):
                    pst = ps[0] if Mt % 2 == 0 else ps[1]
                    W("t", "sg", C["sg"])
                    W("t", "va", C["va"])

                    def _t(h, Mt=Mt, pst=pst):
                        for ec in range(16):
                            r = h.matmul(pst[:, 0:512], woT[:, ec, 128 * Mt:128 * (Mt + 1)],
                                         grhs[:, ec, :], start=(ec == 0), stop=(ec == 15))
                        r.then_inc(sems["pe"], 1)
                    R.t(_t)
                    INC("pe")
                    W("s", "vd", C["vd"])
                    W("s", "sg", C["sg"])
                    R.s(lambda h, Mt=Mt, f=f: h.dma_start(
                        out=x2c, in_=x2R[:, Mt, 512 * f:512 * (f + 1)])
                        .then_inc(sems["sd"], 16))
                    INC("sd", 16)
                    W("a", "pe")
                    W("a", "vd", C["vd"])
                    R.a(lambda h, pst=pst: h.activation(
                        out=tmpr, in_=pst[:, 0:512], func=AF.Identity)
                        .then_inc(sems["va"], 1))
                    INC("va")
                    W("v", "va")
                    W("v", "sd")
                    R.v(lambda h: h.tensor_add(out=x2c, in0=x2c, in1=tmpr)
                        .then_inc(sems["vd"], 1))
                    INC("vd")
                    W("g", "vd")

                    def _g(h, Mt=Mt, f=f):
                        r = h.dma_start(out=x3R[:, Mt, 512 * f:512 * (f + 1)], in_=x2c)
                        r.then_inc(sems["sg"], 16)
                        if STAGE == 3:
                            h.dma_start(out=outR3[:, Mt, 512 * f:512 * (f + 1)],
                                        in_=x2c).then_inc(sems["so"], 16)
                    R.g(_g)
                    INC("sg", 16)
                    if STAGE == 3:
                        INC("so", 16)
                    W("v", "sg", C["sg"])

        def p10():
            s1R = s1T_d.rearrange("(j p) t -> p j t", p=128)
            s2R = s2T_d.rearrange("(j p) t -> p j t", p=128)
            W("s", "va", C["va"])
            W("s", "vd", C["vd"])
            W("s", "pe", C["pe"])
            W("s", "sg", C["sg"])

            def _s(h):
                h.dma_start(out=s1T[:], in_=s1R).then_inc(sems["sd"], 16)
                h.dma_start(out=s2T[:], in_=s2R).then_inc(sems["sd"], 16)
            R.s(_s)
            INC("sd", 32)
            W("t", "sd")
            ffn_chunks(x3_d, x1_d, STAGE == 4)

        p0()
        p1()
        if DBG:
            W("g", "vd", C["vd"])
            def _gdbg(h):
                h.dma_start(out=dbg_ln1.rearrange("(j p) t -> p j t", p=128),
                            in_=ln1[:]).then_inc(sems["so"], 16)
            R.g(_gdbg)
            INC("so", 16)
        p2()
        p3()
        if STAGE >= 2:
            p4()
            ffn_chunks(x1_d, x2_d, STAGE == 2)
        if STAGE >= 3:
            p6()
            p7()
            p8()
            p9()
        if STAGE >= 4:
            p10()

        if DBG and STAGE >= 3:
            W("g", "sg", C["sg"])
            W("g", "vc", C["vc"])

            def _gm(h):
                h.dma_start(out=dbg_u, in_=u_d[:, 0:2048]).then_inc(sems["so"], 16)
                h.dma_start(out=dbg_z, in_=zs_d[:, 0:2048]).then_inc(sems["so"], 16)
                h.dma_start(out=dbg_bcd, in_=bcd_all[:, 0:2048]).then_inc(sems["so"], 16)
                h.dma_start(out=dbg_g, in_=gated_all[:, 0:2048]).then_inc(sems["so"], 16)
                h.dma_start(out=dbg_l3, in_=ln3_all[0:D, 0:2048]).then_inc(sems["so"], 16)
            R.g(_gm)
            INC("so", 80)
        W("g", "sg", C["sg"])
        if C["so"]:
            W("g", "so", C["so"])
        R.g(lambda h: h.nop())

        with nc.Block() as blk:
            @blk.sync
            def _(h):
                for fn in R.ops["s"]:
                    fn(h)

            @blk.scalar
            def _(h):
                for fn in R.ops["a"]:
                    fn(h)

            @blk.vector
            def _(h):
                for fn in R.ops["v"]:
                    fn(h)

            @blk.tensor
            def _(h):
                for fn in R.ops["t"]:
                    fn(h)

            @blk.gpsimd
            def _(h):
                for fn in R.ops["g"]:
                    fn(h)

    return nc


_NC_CACHE = None


def _get_nc():
    global _NC_CACHE
    if _NC_CACHE is None:
        _NC_CACHE = build_nc()
    return _NC_CACHE


def _prep_inmaps(inputs):
    f32 = lambda a: np.ascontiguousarray(np.asarray(a, dtype=np.float32))
    x = f32(inputs["L0_attn_a_x"])
    circ = {k: np.tile(f32(inputs[f"L0_attn_a_w{k}_c"]), CTILE // D) for k in "qkvo"}
    bias_cols = {k: np.ascontiguousarray(
        f32(inputs[f"L0_attn_a_w{k}_b"]).reshape(8, 128).T) for k in "qkvo"}
    wvb_row = f32(inputs["L0_attn_a_wv_b"]).reshape(1, D)
    w1T = np.ascontiguousarray(f32(inputs["L0_ffn_w1"]).T)
    w2T = np.ascontiguousarray(f32(inputs["L0_ffn_w2"]).T)
    b1c = np.ascontiguousarray(f32(inputs["L0_ffn_b1"]).reshape(HDIM // 128, 128).T)
    b2c = np.ascontiguousarray(f32(inputs["L0_ffn_b2"]).reshape(D // 128, 128).T)
    inw = f32(inputs["L0_attn_b_in_proj_w"])
    convw = f32(inputs["L0_attn_b_conv_w"])
    convb = f32(inputs["L0_attn_b_conv_b"])
    xpw = f32(inputs["L0_attn_b_x_proj_w"])
    dtw = f32(inputs["L0_attn_b_dt_proj_w"])[:, 0]
    dtb = f32(inputs["L0_attn_b_dt_proj_b"])
    alog = f32(inputs["L0_attn_b_A_log"])
    dpar = f32(inputs["L0_attn_b_D_param"])
    w_outT = np.ascontiguousarray(f32(inputs["L0_attn_b_out_proj_w"]).T)

    def cols2(v256):
        return np.ascontiguousarray(v256.reshape(2, 128).T)

    in_maps = []
    for c in range(NCORE):
        b, hh = c // 2, c % 2
        sl = slice(DSL * c, DSL * (c + 1))
        m = dict(
            xT_full=np.ascontiguousarray(np.concatenate(
                [x[b, TOK * hh:TOK * (hh + 1), :],
                 x[b, TOK * (1 - hh):TOK * (2 - hh), :]], axis=0).T),
            x_ownT=np.ascontiguousarray(x[b, TOK * hh:TOK * (hh + 1), :].T),
            w1T=w1T, w2T=w2T, b1_c=b1c, b2_c=b2c,
            w1N=f32(inputs["L0_ffn_w1"]), w2N=f32(inputs["L0_ffn_w2"]),
            wvb_row=wvb_row,
            w_inT_x=np.ascontiguousarray(inw[sl, :].T),
            w_inT_z=np.ascontiguousarray(inw[DI + DSL * c:DI + DSL * (c + 1), :].T),
            conv_w_c=np.ascontiguousarray(
                convw[sl].reshape(2, 128, 4).transpose(1, 0, 2)),
            conv_b_c=cols2(convb[sl]),
            wxT_sl=np.ascontiguousarray(xpw[:, sl].T),
            dtw_c=cols2(dtw[sl]),
            dtb_c=cols2(dtb[sl]),
            alog_c=np.ascontiguousarray(
                alog[sl].reshape(2, 128, NST).transpose(1, 0, 2)),
            dpar_c=cols2(dpar[sl]),
            w_outT=w_outT,
        )
        for k in "qkvo":
            m[f"c{k}2"] = circ[k]
            m[f"w{k}b_c"] = bias_cols[k]
        in_maps.append(m)
    return in_maps


def kernel(**inputs):
    nc = _get_nc()
    in_maps = _prep_inmaps(inputs)
    res = run_bass_kernel_spmd(nc, in_maps, core_ids=list(range(NCORE)))
    out = np.empty((4, S, D), dtype=np.float32)
    for c in range(NCORE):
        b, hh = c // 2, c % 2
        out[b, TOK * hh:TOK * (hh + 1), :] = res.results[c]["outT"].T
    return out


# version 09 (final)
# speedup vs baseline: 1.1965x; 1.0022x over previous
import numpy as np
import ml_dtypes
from contextlib import ExitStack

import concourse.bass as bass
import concourse.mybir as mybir
from concourse.bass_utils import run_bass_kernel_spmd

F32 = mybir.dt.float32
BF16 = mybir.dt.bfloat16
AF = mybir.ActivationFunctionType
ALU = mybir.AluOpType

D = 1024
S = 4096
TOK = 2048
H, HD = 16, 64
HDIM = 4096
DI = 2048
DSL = 256
NST = 16
T16 = 16384
NCORE = 8
CTILE = 132 * D

STAGE = 4

RG = [[0, 1, 2, 3, 4, 5, 6, 7]]


class Rec:
    def __init__(self):
        self.ops = {e: [] for e in "savtg"}

    def s(self, fn): self.ops["s"].append(fn)
    def a(self, fn): self.ops["a"].append(fn)
    def v(self, fn): self.ops["v"].append(fn)
    def t(self, fn): self.ops["t"].append(fn)
    def g(self, fn): self.ops["g"].append(fn)


def build_nc():
    nc = bass.Bass("TRN2", target_bir_lowering=False)

    def inp(name, shape, dt=F32):
        return nc.dram_tensor(name, shape, dt, kind="ExternalInput").ap()

    xT_full = inp("xT_full", [D, S])
    x_ownT = inp("x_ownT", [D, TOK])
    circ2 = {k: inp(f"c{k}2", [CTILE]) for k in "qkvo"}
    bias_in = {k: inp(f"w{k}b_c", [128, 8]) for k in "qkvo"}
    wvb_row_in = inp("wvb_row", [1, D])
    w1T_in = inp("w1T", [D, HDIM])
    w2T_in = inp("w2T", [HDIM, D])
    w1N_in = inp("w1N", [HDIM, D])
    w2N_in = inp("w2N", [D, HDIM])
    b1_c = inp("b1_c", [128, HDIM // 128])
    b2_c = inp("b2_c", [128, D // 128])
    w_inT_x = inp("w_inT_x", [D, DSL])
    w_inT_z = inp("w_inT_z", [D, DSL])
    conv_w_c = inp("conv_w_c", [128, 2, 4])
    conv_b_c = inp("conv_b_c", [128, 2])
    wxT_sl = inp("wxT_sl", [DSL, 33])
    dtw_c = inp("dtw_c", [128, 2])
    dtb_c = inp("dtb_c", [128, 2])
    alog_c = inp("alog_c", [128, 2, NST])
    dpar_c = inp("dpar_c", [128, 2])
    w_outT_in = inp("w_outT", [DI, D])

    outT = nc.dram_tensor("outT", [D, TOK], F32, kind="ExternalOutput").ap()
    DBG = False
    if DBG:
        dbg_ln1 = nc.dram_tensor("dbg_ln1", [D, S], BF16, kind="ExternalOutput").ap()
        dbg_om = nc.dram_tensor("dbg_om", [D, TOK], BF16, kind="ExternalOutput").ap()
        dbg_kT = nc.dram_tensor("dbg_kT", [128, S], BF16, kind="ExternalOutput").ap()
        dbg_qT = nc.dram_tensor("dbg_qT", [128, TOK], BF16, kind="ExternalOutput").ap()
        dbg_vtm = nc.dram_tensor("dbg_vtm", [128, 32 * 130], BF16, kind="ExternalOutput").ap()
        dbg_neg = nc.dram_tensor("dbg_neg", [128, 2], F32, kind="ExternalOutput").ap()
        dbg_sc = nc.dram_tensor("dbg_sc", [2, 2 * NST], F32, kind="ExternalOutput").ap()
        dbg_m = nc.dram_tensor("dbg_m", [128, 2], BF16, kind="ExternalOutput").ap()
        dbg_al = nc.dram_tensor("dbg_al", [128, 48], F32, kind="ExternalOutput").ap()
        dbg_ln2 = nc.dram_tensor("dbg_ln2", [128, 8, 512], BF16, kind="ExternalOutput").ap()
        dbg_h = nc.dram_tensor("dbg_h", [128, 4, 512], BF16, kind="ExternalOutput").ap()
        dbg_xb = nc.dram_tensor("dbg_xb", [128, 8, 512], F32, kind="ExternalOutput").ap()
        dbg_u = nc.dram_tensor("dbg_u", [DSL, 2048], F32, kind="ExternalOutput").ap()
        dbg_z = nc.dram_tensor("dbg_z", [DSL, 2048], F32, kind="ExternalOutput").ap()
        dbg_bcd = nc.dram_tensor("dbg_bcd", [33, 2048], F32, kind="ExternalOutput").ap()
        dbg_g = nc.dram_tensor("dbg_g", [DI, 2048], BF16, kind="ExternalOutput").ap()
        dbg_l3 = nc.dram_tensor("dbg_l3", [D, 2048], BF16, kind="ExternalOutput").ap()
        dbg_gr = nc.dram_tensor("dbg_gr", [128, 16, 512], BF16, kind="ExternalOutput").ap()
        dbg_wo = nc.dram_tensor("dbg_wo", [128, 16, 1024], BF16, kind="ExternalOutput").ap()
        dbg_wo2 = nc.dram_tensor("dbg_wo2", [128, 16, 1024], BF16, kind="ExternalOutput").ap()
        dbg_sq = nc.dram_tensor("dbg_sq", [128, 4096], BF16, kind="ExternalOutput").ap()

    def scratch(name, shape, dt=F32):
        return nc.dram_tensor(name, shape, dt, kind="Internal").ap()

    x1_d = scratch("x1_d", [D, TOK])
    x2_d = scratch("x2_d", [D, TOK])
    x3_d = scratch("x3_d", [D, TOK])
    s1T_d = scratch("s1T_d", [D, HDIM], BF16)
    s2T_d = scratch("s2T_d", [HDIM, D], BF16)
    ln3_loc = scratch("ln3_loc", [D, TOK], BF16)
    ln3_all = scratch("ln3_all", [NCORE * D, TOK], BF16)
    bcd_part = scratch("bcd_part", [33, T16])
    bcd_all = scratch("bcd_all", [33, T16])
    u_d = scratch("u_d", [DSL, T16])
    zs_d = scratch("zs_d", [DSL, T16])
    gated_loc = scratch("gated_loc", [DSL, T16], BF16)
    gated_all = scratch("gated_all", [DI, T16], BF16)

    es = ExitStack()
    with es:
        sbt = lambda name, shape, dt: es.enter_context(nc.sbuf_tensor(name, shape, dt))

        bias_q = sbt("bias_q", [128, 8], F32)
        bias_k = sbt("bias_k", [128, 8], F32)
        bias_v = sbt("bias_v", [128, 8], F32)
        bias_o = sbt("bias_o", [128, 8], F32)
        vbrow = sbt("vbrow", [1, D], BF16)
        ones_col = sbt("ones_col", [128, 1], BF16)
        ones_row = sbt("ones_row", [1, 128], BF16)
        ones_rowf = sbt("ones_rowf", [1, 128], F32)
        onesq = sbt("onesq", [128, 128], F32)
        onesq_b = sbt("onesq_b", [128, 128], BF16)
        maskAB = sbt("maskAB", [128, 2], BF16)
        iota_col = sbt("iota_col", [128, 1], mybir.dt.int32)
        negMrow = sbt("negMrow", [1, 2], F32)
        negMb = sbt("negMb", [128, 2], F32)
        nsc = sbt("nsc", [1, 8], F32)
        carry = sbt("carry", [128, 2, NST], F32)
        A_sb = sbt("A_sb", [128, 2, NST], F32)
        alpha1 = sbt("alpha1", [128, HDIM // 128], F32)
        alpha2 = sbt("alpha2", [128, D // 128], F32)
        b1cols = sbt("b1cols", [128, HDIM // 128], F32)
        b2cols = sbt("b2cols", [128, D // 128], F32)
        conv_sb = sbt("conv_sb", [128, 2, 4], F32)
        convb_sb = sbt("convb_sb", [128, 2], F32)
        dtw_sb = sbt("dtw_sb", [128, 2], F32)
        dtb_sb = sbt("dtb_sb", [128, 2], F32)
        dpar_sb = sbt("dpar_sb", [128, 2], F32)
        wioT = sbt("wioT", [128, 8, 512], BF16)
        wxTb = sbt("wxTb", [128, 2, 33], BF16)

        arA = sbt("arA", [128, 16384], F32)
        arB = sbt("arB", [128, 16384], F32)
        arC = sbt("arC", [128, 8192], F32)
        arD = sbt("arD", [128, 8192], F32)

        ps = [es.enter_context(nc.psum_tensor(f"ps{i}", [128, 1024], F32))
              for i in range(4)]

        sems = {}
        for name in ("sd", "sg", "ssc", "pe", "va", "vd", "vg", "vc", "so"):
            sems[name] = es.enter_context(nc.semaphore(name))
        C = {k: 0 for k in sems}

        R = Rec()

        def W(eng, name, val=None):
            v = C[name] if val is None else val
            sem = sems[name]
            getattr(R, eng)(lambda h: h.wait_ge(sem, v))

        def INC(name, n=1):
            C[name] += n
            return C[name]

        arA_b = arA.ap().bitcast(BF16)
        arB_b = arB.ap().bitcast(BF16)
        arC_b = arC.ap().bitcast(BF16)
        arD_b = arD.ap().bitcast(BF16)
        arD_f = arD.ap()

        ln1 = arA_b.rearrange("p (j t) -> p j t", j=8)
        CT = {k: arB_b[:, 8192 * i:8192 * (i + 1)].rearrange("p (j i) -> p j i", j=8)
              for i, k in enumerate("qkvo")}
        om = arC_b.rearrange("p (j t) -> p j t", j=8)
        stage = arC.ap().rearrange("p (r j t) -> p r j t", r=2, j=8)

        ln_xb = arD_b[:, 0:4096].rearrange("p (j t) -> p j t", j=8)
        ln_x2 = arD_b[:, 4096:8192].rearrange("p (j t) -> p j t", j=8)
        lns_ln = arD_f[:, 4096:6144].rearrange("p (k t) -> p k t", k=4)
        tmp_ln = arD_f[:, 6144:6656]

        kTv = arD_b[:, 0:4096]
        vtm = arD_b[:, 4096:8256].rearrange("p (tt c) -> p tt c", c=130)
        qTv = arD_b[:, 8256:10304]
        pT = arD_b[:, 10304:14400].rearrange("p (r t) -> p r t", r=4)
        rb64 = arD_f[:, 5152:6176]
        lns_at = arD_f[:, 6176:7200].rearrange("p (k t) -> p k t", k=2)
        lns_at2 = arD_f[:, 5152:6176].rearrange("p (k t) -> p k t", k=2)
        x0c = arD_f[:, 0:512]
        x1c = arD_f[:, 512:1024]
        tmpo = arD_f[:, 1024:1536]

        def p0():
            def _s(h):
                for t, src in ((bias_q, bias_in["q"]), (bias_k, bias_in["k"]),
                               (bias_v, bias_in["v"]), (bias_o, bias_in["o"]),
                               (arD_f[0:1, 0:1024], wvb_row_in), (b1cols, b1_c), (b2cols, b2_c),
                               (conv_sb, conv_w_c), (convb_sb, conv_b_c),
                               (dtw_sb, dtw_c), (dtb_sb, dtb_c), (dpar_sb, dpar_c),
                               (A_sb, alog_c)):
                    h.dma_start(out=t[:], in_=src).then_inc(sems["sd"], 16)
            R.s(_s)
            INC("sd", 16 * 13)

            R.g(lambda h: h.iota(iota_col[:], pattern=[[0, 1]], base=0,
                                 channel_multiplier=1).then_inc(sems["vg"], 1))
            INC("vg")
            W("v", "sd")
            W("v", "vg")

            def _v(h):
                h.memset(ones_col[:], 1.0)
                h.memset(ones_row[:], 1.0)
                h.memset(ones_rowf[:], 1.0)
                h.memset(onesq[:], 1.0)
                h.memset(onesq_b[:], 1.0)
                h.memset(nsc[:], 0.0)
                h.tensor_single_scalar(out=maskAB[:, 0:1], in_=iota_col[:], scalar=64, op=ALU.is_lt)
                h.tensor_single_scalar(out=maskAB[:, 1:2], in_=iota_col[:], scalar=64, op=ALU.is_ge)
                h.tensor_copy(out=vbrow[:], in_=arD_f[0:1, 0:1024]).then_inc(sems["vd"], 1)
            R.v(_v)
            INC("vd")
            W("a", "sd")
            R.a(lambda h: h.activation(out=A_sb[:], in_=A_sb.ap() if False else A_sb[:],
                                       func=AF.Exp).then_inc(sems["va"], 1))
            INC("va")
            W("v", "va")
            R.v(lambda h: h.tensor_single_scalar(out=A_sb[:], in_=A_sb[:],
                                                 scalar=-1.0, op=ALU.mult)
                .then_inc(sems["vd"], 1))
            INC("vd")

        def ln_rec(src_dram, dst_store, ntok, keep_store=None):
            nchunks = ntok // 512
            srcR = src_dram.rearrange("(j p) t -> p j t", p=128)
            for c in range(nchunks):
                st = stage[:, c % 2]
                W("s", "vd")
                R.s(lambda h, st=st, c=c: h.dma_start(
                    out=st[:], in_=srcR[:, :, 512 * c:512 * (c + 1)])
                    .then_inc(sems["sd"], 16))
                INC("sd", 16)
                W("a", "sd")
                W("a", "pe", C["pe"])

                def _a(h, st=st):
                    for jc in range(8):
                        h.activation(out=ln_xb[:, jc, :], in_=st[:, jc, :], func=AF.Identity)
                    for jc in range(8):
                        r = h.activation(out=ln_x2[:, jc, :], in_=st[:, jc, :], func=AF.Square)
                    r.then_inc(sems["va"], 1)
                R.a(_a)
                INC("va")
                W("t", "va")
                W("t", "vd", C["vd"])

                def _t(h):
                    for jc in range(8):
                        h.matmul(ps[3][0:1, 0:512], ones_col[:], ln_xb[:, jc, :],
                                 start=(jc == 0), stop=(jc == 7))
                    for jc in range(8):
                        r = h.matmul(ps[3][32:33, 512:1024], ones_col[:], ln_x2[:, jc, :],
                                     start=(jc == 0), stop=(jc == 7))
                    r.then_inc(sems["pe"], 1)
                R.t(_t)
                INC("pe")
                W("v", "pe")

                def _v1(h):
                    h.tensor_single_scalar(out=lns_ln[0:1, 0, :], in_=ps[3][0:1, 0:512],
                                           scalar=1.0 / D, op=ALU.mult)
                    h.tensor_single_scalar(out=lns_ln[0:1, 1, :], in_=ps[3][32:33, 512:1024],
                                           scalar=1.0 / D, op=ALU.mult)
                    h.tensor_mul(out=lns_ln[0:1, 2, :], in0=lns_ln[0:1, 0, :],
                                 in1=lns_ln[0:1, 0, :])
                    h.tensor_sub(out=lns_ln[0:1, 3, :], in0=lns_ln[0:1, 1, :],
                                 in1=lns_ln[0:1, 2, :])
                    r = h.tensor_single_scalar(out=lns_ln[0:1, 3, :], in_=lns_ln[0:1, 3, :],
                                               scalar=1e-5, op=ALU.add)
                    r.then_inc(sems["vd"], 1)
                R.v(_v1)
                INC("vd")
                W("a", "vd")
                R.a(lambda h: h.activation(out=lns_ln[0:1, 3, :], in_=lns_ln[0:1, 3, :],
                                           func=AF.Sqrt).then_inc(sems["va"], 1))
                INC("va")
                W("v", "va")
                R.v(lambda h: h.reciprocal(out=lns_ln[0:1, 1, :], in_=lns_ln[0:1, 3, :])
                    .then_inc(sems["vd"], 1))
                INC("vd")
                W("t", "vd")

                def _t2(h):
                    h.matmul(ps[2][:, 0:512], ones_rowf[:], lns_ln[0:1, 0, :],
                             start=True, stop=True)
                    h.matmul(ps[2][:, 512:1024], ones_rowf[:], lns_ln[0:1, 1, :],
                             start=True, stop=True).then_inc(sems["pe"], 1)
                R.t(_t2)
                INC("pe")
                W("v", "pe")

                def _v2(h, st=st, c=c):
                    for jc in range(8):
                        h.tensor_sub(out=tmp_ln[:], in0=st[:, jc, :], in1=ps[2][:, 0:512])
                        r = h.tensor_mul(out=dst_store(jc, c), in0=tmp_ln[:],
                                         in1=ps[2][:, 512:1024])
                        if keep_store is not None:
                            r = h.tensor_copy(out=keep_store(jc, c), in_=st[:, jc, :])
                    r.then_inc(sems["vd"], 1)
                R.v(_v2)
                INC("vd")

        def p1():
            ln_rec(xT_full, lambda jc, c: ln1[:, jc, 512 * c:512 * (c + 1)], S)

        def p2():
            for k in "qkvo":
                for jc in range(8):
                    st2 = stage[:, jc % 2, 0:2]
                    W("s", "va")
                    W("s", "vd")
                    off = 2 * D - 128 * jc
                    R.s(lambda h, off=off, k=k, st2=st2: h.dma_start(
                        out=st2[:],
                        in_=bass.AP(circ2[k].tensor, off, [[D - 1, 128], [1, D]]))
                        .then_inc(sems["sd"], 16))
                    INC("sd", 16)
                    W("a", "sd")
                    R.a(lambda h, k=k, jc=jc, st2=st2: h.activation(
                        out=CT[k][:, jc, :], in_=st2.rearrange("p a t -> p (a t)"),
                        func=AF.Identity).then_inc(sems["va"], 1))
                    INC("va")

        def p3():
            dP = 16384
            qT_h = qTv.tensor
            QOFF0 = qTv.offset

            prj = [ps[2][:, 0:512], ps[2][:, 512:1024],
                   ps[3][:, 0:512], ps[3][:, 512:1024]]
            prj_free = [C["va"]] * 4

            W("v", "vd", C["vd"])
            R.v(lambda h: h.memset(
                bass.AP(vtm.tensor, vtm.offset + 64, [[dP, 128], [130, 32], [65, 2]]),
                1.0).then_inc(sems["vd"], 1))
            INC("vd")
            vtm_ones_vd = C["vd"]

            for hp in range(8):
                for f in range(8):
                    slot = f % 4
                    W("t", "va", prj_free[slot])

                    def _t(h, hp=hp, f=f, slot=slot):
                        for jc in range(8):
                            r = h.matmul(prj[slot],
                                         CT["k"][:, jc, 128 * hp:128 * (hp + 1)],
                                         ln1[:, jc, 512 * f:512 * (f + 1)],
                                         start=(jc == 0), stop=(jc == 7))
                        r.then_inc(sems["pe"], 1)
                    R.t(_t)
                    INC("pe")
                    W("a", "pe")
                    R.a(lambda h, f=f, slot=slot, hp=hp: h.activation(
                        out=kTv[:, 512 * f:512 * (f + 1)], in_=prj[slot],
                        func=AF.Identity, bias=bias_k[:, hp:hp + 1])
                        .then_inc(sems["va"], 1))
                    INC("va")
                    prj_free[slot] = C["va"]

                for tt in range(32):
                    slot = tt % 4
                    W("t", "va", prj_free[slot])

                    def _t(h, hp=hp, tt=tt, slot=slot):
                        for jc in range(8):
                            h.matmul(prj[slot][:, 0:128],
                                     ln1[:, jc, 128 * tt:128 * (tt + 1)],
                                     CT["v"][:, jc, 128 * hp:128 * (hp + 1)],
                                     start=(jc == 0), stop=False)
                        r = h.matmul(prj[slot][:, 0:128], ones_row[:],
                                     vbrow[0:1, 128 * hp:128 * (hp + 1)],
                                     start=False, stop=True)
                        r.then_inc(sems["pe"], 1)
                    R.t(_t)
                    INC("pe")
                    W("a", "pe")

                    def _a(h, tt=tt, slot=slot):
                        h.activation(out=vtm[:, tt, 0:64], in_=prj[slot][:, 0:64],
                                     func=AF.Identity)
                        r = h.activation(out=vtm[:, tt, 65:129], in_=prj[slot][:, 64:128],
                                         func=AF.Identity)
                        r.then_inc(sems["va"], 1)
                    R.a(_a)
                    INC("va")
                    prj_free[slot] = C["va"]

                for f in range(4):
                    slot = f % 4
                    W("t", "va", prj_free[slot])

                    def _t(h, hp=hp, f=f, slot=slot):
                        for jc in range(8):
                            r = h.matmul(prj[slot],
                                         CT["q"][:, jc, 128 * hp:128 * (hp + 1)],
                                         ln1[:, jc, 512 * f:512 * (f + 1)],
                                         start=(jc == 0), stop=(jc == 7))
                        r.then_inc(sems["pe"], 1)
                    R.t(_t)
                    INC("pe")
                    W("a", "pe")
                    R.a(lambda h, f=f, slot=slot, hp=hp: h.activation(
                        out=qTv[:, 512 * f:512 * (f + 1)], in_=prj[slot],
                        func=AF.Identity, bias=bias_q[:, hp:hp + 1])
                        .then_inc(sems["va"], 1))
                    INC("va")
                    prj_free[slot] = C["va"]

                sqscr = pT.rearrange("p r t -> p (r t)")
                qacc = carry[0:2, 0, 0:1]
                kacc = carry[0:2, 0, 1:2]
                ntmp = carry[0:2, 0, 2:3]
                negM2 = carry[0:2, 0, 3:4]
                qmaxs = carry[0:2, 1, 0:4]
                kmaxs = carry[0:2, 1, 4:12]
                for (srcv, nch, maxs, acc) in ((qTv, 4, qmaxs, qacc),
                                               (kTv, 8, kmaxs, kacc)):
                    W("a", "vd", C["vd"])
                    R.a(lambda h, srcv=srcv, nch=nch: h.activation(
                        out=sqscr[:, 0:512 * nch], in_=srcv, func=AF.Square)
                        .then_inc(sems["va"], 1))
                    INC("va")
                    W("t", "va")
                    for ch in range(nch):
                        pst = ps[0] if ch % 2 == 0 else ps[1]
                        W("t", "vd")
                        R.t(lambda h, ch=ch, pst=pst: h.matmul(
                            pst[0:2, 0:512], maskAB[:],
                            sqscr[:, 512 * ch:512 * (ch + 1)],
                            start=True, stop=True).then_inc(sems["pe"], 1))
                        INC("pe")
                        W("v", "pe")
                        R.v(lambda h, ch=ch, pst=pst, maxs=maxs: h.reduce_max(
                            out=maxs[:, ch:ch + 1], in_=pst[0:2, 0:512],
                            axis=mybir.AxisListType.X).then_inc(sems["vd"], 1))
                        INC("vd")
                    W("v", "vd")
                    R.v(lambda h, maxs=maxs, acc=acc, nch=nch: h.reduce_max(
                        out=acc, in_=maxs[:, 0:nch],
                        axis=mybir.AxisListType.X).then_inc(sems["vd"], 1))
                    INC("vd")
                W("v", "vd")
                R.v(lambda h: h.tensor_mul(out=ntmp, in0=qacc, in1=kacc)
                    .then_inc(sems["vd"], 1))
                INC("vd")
                W("a", "vd")
                R.a(lambda h: h.activation(out=ntmp, in_=ntmp, func=AF.Sqrt)
                    .then_inc(sems["va"], 1))
                INC("va")
                W("v", "va")
                R.v(lambda h: h.tensor_single_scalar(
                    out=negM2, in_=ntmp, scalar=-0.125 * 1.02, op=ALU.mult)
                    .then_inc(sems["vd"], 1))
                INC("vd")
                W("s", "vd")
                R.s(lambda h: h.dma_start(out=negMrow[0:1, 0:2], in_=negM2)
                    .then_inc(sems["sd"], 16))
                INC("sd", 16)
                W("t", "sd")
                R.t(lambda h: h.matmul(ps[0][:, 0:2], ones_rowf[:], negMrow[0:1, 0:2],
                                       start=True, stop=True).then_inc(sems["pe"], 1))
                INC("pe")
                W("a", "pe")
                R.a(lambda h: h.activation(out=negMb[:], in_=ps[0][:, 0:2],
                                           func=AF.Identity).then_inc(sems["va"], 1))
                INC("va")

                if DBG and hp == 7:
                    W("g", "va", C["va"])
                    W("g", "vd", C["vd"])
                    def _gd(h):
                        h.dma_start(out=dbg_kT, in_=kTv).then_inc(sems["so"], 16)
                        h.dma_start(out=dbg_qT, in_=qTv).then_inc(sems["so"], 16)
                        h.dma_start(out=dbg_vtm, in_=vtm.rearrange("p a b -> p (a b)")).then_inc(sems["so"], 16)
                        h.dma_start(out=dbg_neg, in_=negMb[:]).then_inc(sems["so"], 16)
                        h.dma_start(out=dbg_sc, in_=carry[0:2, :, :].rearrange("p a b -> p (a b)")).then_inc(sems["so"], 16)
                        h.dma_start(out=dbg_m, in_=maskAB[:]).then_inc(sems["so"], 16)
                        h.dma_start(out=dbg_sq, in_=pT.rearrange("p r t -> p (r t)")).then_inc(sems["so"], 16)
                    R.g(_gd)
                    INC("so", 112)
                    W("t", "so", C["so"])
                W("t", "vd", vtm_ones_vd)
                for qc in range(2):
                    for kc in range(32):
                        W("t", "va")

                        def _t(h, qc=qc, kc=kc):
                            for fh in range(2):
                                for head in (0, 1):
                                    pst = ps[0] if head == 0 else ps[1]
                                    rhs = bass.AP(
                                        qT_h,
                                        64 * head * dP + QOFF0 + 1024 * qc + 512 * fh,
                                        [[dP, 64], [1, 512]])
                                    r = h.matmul(
                                        pst[:, 512 * fh:512 * (fh + 1)],
                                        kTv[64 * head:64 * (head + 1),
                                            128 * kc:128 * (kc + 1)],
                                        rhs, start=True, stop=True,
                                        tile_position=(64 * head, 0))
                            r.then_inc(sems["pe"], 1)
                        R.t(_t)
                        INC("pe")
                        W("a", "pe")

                        def _a(h, kc=kc):
                            h.activation(out=pT[:, 2 * (kc % 2) + 0, :], in_=ps[0][:],
                                         func=AF.Exp, scale=0.125, bias=negMb[:, 0:1])
                            r = h.activation(out=pT[:, 2 * (kc % 2) + 1, :], in_=ps[1][:],
                                             func=AF.Exp, scale=0.125, bias=negMb[:, 1:2])
                            r.then_inc(sems["va"], 1)
                        R.a(_a)
                        INC("va")
                        W("t", "va")

                        def _t2(h, kc=kc):
                            for head in (0, 1):
                                pst = ps[2] if head == 0 else ps[3]
                                for fh in range(2):
                                    r = h.matmul(
                                        pst[0:65, 512 * fh:512 * (fh + 1)],
                                        vtm[:, kc, 65 * head:65 * head + 65],
                                        pT[:, 2 * (kc % 2) + head,
                                           512 * fh:512 * (fh + 1)],
                                        start=(kc == 0), stop=(kc == 31))
                            r.then_inc(sems["pe"], 1)
                        R.t(_t2)
                        INC("pe")

                    W("v", "pe")

                    def _v(h):
                        h.reciprocal(out=lns_at[0:1, 0, :], in_=ps[2][64:65, 0:512])
                        h.reciprocal(out=lns_at[0:1, 1, :], in_=ps[2][64:65, 512:1024])
                        h.reciprocal(out=lns_at2[0:1, 0, :], in_=ps[3][64:65, 0:512])
                        r = h.reciprocal(out=lns_at2[0:1, 1, :], in_=ps[3][64:65, 512:1024])
                        r.then_inc(sems["vd"], 1)
                    R.v(_v)
                    INC("vd")
                    W("t", "vd")

                    def _t3(h):
                        h.matmul(ps[0][0:64, 0:512], ones_rowf[0:1, 0:64],
                                 lns_at[0:1, 0, :], start=True, stop=True)
                        h.matmul(ps[0][0:64, 512:1024], ones_rowf[0:1, 0:64],
                                 lns_at[0:1, 1, :], start=True, stop=True)
                        h.matmul(ps[1][0:64, 0:512], ones_rowf[0:1, 0:64],
                                 lns_at2[0:1, 0, :], start=True, stop=True)
                        r = h.matmul(ps[1][0:64, 512:1024], ones_rowf[0:1, 0:64],
                                     lns_at2[0:1, 1, :], start=True, stop=True)
                        r.then_inc(sems["pe"], 1)
                    R.t(_t3)
                    INC("pe")
                    W("a", "pe")

                    def _a2(h):
                        h.activation(out=rb64[0:64, :], in_=ps[0][0:64, :], func=AF.Identity)
                        r = h.activation(out=rb64[64:128, :], in_=ps[1][0:64, :],
                                         func=AF.Identity)
                        r.then_inc(sems["va"], 1)
                    R.a(_a2)
                    INC("va")
                    W("v", "va")

                    def _v2(h, hp=hp, qc=qc):
                        h.tensor_mul(out=om[0:64, hp, 1024 * qc:1024 * (qc + 1)],
                                     in0=ps[2][0:64, :], in1=rb64[0:64, :])
                        r = h.tensor_mul(out=om[64:128, hp, 1024 * qc:1024 * (qc + 1)],
                                         in0=ps[3][0:64, :], in1=rb64[64:128, :])
                        r.then_inc(sems["vd"], 1)
                    R.v(_v2)
                    INC("vd")
                    W("t", "vd")

            if DBG:
                W("g", "vd", C["vd"])
                def _gd2(h):
                    h.dma_start(out=dbg_om.rearrange("(j p) t -> p j t", p=128),
                                in_=om[:]).then_inc(sems["so"], 16)
                R.g(_gd2)
                INC("so", 16)
            W("t", "vd")
            x_ownR = x_ownT.rearrange("(j p) t -> p j t", p=128)
            x1R = x1_d.rearrange("(j p) t -> p j t", p=128)
            outR = outT.rearrange("(j p) t -> p j t", p=128)
            for mt in range(8):
                for f in range(4):
                    pst = ps[0] if (mt * 4 + f) % 2 == 0 else ps[1]
                    W("s", "vd", C["vd"])
                    R.s(lambda h, mt=mt, f=f: h.dma_start(
                        out=x0c[:], in_=x_ownR[:, mt, 512 * f:512 * (f + 1)])
                        .then_inc(sems["sd"], 16))
                    INC("sd", 16)
                    W("t", "va", C["va"])

                    def _t(h, mt=mt, f=f, pst=pst):
                        for jc in range(8):
                            r = h.matmul(pst[:, 0:512],
                                         CT["o"][:, jc, 128 * mt:128 * (mt + 1)],
                                         om[:, jc, 512 * f:512 * (f + 1)],
                                         start=(jc == 0), stop=(jc == 7))
                        r.then_inc(sems["pe"], 1)
                    R.t(_t)
                    INC("pe")
                    W("a", "pe")
                    W("a", "vd", C["vd"])
                    R.a(lambda h, mt=mt, pst=pst: h.activation(
                        out=tmpo[:], in_=pst[:, 0:512], func=AF.Identity,
                        bias=bias_o[:, mt:mt + 1]).then_inc(sems["va"], 1))
                    INC("va")
                    W("v", "va")
                    W("v", "sd")
                    R.v(lambda h: h.tensor_add(out=x1c[:], in0=x0c[:], in1=tmpo[:])
                        .then_inc(sems["vd"], 1))
                    INC("vd")
                    W("g", "vd")

                    def _g(h, mt=mt, f=f):
                        r = h.dma_start(out=x1R[:, mt, 512 * f:512 * (f + 1)], in_=x1c[:])
                        r.then_inc(sems["sg"], 16)
                        if STAGE == 1:
                            h.dma_start(out=outR[:, mt, 512 * f:512 * (f + 1)],
                                        in_=x1c[:]).then_inc(sems["so"], 16)
                    R.g(_g)
                    INC("sg", 16)
                    if STAGE == 1:
                        INC("so", 16)
                    W("v", "sg", C["sg"])

        s1T = arA_b.rearrange("p (j t) -> p j t", j=8)
        s2T = arB_b.rearrange("p (j t) -> p j t", j=32)
        w1R = w1T_in.rearrange("(j p) t -> p j t", p=128)
        w2R = w2T_in.rearrange("(j p) t -> p j t", p=128)
        arC_f = arC.ap()

        def p4():
            for jc in range(8):
                stg = arC_f[:, 4096 * (jc % 2):4096 * (jc % 2) + 4096]
                W("s", "va", C["va"])
                R.s(lambda h, jc=jc, stg=stg: h.dma_start(
                    out=stg, in_=w1R[:, jc, :]).then_inc(sems["sd"], 16))
                INC("sd", 16)
                W("a", "sd")
                R.a(lambda h, jc=jc, stg=stg: h.activation(
                    out=s1T[:, jc, :], in_=stg, func=AF.Sign).then_inc(sems["va"], 1))
                INC("va")
            for hc in range(32):
                stg = arC_f[:, 4096 * (hc % 2):4096 * (hc % 2) + 1024]
                W("s", "va", C["va"])
                R.s(lambda h, hc=hc, stg=stg: h.dma_start(
                    out=stg, in_=w2R[:, hc, :]).then_inc(sems["sd"], 16))
                INC("sd", 16)
                W("a", "sd")
                R.a(lambda h, hc=hc, stg=stg: h.activation(
                    out=s2T[:, hc, :], in_=stg, func=AF.Sign).then_inc(sems["va"], 1))
                INC("va")
            w1NR = w1N_in.rearrange("(m p) t -> p m t", p=128)
            w2NR = w2N_in.rearrange("(m p) t -> p m t", p=128)
            for (wR, ntile, alout, dd) in ((w1NR, 32, alpha1, D), (w2NR, 8, alpha2, HDIM)):
                for Mt in range(ntile):
                    stg = arC_f[:, dd * (Mt % 2):dd * (Mt % 2) + dd]
                    W("s", "vd", C["vd"])
                    W("s", "va", C["va"])
                    R.s(lambda h, wR=wR, Mt=Mt, stg=stg: h.dma_start(
                        out=stg, in_=wR[:, Mt, :]).then_inc(sems["sd"], 16))
                    INC("sd", 16)
                    W("a", "sd")
                    W("a", "vd", C["vd"])
                    absdst = arD_f[:, 0:dd]
                    R.a(lambda h, stg=stg, absdst=absdst, dd=dd: h.activation(
                        out=absdst, in_=stg, func=AF.Abs, scale=1.0 / dd)
                        .then_inc(sems["va"], 1))
                    INC("va")
                    W("v", "va")

                    def _v(h, absdst=absdst, alout=alout, Mt=Mt):
                        r = h.tensor_reduce(out=alout[:, Mt:Mt + 1], in_=absdst,
                                            axis=mybir.AxisListType.X, op=ALU.add)
                        r.then_inc(sems["vd"], 1)
                    R.v(_v)
                    INC("vd")
            if DBG:
                W("g", "vd", C["vd"])
                R.g(lambda h: h.dma_start(out=dbg_al[:, 0:32], in_=alpha1[:])
                    .then_inc(sems["so"], 16))
                INC("so", 16)
                W("v", "so", C["so"])
            W("g", "va", C["va"])
            s1R = s1T_d.rearrange("(j p) t -> p j t", p=128)
            s2R = s2T_d.rearrange("(j p) t -> p j t", p=128)

            def _g(h):
                h.dma_start(out=s1R[:, :, :], in_=s1T[:]).then_inc(sems["sg"], 16)
                h.dma_start(out=s2R[:, :, :], in_=s2T[:]).then_inc(sems["sg"], 16)
            R.g(_g)
            INC("sg", 32)

        xbuf = arD_f[:, 0:4096].rearrange("p (j t) -> p j t", j=8)
        lnsF = arD_f[:, 4096:5120].rearrange("p (k t) -> p k t", k=2)
        tmpo2 = arD_f[:, 5120:5632]
        ln2c = arD_b[:, 11264:15360].rearrange("p (j t) -> p j t", j=8)
        castb = arD_b[:, 15360:15872]
        sqb = arD_b[:, 15872:16384]

        def ffn_chunks(xsrc_d, xdst_d, also_out):
            xsR = xsrc_d.rearrange("(j p) t -> p j t", p=128)
            xdR = xdst_d.rearrange("(j p) t -> p j t", p=128)
            outR2 = outT.rearrange("(j p) t -> p j t", p=128)
            W("s", "sg", C["sg"])
            for c in range(4):
                W("s", "vd", C["vd"])
                W("s", "sg", C["sg"])
                R.s(lambda h, c=c: h.dma_start(
                    out=xbuf[:], in_=xsR[:, :, 512 * c:512 * (c + 1)])
                    .then_inc(sems["sd"], 16))
                INC("sd", 16)
                W("a", "sd")
                for jc in range(8):
                    if jc > 0:
                        W("a", "pe", C["pe"])

                    def _a(h, jc=jc):
                        h.activation(out=castb, in_=xbuf[:, jc, :], func=AF.Identity)
                        r = h.activation(out=sqb, in_=xbuf[:, jc, :], func=AF.Square)
                        r.then_inc(sems["va"], 1)
                    R.a(_a)
                    INC("va")
                    W("t", "va")

                    def _t(h, jc=jc):
                        h.matmul(ps[3][0:1, 0:512], ones_col[:], castb,
                                 start=(jc == 0), stop=(jc == 7))
                        r = h.matmul(ps[3][0:1, 512:1024], ones_col[:], sqb,
                                     start=(jc == 0), stop=(jc == 7))
                        r.then_inc(sems["pe"], 1)
                    R.t(_t)
                    INC("pe")
                W("v", "pe")

                def _v1(h):
                    h.tensor_single_scalar(out=lnsF[0:1, 0, :], in_=ps[3][0:1, 0:512],
                                           scalar=1.0 / D, op=ALU.mult)
                    h.tensor_single_scalar(out=tmpo2[0:1, :], in_=ps[3][0:1, 512:1024],
                                           scalar=1.0 / D, op=ALU.mult)
                    h.tensor_mul(out=castb[0:1, :], in0=lnsF[0:1, 0, :],
                                 in1=lnsF[0:1, 0, :])
                    h.tensor_sub(out=tmpo2[0:1, :], in0=tmpo2[0:1, :],
                                 in1=castb[0:1, :])
                    r = h.tensor_single_scalar(out=tmpo2[0:1, :],
                                               in_=tmpo2[0:1, :],
                                               scalar=1e-5, op=ALU.add)
                    r.then_inc(sems["vd"], 1)
                R.v(_v1)
                INC("vd")
                W("a", "vd")
                R.a(lambda h: h.activation(out=tmpo2[0:1, :],
                                           in_=tmpo2[0:1, :],
                                           func=AF.Sqrt).then_inc(sems["va"], 1))
                INC("va")
                W("v", "va")
                R.v(lambda h: h.reciprocal(out=lnsF[0:1, 1, :], in_=tmpo2[0:1, :])
                    .then_inc(sems["vd"], 1))
                INC("vd")
                W("t", "vd")

                def _t2(h):
                    h.matmul(ps[2][:, 0:512], ones_rowf[:], lnsF[0:1, 0, :],
                             start=True, stop=True)
                    r = h.matmul(ps[2][:, 512:1024], ones_rowf[:], lnsF[0:1, 1, :],
                                 start=True, stop=True)
                    r.then_inc(sems["pe"], 1)
                R.t(_t2)
                INC("pe")
                W("v", "pe")

                def _v2(h):
                    for jc in range(8):
                        h.tensor_sub(out=castb, in0=xbuf[:, jc, :], in1=ps[2][:, 0:512])
                        r = h.tensor_mul(out=ln2c[:, jc, :], in0=castb,
                                         in1=ps[2][:, 512:1024])
                    r.then_inc(sems["vd"], 1)
                R.v(_v2)
                INC("vd")
                if DBG and also_out and c == 0:
                    W("g", "vd", C["vd"])
                    def _gdf(h):
                        h.dma_start(out=dbg_ln2, in_=ln2c[:]).then_inc(sems["so"], 16)
                        h.dma_start(out=dbg_xb, in_=xbuf[:]).then_inc(sems["so"], 16)
                        al = dbg_al
                        h.dma_start(out=al[:, 32:40], in_=alpha2[:]).then_inc(sems["so"], 16)
                        h.dma_start(out=al[:, 40:44], in_=b2cols[:, 0:4]).then_inc(sems["so"], 16)
                        h.dma_start(out=al[:, 44:48], in_=bias_q[:, 0:4]).then_inc(sems["so"], 16)
                    R.g(_gdf)
                    INC("so", 80)
                    W("v", "so", C["so"])
                h_bf = arC_b.rearrange("p (m t) -> p m t", m=32)
                W("t", "vd")
                va_g = {}
                va_mm1_0 = C["va"]
                for Mt in range(32):
                    pst = ps[0] if Mt % 2 == 0 else ps[1]
                    W("t", "va", va_g.get(Mt - 2, va_mm1_0))

                    def _t3(h, Mt=Mt, pst=pst):
                        for jc in range(8):
                            r = h.matmul(pst[:, 0:512], s1T[:, jc, 128 * Mt:128 * (Mt + 1)],
                                         ln2c[:, jc, :], start=(jc == 0), stop=(jc == 7))
                        r.then_inc(sems["pe"], 1)
                    R.t(_t3)
                    INC("pe")
                    W("a", "pe")
                    R.a(lambda h, Mt=Mt, pst=pst: h.activation(
                        out=h_bf[:, Mt, :], in_=pst[:, 0:512], func=AF.Gelu,
                        scale=alpha1[:, Mt:Mt + 1], bias=b1cols[:, Mt:Mt + 1])
                        .then_inc(sems["va"], 1))
                    INC("va")
                    va_g[Mt] = C["va"]
                if DBG and also_out and c == 0:
                    W("g", "va", C["va"])
                    R.g(lambda h: h.dma_start(out=dbg_h, in_=h_bf[:, 0:4, :])
                        .then_inc(sems["so"], 16))
                    INC("so", 16)
                    W("t", "so", C["so"])
                tmpoB = lnsF[:, 0, :]
                va_e = {}
                vd_a = {}
                va_mm2_0 = C["va"]
                vd_mm2_0 = C["vd"]
                for Mt in range(8):
                    pst = ps[0] if Mt % 2 == 0 else ps[1]
                    tob = tmpo2 if Mt % 2 == 0 else tmpoB
                    W("t", "va", va_e.get(Mt - 2, va_mm2_0))

                    def _t4(h, Mt=Mt, pst=pst):
                        for hc in range(32):
                            r = h.matmul(pst[:, 0:512], s2T[:, hc, 128 * Mt:128 * (Mt + 1)],
                                         h_bf[:, hc, :], start=(hc == 0), stop=(hc == 31))
                        r.then_inc(sems["pe"], 1)
                    R.t(_t4)
                    INC("pe")
                    W("a", "pe")
                    W("a", "vd", vd_a.get(Mt - 2, vd_mm2_0))
                    R.a(lambda h, Mt=Mt, pst=pst, tob=tob: h.activation(
                        out=tob, in_=pst[:, 0:512], func=AF.Identity,
                        scale=alpha2[:, Mt:Mt + 1], bias=b2cols[:, Mt:Mt + 1])
                        .then_inc(sems["va"], 1))
                    INC("va")
                    va_e[Mt] = C["va"]
                    W("v", "va")
                    R.v(lambda h, Mt=Mt, tob=tob: h.tensor_add(
                        out=xbuf[:, Mt, :], in0=xbuf[:, Mt, :], in1=tob)
                        .then_inc(sems["vd"], 1))
                    INC("vd")
                    vd_a[Mt] = C["vd"]
                    W("g", "vd")

                    def _g(h, Mt=Mt, c=c):
                        r = h.dma_start(out=xdR[:, Mt, 512 * c:512 * (c + 1)],
                                        in_=xbuf[:, Mt, :])
                        r.then_inc(sems["sg"], 16)
                        if also_out:
                            h.dma_start(out=outR2[:, Mt, 512 * c:512 * (c + 1)],
                                        in_=xbuf[:, Mt, :]).then_inc(sems["so"], 16)
                    R.g(_g)
                    INC("sg", 16)
                    if also_out:
                        INC("so", 16)
                    W("v", "sg", C["sg"])

        def p6():
            xsR = x2_d.rearrange("(j p) t -> p j t", p=128)
            l3R = ln3_loc.rearrange("(j p) t -> p j t", p=128)
            W("s", "sg", C["sg"])
            for c in range(4):
                W("s", "vd", C["vd"])
                W("s", "sg", C["sg"])
                R.s(lambda h, c=c: h.dma_start(
                    out=xbuf[:], in_=xsR[:, :, 512 * c:512 * (c + 1)])
                    .then_inc(sems["sd"], 16))
                INC("sd", 16)
                W("a", "sd")
                for jc in range(8):
                    if jc > 0:
                        W("a", "pe", C["pe"])

                    def _a(h, jc=jc):
                        h.activation(out=castb, in_=xbuf[:, jc, :], func=AF.Identity)
                        r = h.activation(out=sqb, in_=xbuf[:, jc, :], func=AF.Square)
                        r.then_inc(sems["va"], 1)
                    R.a(_a)
                    INC("va")
                    W("t", "va")

                    def _t(h, jc=jc):
                        h.matmul(ps[3][0:1, 0:512], ones_col[:], castb,
                                 start=(jc == 0), stop=(jc == 7))
                        r = h.matmul(ps[3][0:1, 512:1024], ones_col[:], sqb,
                                     start=(jc == 0), stop=(jc == 7))
                        r.then_inc(sems["pe"], 1)
                    R.t(_t)
                    INC("pe")
                W("v", "pe")

                def _v1(h):
                    h.tensor_single_scalar(out=lnsF[0:1, 0, :], in_=ps[3][0:1, 0:512],
                                           scalar=1.0 / D, op=ALU.mult)
                    h.tensor_single_scalar(out=tmpo2[0:1, :], in_=ps[3][0:1, 512:1024],
                                           scalar=1.0 / D, op=ALU.mult)
                    h.tensor_mul(out=castb[0:1, :], in0=lnsF[0:1, 0, :],
                                 in1=lnsF[0:1, 0, :])
                    h.tensor_sub(out=tmpo2[0:1, :], in0=tmpo2[0:1, :], in1=castb[0:1, :])
                    r = h.tensor_single_scalar(out=tmpo2[0:1, :], in_=tmpo2[0:1, :],
                                               scalar=1e-5, op=ALU.add)
                    r.then_inc(sems["vd"], 1)
                R.v(_v1)
                INC("vd")
                W("a", "vd")
                R.a(lambda h: h.activation(out=tmpo2[0:1, :], in_=tmpo2[0:1, :],
                                           func=AF.Sqrt).then_inc(sems["va"], 1))
                INC("va")
                W("v", "va")
                R.v(lambda h: h.reciprocal(out=lnsF[0:1, 1, :], in_=tmpo2[0:1, :])
                    .then_inc(sems["vd"], 1))
                INC("vd")
                W("t", "vd")

                def _t2(h):
                    h.matmul(ps[2][:, 0:512], ones_rowf[:], lnsF[0:1, 0, :],
                             start=True, stop=True)
                    r = h.matmul(ps[2][:, 512:1024], ones_rowf[:], lnsF[0:1, 1, :],
                                 start=True, stop=True)
                    r.then_inc(sems["pe"], 1)
                R.t(_t2)
                INC("pe")
                W("v", "pe")

                def _v2(h):
                    for jc in range(8):
                        h.tensor_sub(out=castb, in0=xbuf[:, jc, :], in1=ps[2][:, 0:512])
                        r = h.tensor_mul(out=ln2c[:, jc, :], in0=castb,
                                         in1=ps[2][:, 512:1024])
                    r.then_inc(sems["vd"], 1)
                R.v(_v2)
                INC("vd")
                W("g", "vd")
                R.g(lambda h, c=c: h.dma_start(
                    out=l3R[:, :, 512 * c:512 * (c + 1)], in_=ln2c[:])
                    .then_inc(sems["sg"], 16))
                INC("sg", 16)
                W("v", "sg", C["sg"])
            W("g", "sg", C["sg"])
            R.g(lambda h: h.collective_compute(
                "AllGather", ALU.bypass, replica_groups=RG,
                ins=[ln3_loc], outs=[ln3_all]).then_inc(sems["vc"], 1))
            INC("vc")

        def p7():
            for (wsrc, off) in ((w_inT_x, 0), (w_inT_z, 256)):
                wr = wsrc.rearrange("(j p) e -> p j e", p=128)
                W("s", "va", C["va"])
                R.s(lambda h, wr=wr: h.dma_start(
                    out=arC_f[:, 0:2048].rearrange("p (j e) -> p j e", j=8),
                    in_=wr).then_inc(sems["sd"], 16))
                INC("sd", 16)
                W("a", "sd")

                def _a(h, off=off):
                    src2 = arC_f[:, 0:2048].rearrange("p (j e) -> p j e", j=8)
                    for jc in range(8):
                        r = h.activation(out=wioT[:, jc, off:off + 256],
                                         in_=src2[:, jc, :], func=AF.Identity)
                    r.then_inc(sems["va"], 1)
                R.a(_a)
                INC("va")
            wxr = wxT_sl.rearrange("(k p) n -> p k n", p=128)
            W("s", "va", C["va"])
            R.s(lambda h: h.dma_start(
                out=arC_f[:, 0:66].rearrange("p (k n) -> p k n", k=2), in_=wxr)
                .then_inc(sems["sd"], 16))
            INC("sd", 16)
            W("a", "sd")

            def _awx(h):
                src2 = arC_f[:, 0:66].rearrange("p (k n) -> p k n", k=2)
                h.activation(out=wxTb[:, 0, :], in_=src2[:, 0, :], func=AF.Identity)
                r = h.activation(out=wxTb[:, 1, :], in_=src2[:, 1, :], func=AF.Identity)
                r.then_inc(sems["va"], 1)
            R.a(_awx)
            INC("va")

            lnall = ln3_all.rearrange("(bk j p) t -> bk p j t", p=128, j=8)
            uR = u_d.rearrange("(k p) t -> p k t", p=128)
            zR = zs_d.rearrange("(k p) t -> p k t", p=128)
            bpR = bcd_part
            rhsT = arA_b[:, 0:16384].rearrange("p (j t) -> p j t", j=8)
            ublk = arC_f[:, 0:4102].rearrange("p (k t) -> p k t", k=2)
            zblk = arB.ap()[:, 0:4096].rearrange("p (k t) -> p k t", k=2)
            uS = arA.ap()[:, 8192:12288].rearrange("p (k t) -> p k t", k=2)
            halo = arA.ap()[:, 12288:12294].rearrange("p (k t) -> p k t", k=2)
            xpev = arA.ap()[:, 12300:14348][0:33, :]
            W("v", "vc", C["vc"])
            for tcb in range(8):
                seq_start = (tcb % 2 == 0)
                W("s", "vc", C["vc"])
                W("s", "va", C["va"])
                W("s", "pe", C["pe"])
                R.s(lambda h, tcb=tcb: h.dma_start(
                    out=rhsT[:], in_=lnall[tcb]).then_inc(sems["sd"], 16))
                INC("sd", 16)
                for th in range(2):
                    W("t", "sd")
                    W("t", "va", C["va"])

                    def _t(h, th=th):
                        for jc in range(8):
                            for Mt in range(4):
                                for tc in range(2):
                                    r = h.matmul(
                                        ps[Mt][:, 512 * tc:512 * (tc + 1)],
                                        wioT[:, jc, 128 * Mt:128 * (Mt + 1)],
                                        rhsT[:, jc, 1024 * th + 512 * tc:
                                             1024 * th + 512 * tc + 512],
                                        start=(jc == 0), stop=(jc == 7))
                        r.then_inc(sems["pe"], 1)
                    R.t(_t)
                    INC("pe")
                    W("a", "pe")

                    def _a(h, th=th):
                        for Mt in range(2):
                            h.activation(out=ublk[:, Mt, 3 + 1024 * th:3 + 1024 * th + 1024],
                                         in_=ps[Mt][:, 0:1024], func=AF.Identity)
                        for Mt in range(2):
                            r = h.activation(out=zblk[:, Mt, 1024 * th:1024 * th + 1024],
                                             in_=ps[Mt + 2][:, 0:1024], func=AF.Silu)
                        r.then_inc(sems["va"], 1)
                    R.a(_a)
                    INC("va")
                W("v", "va")
                if seq_start:
                    R.v(lambda h: h.memset(ublk[:, :, 0:3], 0.0).then_inc(sems["vd"], 1))
                else:
                    R.v(lambda h: h.tensor_copy(out=ublk[:, :, 0:3], in_=halo[:])
                        .then_inc(sems["vd"], 1))
                INC("vd")
                W("v", "vd")

                def _vc(h):
                    for k in range(2):
                        h.tensor_scalar(out=uS[:, k, :], in0=ublk[:, k, 3:2051],
                                        scalar1=conv_sb[:, k, 3:4],
                                        scalar2=convb_sb[:, k:k + 1],
                                        op0=ALU.mult, op1=ALU.add)
                        for sh in (1, 2, 3):
                            r = h.scalar_tensor_tensor(
                                out=uS[:, k, :], in0=ublk[:, k, 3 - sh:2051 - sh],
                                scalar=conv_sb[:, k, 3 - sh:4 - sh],
                                in1=uS[:, k, :], op0=ALU.mult, op1=ALU.add)
                    r = h.tensor_copy(out=halo[:], in_=ublk[:, :, 2048:2051])
                    r.then_inc(sems["vd"], 1)
                R.v(_vc)
                INC("vd")
                W("a", "vd")
                R.a(lambda h: h.activation(out=uS[:, :, :].rearrange("p a b -> p (a b)"),
                                           in_=uS[:, :, :].rearrange("p a b -> p (a b)"),
                                           func=AF.Silu).then_inc(sems["va"], 1))
                INC("va")
                ubf = arA_b[:, 28672:32768].rearrange("p (k t) -> p k t", k=2)
                W("v", "va")
                R.v(lambda h: h.tensor_copy(out=ubf[:, :, :], in_=uS[:, :, :])
                    .then_inc(sems["vd"], 1))
                INC("vd")
                for tc in range(4):
                    W("t", "vd", C["vd"])
                    W("t", "va", C["va"])

                    def _tx(h, tc=tc):
                        for k in range(2):
                            r = h.matmul(ps[0][0:33, 512 * (tc % 2):512 * (tc % 2) + 512],
                                         wxTb[:, k, :],
                                         ubf[:, k, 512 * tc:512 * (tc + 1)],
                                         start=(k == 0), stop=(k == 1))
                        r.then_inc(sems["pe"], 1)
                    R.t(_tx)
                    INC("pe")
                    W("a", "pe")
                    R.a(lambda h, tc=tc: h.activation(
                        out=xpev[:, 512 * tc:512 * (tc + 1)],
                        in_=ps[0][0:33, 512 * (tc % 2):512 * (tc % 2) + 512],
                        func=AF.Identity).then_inc(sems["va"], 1))
                    INC("va")
                W("g", "va", C["va"])
                W("g", "vd", C["vd"])

                def _g(h, tcb=tcb):
                    h.dma_start(out=uR[:, :, 2048 * tcb:2048 * (tcb + 1)],
                                in_=uS[:, :, :]).then_inc(sems["sg"], 16)
                    h.dma_start(out=zR[:, :, 2048 * tcb:2048 * (tcb + 1)],
                                in_=zblk[:, :, :]).then_inc(sems["sg"], 16)
                    h.dma_start(out=bpR[:, 2048 * tcb:2048 * (tcb + 1)],
                                in_=xpev[:]).then_inc(sems["sg"], 16)
                R.g(_g)
                INC("sg", 48)
                W("a", "sg", C["sg"])
                W("v", "sg", C["sg"])
            W("g", "sg", C["sg"])
            R.g(lambda h: h.collective_compute(
                "AllReduce", ALU.add, replica_groups=RG,
                ins=[bcd_part], outs=[bcd_all]).then_inc(sems["vc"], 1))
            INC("vc")

        def p8():
            gR = gated_loc.rearrange("(k p) t -> p k t", p=128)
            uR = u_d.rearrange("(k p) t -> p k t", p=128)
            zR = zs_d.rearrange("(k p) t -> p k t", p=128)
            bcdc = arB.ap()[0:33, 0:2048]
            slotA = arB.ap()[0:1, 14336:16384]
            arBb = arB.ap().bitcast(BF16)
            bcdb = arBb[0:33, 4096:6144]
            sl_b = [arBb[0:1, 28672:30720], arBb[32:33, 28672:30720]]
            sl_c = [arBb[64:65, 28672:30720], arBb[0:1, 30720:32768]]
            ob_b = [None, None]
            ps_bB = [None, None]
            ps_bC = [None, None]
            scr1 = arB.ap()[:, 2048:6144].rearrange("p (k t) -> p k t", k=2)
            scr2 = arB.ap()[:, 6144:10240].rearrange("p (k t) -> p k t", k=2)
            tmpy = arB.ap()[:, 10240:14336].rearrange("p (k t) -> p k t", k=2)
            dt_v = arD_f[:, 0:4096].rearrange("p (k t) -> p k t", k=2)
            dtu_v = arD_f[:, 4096:8192].rearrange("p (k t) -> p k t", k=2)
            dA_v = arC_f[:, 0:4096].rearrange("p (k t) -> p k t", k=2)
            dA_v2 = arC_f[:, 4096:8192].rearrange("p (k t) -> p k t", k=2)
            hbuf = arB.ap()[:, 6144:10240].rearrange("p (k t) -> p k t", k=2)
            uc_v = arA.ap()[:, 0:4096].rearrange("p (k t) -> p k t", k=2)
            zc_v = arA.ap()[:, 4096:8192].rearrange("p (k t) -> p k t", k=2)
            yacc = arA.ap()[:, 8192:12288].rearrange("p (k t) -> p k t", k=2)
            gbf = arA_b[:, 24576:28672].rearrange("p (k t) -> p k t", k=2)

            for sc in range(8):
                seq_start = (sc % 2 == 0)
                W("s", "vc", C["vc"])
                W("s", "vd", C["vd"])
                W("s", "va", C["va"])

                def _s(h, sc=sc):
                    h.dma_start(out=bcdc, in_=bcd_all[:, 2048 * sc:2048 * (sc + 1)])                         .then_inc(sems["sd"], 16)
                    h.dma_start(out=uc_v[:, :, :], in_=uR[:, :, 2048 * sc:2048 * (sc + 1)])                         .then_inc(sems["sd"], 16)
                    h.dma_start(out=zc_v[:, :, :], in_=zR[:, :, 2048 * sc:2048 * (sc + 1)])                         .then_inc(sems["sd"], 16)
                R.s(_s)
                INC("sd", 48)
                W("s", "sd", C["sd"])
                W("s", "pe", C["pe"])
                R.s(lambda h: h.dma_start(out=slotA, in_=bcdc[32:33, :])
                    .then_inc(sems["sd"], 16))
                INC("sd", 16)
                W("t", "sd")
                W("t", "va", C["va"])
                W("t", "vd", C["vd"])

                def _tb(h):
                    for tc in range(4):
                        pst = ps[2] if tc < 2 else ps[3]
                        r = h.matmul(pst[:, 512 * (tc % 2):512 * (tc % 2) + 512],
                                     ones_rowf[:], slotA[0:1, 512 * tc:512 * (tc + 1)],
                                     start=True, stop=True)
                    r.then_inc(sems["pe"], 1)
                R.t(_tb)
                INC("pe")
                W("a", "pe")

                def _asp(h):
                    for k in range(2):
                        for half in range(2):
                            pst = ps[2] if half == 0 else ps[3]
                            h.activation(out=scr1[:, k, 1024 * half:1024 * (half + 1)],
                                         in_=pst[:, 0:1024], func=AF.Identity,
                                         scale=dtw_sb[:, k:k + 1], bias=dtb_sb[:, k:k + 1])
                    h.activation(out=scr2.rearrange("p a b -> p (a b)"),
                                 in_=scr1.rearrange("p a b -> p (a b)"), func=AF.Abs)
                    h.activation(out=scr2.rearrange("p a b -> p (a b)"),
                                 in_=scr2.rearrange("p a b -> p (a b)"), func=AF.Exp,
                                 scale=-1.0)
                    r = h.activation(out=scr2.rearrange("p a b -> p (a b)"),
                                     in_=scr2.rearrange("p a b -> p (a b)"), func=AF.Ln,
                                     bias=1.0)
                    r.then_inc(sems["va"], 1)
                R.a(_asp)
                INC("va")
                W("v", "va")

                def _vsp(h):
                    h.scalar_tensor_tensor(out=dt_v.rearrange("p a b -> p (a b)"),
                                           in0=scr1.rearrange("p a b -> p (a b)"),
                                           scalar=0.0, in1=scr2.rearrange("p a b -> p (a b)"),
                                           op0=ALU.max, op1=ALU.add)
                    r = h.tensor_mul(out=dtu_v.rearrange("p a b -> p (a b)"),
                                     in0=dt_v.rearrange("p a b -> p (a b)"),
                                     in1=uc_v.rearrange("p a b -> p (a b)"))
                    r.then_inc(sems["vd"], 1)
                R.v(_vsp)
                INC("vd")
                W("a", "sd", C["sd"])
                W("a", "va", C["va"])
                R.a(lambda h: h.activation(out=bcdb, in_=bcdc, func=AF.Identity)
                    .then_inc(sems["va"], 1))
                INC("va")
                vd_hist = {}
                for n in range(NST):
                    par = n % 2
                    pA = ps[0] if par == 0 else ps[2]
                    pB = ps[1] if par == 0 else ps[3]
                    sB = sl_b[par]
                    sC = sl_c[par]
                    obB = ones_row[:] if par == 0 else onesq_b[32:33, :]
                    obC = onesq_b[64:65, :] if par == 0 else ones_row[:]
                    W("s", "pe", C["pe"])
                    W("s", "va", C["va"])

                    def _sn(h, n=n, sB=sB, sC=sC):
                        h.dma_start(out=sB, in_=bcdb[n:n + 1, :]).then_inc(sems["sd"], 16)
                        h.dma_start(out=sC, in_=bcdb[NST + n:NST + n + 1, :]) \
                            .then_inc(sems["sd"], 16)
                    R.s(_sn)
                    INC("sd", 32)
                    W("t", "vd", vd_hist.get((n - 2, "y"), C["vd"]))
                    W("t", "va", C["va"])
                    W("t", "sd", C["sd"])

                    def _tnB(h, pA=pA, pB=pB, sB=sB, obB=obB):
                        for tc in range(4):
                            pst = pA if tc < 2 else pB
                            r = h.matmul(pst[:, 512 * (tc % 2):512 * (tc % 2) + 512],
                                         obB, sB[0:1, 512 * tc:512 * (tc + 1)],
                                         start=True, stop=True)
                        r.then_inc(sems["pe"], 1)
                    R.t(_tnB)
                    INC("pe")
                    peB = C["pe"]
                    W("a", "pe", C["pe"])
                    W("a", "vd", vd_hist.get((n - 2, "y"), C["vd"]))
                    dAp = dA_v if par == 0 else dA_v2

                    def _an(h, n=n, dAp=dAp):
                        for k in range(2):
                            r = h.activation(out=dAp[:, k, :], in_=dt_v[:, k, :],
                                             func=AF.Exp, scale=A_sb[:, k, n:n + 1])
                        r.then_inc(sems["va"], 1)
                    R.a(_an)
                    INC("va")
                    W("v", "pe", peB)

                    def _v1(h, pA=pA, pB=pB):
                        for k in range(2):
                            for half in range(2):
                                pst = pA if half == 0 else pB
                                r = h.tensor_mul(
                                    out=hbuf[:, k, 1024 * half:1024 * (half + 1)],
                                    in0=dtu_v[:, k, 1024 * half:1024 * (half + 1)],
                                    in1=pst[:, 0:1024])
                        r.then_inc(sems["vd"], 1)
                    R.v(_v1)
                    INC("vd")
                    vd_hist[(n, "b")] = C["vd"]
                    W("t", "vd", vd_hist[(n, "b")])

                    def _tnC(h, pA=pA, pB=pB, sC=sC, obC=obC):
                        for tc in range(4):
                            pst = pA if tc < 2 else pB
                            r = h.matmul(pst[:, 512 * (tc % 2):512 * (tc % 2) + 512],
                                         obC, sC[0:1, 512 * tc:512 * (tc + 1)],
                                         start=True, stop=True)
                        r.then_inc(sems["pe"], 1)
                    R.t(_tnC)
                    INC("pe")
                    peC = C["pe"]
                    W("v", "va")

                    def _v2(h, n=n, seq_start=seq_start, dAp=dAp, pA=pA, pB=pB, peC=peC):
                        for k in range(2):
                            ini = 0.0 if seq_start else carry[:, k, n:n + 1]
                            h.tensor_tensor_scan(out=hbuf[:, k, :], data0=dAp[:, k, :],
                                                 data1=hbuf[:, k, :], initial=ini,
                                                 op0=ALU.mult, op1=ALU.add)
                        h.tensor_copy(out=carry[:, 0, n:n + 1], in_=hbuf[:, 0, 2047:2048])
                        h.tensor_copy(out=carry[:, 1, n:n + 1], in_=hbuf[:, 1, 2047:2048])
                        h.wait_ge(sems["pe"], peC)
                        for k in range(2):
                            for half in range(2):
                                pst = pA if half == 0 else pB
                                h.tensor_mul(out=tmpy[:, k, 1024 * half:1024 * (half + 1)],
                                             in0=hbuf[:, k, 1024 * half:1024 * (half + 1)],
                                             in1=pst[:, 0:1024])
                        if n == 0:
                            r = h.tensor_copy(out=yacc.rearrange("p a b -> p (a b)"),
                                              in_=tmpy.rearrange("p a b -> p (a b)"))
                        else:
                            r = h.tensor_add(out=yacc.rearrange("p a b -> p (a b)"),
                                             in0=yacc.rearrange("p a b -> p (a b)"),
                                             in1=tmpy.rearrange("p a b -> p (a b)"))
                        r.then_inc(sems["vd"], 1)
                    R.v(_v2)
                    INC("vd")
                    vd_hist[(n, "y")] = C["vd"]
                W("v", "vd")

                def _vg(h):
                    for k in range(2):
                        h.scalar_tensor_tensor(out=yacc[:, k, :], in0=uc_v[:, k, :],
                                               scalar=dpar_sb[:, k:k + 1],
                                               in1=yacc[:, k, :],
                                               op0=ALU.mult, op1=ALU.add)
                    r = h.tensor_mul(out=gbf.rearrange("p a b -> p (a b)"),
                                     in0=yacc.rearrange("p a b -> p (a b)"),
                                     in1=zc_v.rearrange("p a b -> p (a b)"))
                    r.then_inc(sems["vd"], 1)
                R.v(_vg)
                INC("vd")
                W("g", "vd")
                R.g(lambda h, sc=sc: h.dma_start(
                    out=gR[:, :, 2048 * sc:2048 * (sc + 1)], in_=gbf[:, :, :])
                    .then_inc(sems["sg"], 16))
                INC("sg", 16)
                W("v", "sg", C["sg"])
            W("g", "sg", C["sg"])
            R.g(lambda h: h.collective_compute(
                "AllGather", ALU.bypass, replica_groups=RG,
                ins=[gated_loc], outs=[gated_all]).then_inc(sems["vc"], 1))
            INC("vc")

        def p9():
            woR = w_outT_in.rearrange("(k p) dd -> p k dd", p=128)
            woT = arC_b[:, 0:16384].rearrange("p (k dd) -> p k dd", k=16)
            for half in range(2):
                stg9 = arD_f[:, 0:8192] if half == 0 else arB.ap()[:, 0:8192]
                W("s", "va", C["va"])
                W("s", "vd", C["vd"])
                R.s(lambda h, half=half, stg9=stg9: h.dma_start(
                    out=stg9.rearrange("p (k dd) -> p k dd", k=8),
                    in_=woR[:, 8 * half:8 * (half + 1), :]).then_inc(sems["sd"], 16))
                INC("sd", 16)
                W("a", "sd")
                W("a", "vd", C["vd"])
                W("a", "sg", C["sg"])

                def _a(h, half=half, stg9=stg9):
                    src2 = stg9.rearrange("p (k dd) -> p k dd", k=8)
                    for k in range(8):
                        r = h.activation(out=woT[:, 8 * half + k, :], in_=src2[:, k, :],
                                         func=AF.Identity)
                    r.then_inc(sems["va"], 1)
                R.a(_a)
                INC("va")
            W("g", "va", C["va"])
            roff = nc.gpsimd.alloc_register("roff")
            ga_h = gated_all.tensor
            x2R = x2_d.rearrange("(j p) t -> p j t", p=128)
            x3R = x3_d.rearrange("(j p) t -> p j t", p=128)
            outR3 = outT.rearrange("(j p) t -> p j t", p=128)
            grhs = arD_b[:, 0:8192].rearrange("p (k t) -> p k t", k=16)
            x2c = arD_f[:, 4096:4608]
            tmpr = arD_f[:, 4608:5120]
            W("g", "vc", C["vc"])
            for f in range(4):
                W("g", "pe", C["pe"])
                W("g", "vd", C["vd"])
                W("g", "va", C["va"])

                def _s(h, f=f):
                    pid = h.partition_id()
                    h.reg_alu(roff, pid, TOK, ALU.mult)
                    h.reg_alu(roff, roff, 512 * f, ALU.add)
                    h.dma_start(out=grhs[:, :, :],
                                in_=bass.AP(ga_h, roff,
                                            [[T16, 128], [128 * T16, 16], [1, 512]]))                         .then_inc(sems["sg"], 16)
                R.g(_s)
                INC("sg", 16)
                if DBG and f == 0:
                    W("g", "sg", C["sg"])
                    W("g", "va", C["va"])
                    def _gdw(h):
                        h.dma_start(out=dbg_gr, in_=grhs[:, :, :]).then_inc(sems["so"], 16)
                        h.dma_start(out=dbg_wo, in_=woT[:, :, :]).then_inc(sems["so"], 16)
                    R.g(_gdw)
                    INC("so", 32)
                    W("t", "so", C["so"])
                for Mt in range(8):
                    pst = ps[0] if Mt % 2 == 0 else ps[1]
                    W("t", "sg", C["sg"])
                    W("t", "va", C["va"])

                    def _t(h, Mt=Mt, pst=pst):
                        for ec in range(16):
                            r = h.matmul(pst[:, 0:512], woT[:, ec, 128 * Mt:128 * (Mt + 1)],
                                         grhs[:, ec, :], start=(ec == 0), stop=(ec == 15))
                        r.then_inc(sems["pe"], 1)
                    R.t(_t)
                    INC("pe")
                    W("s", "vd", C["vd"])
                    W("s", "sg", C["sg"])
                    R.s(lambda h, Mt=Mt, f=f: h.dma_start(
                        out=x2c, in_=x2R[:, Mt, 512 * f:512 * (f + 1)])
                        .then_inc(sems["sd"], 16))
                    INC("sd", 16)
                    W("a", "pe")
                    W("a", "vd", C["vd"])
                    R.a(lambda h, pst=pst: h.activation(
                        out=tmpr, in_=pst[:, 0:512], func=AF.Identity)
                        .then_inc(sems["va"], 1))
                    INC("va")
                    W("v", "va")
                    W("v", "sd")
                    R.v(lambda h: h.tensor_add(out=x2c, in0=x2c, in1=tmpr)
                        .then_inc(sems["vd"], 1))
                    INC("vd")
                    W("g", "vd")

                    def _g(h, Mt=Mt, f=f):
                        r = h.dma_start(out=x3R[:, Mt, 512 * f:512 * (f + 1)], in_=x2c)
                        r.then_inc(sems["sg"], 16)
                        if STAGE == 3:
                            h.dma_start(out=outR3[:, Mt, 512 * f:512 * (f + 1)],
                                        in_=x2c).then_inc(sems["so"], 16)
                    R.g(_g)
                    INC("sg", 16)
                    if STAGE == 3:
                        INC("so", 16)

        def p10():
            s1R = s1T_d.rearrange("(j p) t -> p j t", p=128)
            s2R = s2T_d.rearrange("(j p) t -> p j t", p=128)
            W("s", "va", C["va"])
            W("s", "vd", C["vd"])
            W("s", "pe", C["pe"])
            W("s", "sg", C["sg"])

            def _s(h):
                h.dma_start(out=s1T[:], in_=s1R).then_inc(sems["sd"], 16)
                h.dma_start(out=s2T[:], in_=s2R).then_inc(sems["sd"], 16)
            R.s(_s)
            INC("sd", 32)
            W("t", "sd")
            ffn_chunks(x3_d, x1_d, STAGE == 4)

        p0()
        p1()
        if DBG:
            W("g", "vd", C["vd"])
            def _gdbg(h):
                h.dma_start(out=dbg_ln1.rearrange("(j p) t -> p j t", p=128),
                            in_=ln1[:]).then_inc(sems["so"], 16)
            R.g(_gdbg)
            INC("so", 16)
        p2()
        p3()
        if STAGE >= 2:
            p4()
            ffn_chunks(x1_d, x2_d, STAGE == 2)
        if STAGE >= 3:
            p6()
            p7()
            p8()
            p9()
        if STAGE >= 4:
            p10()

        if DBG and STAGE >= 3:
            W("g", "sg", C["sg"])
            W("g", "vc", C["vc"])

            def _gm(h):
                h.dma_start(out=dbg_u, in_=u_d[:, 0:2048]).then_inc(sems["so"], 16)
                h.dma_start(out=dbg_z, in_=zs_d[:, 0:2048]).then_inc(sems["so"], 16)
                h.dma_start(out=dbg_bcd, in_=bcd_all[:, 0:2048]).then_inc(sems["so"], 16)
                h.dma_start(out=dbg_g, in_=gated_all[:, 0:2048]).then_inc(sems["so"], 16)
                h.dma_start(out=dbg_l3, in_=ln3_all[0:D, 0:2048]).then_inc(sems["so"], 16)
            R.g(_gm)
            INC("so", 80)
        W("g", "sg", C["sg"])
        if C["so"]:
            W("g", "so", C["so"])
        R.g(lambda h: h.nop())

        with nc.Block() as blk:
            @blk.sync
            def _(h):
                for fn in R.ops["s"]:
                    fn(h)

            @blk.scalar
            def _(h):
                for fn in R.ops["a"]:
                    fn(h)

            @blk.vector
            def _(h):
                for fn in R.ops["v"]:
                    fn(h)

            @blk.tensor
            def _(h):
                for fn in R.ops["t"]:
                    fn(h)

            @blk.gpsimd
            def _(h):
                for fn in R.ops["g"]:
                    fn(h)

    return nc


_NC_CACHE = None


def _get_nc():
    global _NC_CACHE
    if _NC_CACHE is None:
        _NC_CACHE = build_nc()
    return _NC_CACHE


def _prep_inmaps(inputs):
    f32 = lambda a: np.ascontiguousarray(np.asarray(a, dtype=np.float32))
    x = f32(inputs["L0_attn_a_x"])
    circ = {k: np.tile(f32(inputs[f"L0_attn_a_w{k}_c"]), CTILE // D) for k in "qkvo"}
    bias_cols = {k: np.ascontiguousarray(
        f32(inputs[f"L0_attn_a_w{k}_b"]).reshape(8, 128).T) for k in "qkvo"}
    wvb_row = f32(inputs["L0_attn_a_wv_b"]).reshape(1, D)
    w1T = np.ascontiguousarray(f32(inputs["L0_ffn_w1"]).T)
    w2T = np.ascontiguousarray(f32(inputs["L0_ffn_w2"]).T)
    b1c = np.ascontiguousarray(f32(inputs["L0_ffn_b1"]).reshape(HDIM // 128, 128).T)
    b2c = np.ascontiguousarray(f32(inputs["L0_ffn_b2"]).reshape(D // 128, 128).T)
    inw = f32(inputs["L0_attn_b_in_proj_w"])
    convw = f32(inputs["L0_attn_b_conv_w"])
    convb = f32(inputs["L0_attn_b_conv_b"])
    xpw = f32(inputs["L0_attn_b_x_proj_w"])
    dtw = f32(inputs["L0_attn_b_dt_proj_w"])[:, 0]
    dtb = f32(inputs["L0_attn_b_dt_proj_b"])
    alog = f32(inputs["L0_attn_b_A_log"])
    dpar = f32(inputs["L0_attn_b_D_param"])
    w_outT = np.ascontiguousarray(f32(inputs["L0_attn_b_out_proj_w"]).T)

    def cols2(v256):
        return np.ascontiguousarray(v256.reshape(2, 128).T)

    in_maps = []
    for c in range(NCORE):
        b, hh = c // 2, c % 2
        sl = slice(DSL * c, DSL * (c + 1))
        m = dict(
            xT_full=np.ascontiguousarray(np.concatenate(
                [x[b, TOK * hh:TOK * (hh + 1), :],
                 x[b, TOK * (1 - hh):TOK * (2 - hh), :]], axis=0).T),
            x_ownT=np.ascontiguousarray(x[b, TOK * hh:TOK * (hh + 1), :].T),
            w1T=w1T, w2T=w2T, b1_c=b1c, b2_c=b2c,
            w1N=f32(inputs["L0_ffn_w1"]), w2N=f32(inputs["L0_ffn_w2"]),
            wvb_row=wvb_row,
            w_inT_x=np.ascontiguousarray(inw[sl, :].T),
            w_inT_z=np.ascontiguousarray(inw[DI + DSL * c:DI + DSL * (c + 1), :].T),
            conv_w_c=np.ascontiguousarray(
                convw[sl].reshape(2, 128, 4).transpose(1, 0, 2)),
            conv_b_c=cols2(convb[sl]),
            wxT_sl=np.ascontiguousarray(xpw[:, sl].T),
            dtw_c=cols2(dtw[sl]),
            dtb_c=cols2(dtb[sl]),
            alog_c=np.ascontiguousarray(
                alog[sl].reshape(2, 128, NST).transpose(1, 0, 2)),
            dpar_c=cols2(dpar[sl]),
            w_outT=w_outT,
        )
        for k in "qkvo":
            m[f"c{k}2"] = circ[k]
            m[f"w{k}b_c"] = bias_cols[k]
        in_maps.append(m)
    return in_maps


def kernel(**inputs):
    nc = _get_nc()
    in_maps = _prep_inmaps(inputs)
    res = run_bass_kernel_spmd(nc, in_maps, core_ids=list(range(NCORE)))
    out = np.empty((4, S, D), dtype=np.float32)
    for c in range(NCORE):
        b, hh = c // 2, c % 2
        out[b, TOK * hh:TOK * (hh + 1), :] = res.results[c]["outT"].T
    return out
